# version 05
# baseline (speedup 1.0000x reference)
import contextlib
import numpy as np

import concourse.bass as bass
import concourse.mybir as mybir
from concourse.bass_utils import run_bass_kernel_spmd

F32 = mybir.dt.float32
BF16 = mybir.dt.bfloat16
AF = mybir.ActivationFunctionType
ALU = mybir.AluOpType
AX = mybir.AxisListType

B, S, D = 4, 4096, 1024
T = 2048
HD = 512
NPC = 4
NDC = 8
NTT = 4
FH = 4096
EPS = 1e-5

ENGS = ("sync", "tensor", "vector", "scalar", "gpsimd")


class Sem:
    def __init__(self, h):
        self.h = h
        self.n = 0


class Prog:
    def __init__(self):
        self.q = {e: [] for e in ENGS}
        self.hw = {}

    def emit(self, eng, fn, waits=(), inc=None, inc_by=1):
        ss = getattr(self, "selfsem", {}).get(eng)
        if ss is not None:
            waits = tuple(waits) + ((ss, ss.n),)
            if inc is None:
                inc, inc_by = ss, 1
        xs = getattr(self, "crosssem", {}).get(eng)
        if xs is not None:
            waits = tuple(waits) + ((xs, xs.n),)
        eff = []
        for s, v in waits:
            k = (eng, id(s))
            if v > 0 and v > self.hw.get(k, 0):
                self.hw[k] = v
                eff.append((s, v))
        eff = tuple(eff)

        def g(obj, fn=fn, waits=eff, inc=inc, inc_by=inc_by):
            for s, v in waits:
                obj.wait_ge(s.h, v)
            r = fn(obj)
            if inc is not None:
                r.then_inc(inc.h, inc_by)

        self.q[eng].append(g)
        if inc is not None:
            inc.n += inc_by

    def dma(self, eng, out, in_, sem, waits=(), slow=False):
        eff = tuple((s, v) for (s, v) in waits if v > 0)
        for s, v in eff:
            k = (eng, id(s))
            self.hw[k] = max(self.hw.get(k, 0), v)

        def g(obj, out=out, in_=in_, slow=slow, eff=eff, sem=sem):
            for s, v in eff:
                obj.wait_ge(s.h, v)
            obj.dma_start(out=out, in_=in_,
                          allow_slow_non_contiguous=slow).then_inc(sem.h, 16)

        self.q[eng].append(g)
        sem.n += 16

    def wait(self, eng, *pairs):
        eff = []
        for s, v in pairs:
            k = (eng, id(s))
            if v > 0 and v > self.hw.get(k, 0):
                self.hw[k] = v
                eff.append((s, v))
        eff = tuple(eff)

        def g(obj):
            for s, v in eff:
                obj.wait_ge(s.h, v)

        self.q[eng].append(g)


def build_nc():
    nc = bass.Bass("TRN2", target_bir_lowering=False)

    def din(name, shape):
        return nc.dram_tensor(name, shape, F32, kind="ExternalInput").ap()

    x0 = din("x0", [D, T])
    wq, wk, wv, wo = {}, {}, {}, {}
    for L in (0, 1):
        wq[L] = din(f"L{L}_wq", [D, HD])
        wk[L] = din(f"L{L}_wk", [D, HD])
        wv[L] = din(f"L{L}_wv", [D, HD])
        wo[L] = din(f"L{L}_wo", [3, HD, D])
    qbias = din("qbias", [128, NPC])
    kbias = din("kbias", [128, NPC])
    vbias = din("vbias", [1, HD])
    obias = din("obias", [128, NDC])
    wr0 = din("wr0", [D, D])
    wk0 = din("wk0", [D, D])
    wv0 = din("wv0", [D, D])
    mur = din("mur", [128, NDC])
    muk = din("muk", [128, NDC])
    w1t = din("w1t", [D, FH])
    b1c = din("b1c", [128, 32])
    w2t = din("w2t", [FH, D])
    b2c = din("b2c", [128, NDC])
    scal = din("scal", [1, 1])
    pvec = din("pvec", [1, 8])
    tric = din("tric", [128, 128])
    idmc = din("idmc", [128, 128])
    e2cc = din("e2cc", [128, 2])
    emc = din("emc", [4, 4])

    yout = nc.dram_tensor("y", [D, T], F32, kind="ExternalOutput").ap()

    g_in = nc.dram_tensor("g_in", [D, T], BF16).ap()
    g_out = nc.dram_tensor("g_out", [2, 2, 512, T], BF16).ap()
    c_in = nc.dram_tensor("c_in", [NPC, 128, 2065], BF16).ap()
    c_out = nc.dram_tensor("c_out", [2, 2, 2, 128, 2065], BF16).ap()
    w1bf = nc.dram_tensor("w1bf", [D, FH], BF16).ap()
    w2bf = nc.dram_tensor("w2bf", [FH, D], BF16).ap()
    scrb = nc.dram_tensor("scrb", [4, 1024], BF16).ap()
    scrf = nc.dram_tensor("scrf", [1, 16], F32).ap()

    P = Prog()
    LS = {}

    with contextlib.ExitStack() as es:
        A = lambda *a: es.enter_context(nc.sbuf_tensor(*a))
        PSA = lambda *a: es.enter_context(nc.psum_tensor(*a))

        X = [A(f"X{i}", [128, 2049], F32) for i in range(NDC)]
        XIMP = A("XIMP", [128, NDC * 16], F32)
        IMPX = A("IMPX", [128, NDC * 17], F32)
        AR = A("AR", [128, 58112], BF16)

        def ar(off, n):
            return AR[:, off:off + n]

        K_ = lambda pc: ar(pc * 4096, 4096)
        V_ = lambda tc: ar(16384 + tc * 520, 520)
        slot_ = lambda s, pc: ar((s * NPC + pc) * 2065, 2065)
        wqb_ = lambda dc: ar(33024 + dc * 512, 512)
        wkb_ = lambda dc: ar(37120 + dc * 512, 512)
        wvb_ = lambda dc: ar(41216 + dc * 512, 512)
        ctxself_ = lambda pc: ar(37120 + pc * 2048, 2048)
        xg_ = lambda buf, dc: ar(45312 + dc * 512, 512)
        Q_ = lambda pc: ar(49408 + pc * 512, 512)
        expS_ = lambda i: ar(51456 + i * 512, 512)
        ctxqb_ = lambda pc: ar(53504 + pc * 512, 512)
        contrib_ = lambda pc: ar(55552 + pc * 512, 512)
        qsqv = ar(57600, 512)
        xb_ = lambda dc, off=0: ar(dc * 2048 + off, 512)
        kact_ = lambda hc, off=0: ar(16384 + hc * 2048 + off, 512)
        h1_ = lambda hc: ar(16384 + hc * 512, 512)
        w1buf_ = lambda buf, dc: ar(53504 + buf * 1024 + dc * 128, 128)
        w2buf_ = lambda buf, hc: ar(33024 + buf * 4096 + hc * 128, 128)
        w128_ = lambda i: ar(51456 + i * 128, 128)

        LTB = [A(f"LTB{dc}", [128, 16], BF16) for dc in range(NDC)]
        LTT = AR[0:16, 41216:42240]
        LTEM = AR[0:4, 42240:46336]
        LTJM = AR[0:4, 46336:50432]
        A4 = AR[0:4, 50432:52480]
        B4 = [AR[0:4, 55552 + j * 512: 56064 + j * 512] for j in range(4)]
        IMPCTX = A("IMPCTX", [128, NPC * 17], BF16)
        IMPRES = A("IMPRES", [128, NDC * 17], BF16)
        IMPB = A("IMPB", [128, 17], BF16)
        IMPSQ = A("IMPSQ", [128, 17], BF16)

        WST = A("WST", [128, 1024], F32)
        PCST = A("PCST", [128, 1024], BF16)
        WST2 = A("WST2", [128, 1664], F32)
        T1 = A("T1", [128, 512], F32)
        RT = A("RT", [128, 512], F32)
        XBT = PCST[:, 0:512]
        XSQT = PCST[:, 512:1024]

        tri_b = A("tri_b", [128, 128], BF16)
        idm_b = A("idm_b", [128, 128], BF16)
        idmf = A("idmf", [128, 128], F32)
        nb4 = A("nb4", [4, 1], F32)
        e2c_b = A("e2c_b", [128, 2], BF16)
        onesrow_b = A("onesrow_b", [1, 64], BF16)
        rrow_b = A("rrow_b", [1, 512], BF16)
        emask_s = A("emask_s", [4, 4], F32)
        nb8 = A("nb8", [1, 8], F32)
        ones128f = A("ones128f", [1, 128], F32)
        ones128b = A("ones128b", [128, 1], BF16)
        pvec_s = A("pvec_s", [1, 8], F32)
        pbc = A("pbc", [128, 8], F32)
        scal_s = A("scal_s", [1, 1], F32)
        scal_bc = A("scal_bc", [128, 1], F32)
        qb_s = A("qb_s", [128, NPC], F32)
        kb_s = A("kb_s", [128, NPC], F32)
        ob_s = A("ob_s", [128, NDC], F32)
        vb_s = A("vb_s", [1, HD], F32)
        mur_s = A("mur_s", [128, NDC], F32)
        muk_s = A("muk_s", [128, NDC], F32)
        b1_s = A("b1_s", [128, 32], F32)
        b2_s = A("b2_s", [128, NDC], F32)
        bkm = A("bkm", [2, NPC], F32)
        bqm = A("bqm", [2, NPC], F32)
        nbound = A("nbound", [2, NPC], F32)
        bB = A("bB", [128, 8], F32)
        lrow = A("lrow", [1, 512], F32)
        rrow = A("rrow", [1, 512], F32)
        m1row = A("m1row", [1, 512], F32)
        m2row = lrow
        arow = rrow
        brow = lrow
        i1row = A("i1row", [1, 17], F32)
        i2row = A("i2row", [1, 17], F32)
        iarow = A("iarow", [1, 17], F32)
        ibrow = A("ibrow", [1, 17], F32)

        pbS = [PSA("pbS0", [128, 1024], F32), PSA("pbS1", [128, 1024], F32)]
        pb = [pbS[0][:, 0:512], pbS[0][:, 512:1024],
              pbS[1][:, 0:512], pbS[1][:, 512:1024]] + \
             [PSA(f"pb{i}", [128, 512], F32) for i in range(4, 7)]
        pbt = PSA("pbt", [128, 1024], BF16)

        sW = Sem(es.enter_context(nc.semaphore("sW")))
        sWa = Sem(es.enter_context(nc.semaphore("sWa")))
        sWb = Sem(es.enter_context(nc.semaphore("sWb")))
        sW1 = [Sem(es.enter_context(nc.semaphore(f"sW1{i}"))) for i in range(2)]
        sW2 = [Sem(es.enter_context(nc.semaphore(f"sW2{i}"))) for i in range(2)]
        sGx = Sem(es.enter_context(nc.semaphore("sGx")))
        sGsc = Sem(es.enter_context(nc.semaphore("sGsc")))
        sGg = Sem(es.enter_context(nc.semaphore("sGg")))
        sGm = Sem(es.enter_context(nc.semaphore("sGm")))
        sT = Sem(es.enter_context(nc.semaphore("sT")))
        sV = Sem(es.enter_context(nc.semaphore("sV")))
        sC = Sem(es.enter_context(nc.semaphore("sC")))
        sP = Sem(es.enter_context(nc.semaphore("sP")))
        sCC = Sem(es.enter_context(nc.semaphore("sCC")))
        P.selfsem = {"vector": sV, "scalar": sC}
        P.crosssem = {"vector": sC, "scalar": sV}

        def mm(out, lhsT, rhs, start, stop, waits=(), inc=None, tp=None):
            P.emit("tensor",
                   lambda t, out=out, lhsT=lhsT, rhs=rhs, s0=start, s1=stop, tp=tp:
                   t.matmul(out, lhsT, rhs, start=s0, stop=s1, tile_position=tp),
                   waits=waits, inc=inc)

        def act(out, in_, func, bias=None, scale=1.0, waits=(), inc=None):
            kw = {}
            if bias is not None:
                kw["bias"] = bias
            P.emit("scalar",
                   lambda sc, out=out, in_=in_, f=func, scale=scale, kw=kw:
                   sc.activation(out=out, in_=in_, func=f, scale=scale, **kw),
                   waits=waits, inc=inc)

        def vop(fn, waits=(), inc=None):
            P.emit("vector", fn, waits=waits, inc=inc)

        def gop(fn, waits=(), inc=None):
            P.emit("gpsimd", fn, waits=waits, inc=inc)

        class Stager:

            def __init__(self, buf, slot_elems, nslots, sem):
                self.buf, self.se, self.ns = buf, slot_elems, nslots
                self.sem = sem
                self.gates = [0] * nslots
                self.i = 0
                self.pend = []
                self.last = 0

            def stage(self, dst_bf, src, n, reader_gate=()):
                s = self.i % self.ns
                self.i += 1
                stg = self.buf[:, s * self.se: s * self.se + n]
                P.dma("sync", stg, src, self.sem,
                      waits=((sV, max(self.gates[s],
                                      self.last if not self.pend else 0)),))
                self.pend.append((s, stg, dst_bf, tuple(reader_gate)))
                if len(self.pend) == self.ns:
                    self.flush()

            def flush(self):
                if not self.pend:
                    return
                tot = self.sem.n
                for s, stg, dst, rg in self.pend:
                    vop(lambda v, stg=stg, dst=dst: v.tensor_copy(
                        out=dst, in_=stg), waits=((self.sem, tot),) + rg,
                        inc=sV)
                    self.gates[s] = sV.n
                self.pend = []
                self.last = sV.n

        stA = Stager(WST, 512, 2, sWa)
        stB = Stager(WST2, 128, 13, sWb)

        PRE_TASKS = []
        for cc in range(NDC):
            for j in range(8):
                PRE_TASKS.append((w1bf[cc * 128:(cc + 1) * 128,
                                       j * 512:(j + 1) * 512],
                                  w1t[cc * 128:(cc + 1) * 128,
                                      j * 512:(j + 1) * 512]))
        for cc in range(32):
            for j in range(2):
                PRE_TASKS.append((w2bf[cc * 128:(cc + 1) * 128,
                                       j * 512:(j + 1) * 512],
                                  w2t[cc * 128:(cc + 1) * 128,
                                      j * 512:(j + 1) * 512]))
        PRE_STATE = {"i": 0, "gates": [0, 0]}

        def emit_precast(n):
            pout = []
            while n > 0 and PRE_STATE["i"] < len(PRE_TASKS):
                dst, srcw = PRE_TASKS[PRE_STATE["i"]]
                pslot = PRE_STATE["i"] % 2
                PRE_STATE["i"] += 1
                n -= 1
                pcb = PCST[:, pslot * 512:(pslot + 1) * 512]
                stA.stage(pcb, srcw, 512,
                          reader_gate=((sGg, PRE_STATE["gates"][pslot]),))
                pout.append((pslot, dst, pcb))
                if len(pout) == 2:
                    stA.flush()
                    cv = sV.n
                    for ps, d, pc_ in pout:
                        P.dma("gpsimd", d, pc_, sGg, waits=((sV, cv),))
                    PRE_STATE["gates"][0] = PRE_STATE["gates"][1] = sGg.n
                    pout.clear()
                    LS["pcst_dma"] = sGg.n
            assert not pout

        def xg_src(dc, col0, ncol):
            s = col0 // T
            assert (col0 + ncol - 1) // T == s
            o = col0 - s * T
            h = dc // 4
            return g_out[h, s, (dc % 4) * 128:(dc % 4 + 1) * 128, o:o + ncol]

        for dc in range(NDC):
            P.dma("sync", X[dc][:, 1:2049], x0[dc * 128:(dc + 1) * 128, :], sW)
        for (dst, src) in ((qb_s, qbias), (kb_s, kbias), (ob_s, obias),
                           (vb_s, vbias), (mur_s, mur), (muk_s, muk),
                           (b1_s, b1c), (b2_s, b2c), (pvec_s, pvec),
                           (scal_s, scal)):
            P.dma("sync", dst[:, :], src[:, :], sW)
        P.dma("sync", WST2[:, 0:128], tric[:, :], sW)
        P.dma("sync", WST2[:, 128:256], idmc[:, :], sW)
        P.dma("sync", WST[:, 0:2], e2cc[:, :], sW)
        P.dma("sync", emask_s[:, :], emc[:, :], sW)
        consts_l = sW.n
        x_loaded = sW.n
        vop(lambda v: v.tensor_copy(out=tri_b[:, :], in_=WST2[:, 0:128]),
            waits=((sW, consts_l),), inc=sV)
        vop(lambda v: v.tensor_copy(out=idm_b[:, :], in_=WST2[:, 128:256]),
            inc=sV)
        vop(lambda v: v.tensor_copy(out=idmf[:, :], in_=WST2[:, 128:256]),
            inc=sV)
        vop(lambda v: v.tensor_copy(out=e2c_b[:, :], in_=WST[:, 0:2]), inc=sV)
        vop(lambda v: v.memset(ones128f[:, :], 1.0), inc=sV)
        vop(lambda v: v.memset(ones128b[:, :], 1.0), inc=sV)
        vop(lambda v: v.memset(onesrow_b[:, :], 1.0), inc=sV)
        stA.gates[0] = sV.n
        stB.gates[0] = stB.gates[1] = sV.n
        consts_v = sV.n
        mm(pb[0][:, 0:8], ones128f[:, :], pvec_s[:, :], start=True, stop=True,
           waits=((sW, consts_l), (sV, consts_v)), inc=sT)
        mm(pb[0][:, 8:9], ones128f[:, :], scal_s[:, :], start=True, stop=True,
           inc=sT)
        vop(lambda v: v.tensor_copy(out=pbc[:, :], in_=pb[0][:, 0:8]),
            waits=((sT, sT.n),), inc=sV)
        vop(lambda v: v.tensor_copy(out=scal_bc[:, :], in_=pb[0][:, 8:9]),
            inc=sV)
        LS["pbc_v"] = sV.n

        PV_ACTX, PV_BCTX, PV_ASELF, PV_BSELF, PV_G, PV_AL, PV_BL = range(7)

        def pv(i):
            return pbc[:, i:i + 1]

        LS["xg_gate"] = 0
        LS["gin_free_cc"] = 0
        LS["cin_free_cc"] = 0
        LS["arena_lo_T"] = 0
        LS["arena_md_T"] = 0
        LS["arena_hi_T"] = 0

        def emit_ln(make_xb):
            for tt in range(NTT):
                for dc in range(NDC):
                    vop(lambda v, dc=dc, tt=tt: v.tensor_copy(
                        out=XBT[:, :],
                        in_=X[dc][:, 1 + tt * 512:1 + (tt + 1) * 512]),
                        waits=((sT, LS.get("lnst", 0)),
                               (sGg, LS.get("pcst_dma", 0))), inc=sV)
                    vop(lambda v: v.tensor_tensor(
                        out=XSQT[:, :], in0=XBT[:, :], in1=XBT[:, :],
                        op=ALU.mult), inc=sV)
                    mm(pb[2][0:1, :], ones128b[:, :], XBT[:, :],
                       start=(dc == 0), stop=(dc == NDC - 1),
                       waits=((sV, sV.n), (sV, LS.get("lnev", 0))),
                       inc=sT)
                    mm(pb[3][0:1, :], ones128b[:, :], XSQT[:, :],
                       start=(dc == 0), stop=(dc == NDC - 1),
                       inc=sT)
                    LS["lnst"] = sT.n
                st_t = sT.n
                vop(lambda v: v.tensor_scalar(
                    out=m1row[:, :], in0=pb[2][0:1, :], scalar1=1.0 / D,
                    scalar2=None, op0=ALU.mult), waits=((sT, st_t),), inc=sV)
                vop(lambda v: v.tensor_scalar(
                    out=m2row[:, :], in0=pb[3][0:1, :], scalar1=1.0 / D,
                    scalar2=None, op0=ALU.mult), inc=sV)
                LS["lnev"] = sV.n
                vop(lambda v: v.tensor_tensor(out=arow[:, :], in0=m1row[:, :],
                                              in1=m1row[:, :], op=ALU.mult),
                    inc=sV)
                vop(lambda v: v.tensor_tensor(out=arow[:, :], in0=m2row[:, :],
                                              in1=arow[:, :],
                                              op=ALU.subtract), inc=sV)
                vop(lambda v: v.tensor_scalar(out=arow[:, :], in0=arow[:, :],
                                              scalar1=EPS, scalar2=None,
                                              op0=ALU.add), inc=sV)
                vop(lambda v: v.reciprocal(out=arow[:, :], in_=arow[:, :]),
                    inc=sV)
                act(arow[:, :], arow[:, :], AF.Sqrt, waits=((sV, sV.n),),
                    inc=sC)
                vop(lambda v: v.tensor_tensor(out=brow[:, :], in0=m1row[:, :],
                                              in1=arow[:, :], op=ALU.mult),
                    waits=((sC, sC.n),), inc=sV)
                ab_v = sV.n
                mm(pb[2][:, :], ones128f[:, :], arow[:, :], start=True,
                   stop=True, waits=((sV, ab_v), (sV, LS.get("lnap", 0))),
                   inc=sT)
                mm(pb[3][:, :], ones128f[:, :], brow[:, :], start=True,
                   stop=True, inc=sT)
                bc_t = sT.n
                for dc in range(NDC):
                    vop(lambda v, dc=dc, tt=tt: v.tensor_tensor(
                        out=T1[:, :],
                        in0=X[dc][:, 1 + tt * 512:1 + (tt + 1) * 512],
                        in1=pb[2][:, :], op=ALU.mult),
                        waits=((sT, bc_t),), inc=sV)
                    vop(lambda v, dc=dc, tt=tt: v.tensor_tensor(
                        out=X[dc][:, 1 + tt * 512:1 + (tt + 1) * 512],
                        in0=T1[:, :], in1=pb[3][:, :], op=ALU.subtract),
                        inc=sV)
                    if make_xb:
                        vop(lambda v, dc=dc, tt=tt: v.tensor_copy(
                            out=xb_(dc, tt * 512),
                            in_=X[dc][:, 1 + tt * 512:1 + (tt + 1) * 512]),
                            waits=((sT, LS["arena_lo_T"]),), inc=sV)
                LS["lnap"] = sV.n

        def emit_ffn0():
            def blends(mu_s):
                for dc in range(NDC):
                    for tt in range(NTT):
                        vop(lambda v, dc=dc, tt=tt, mu_s=mu_s: v.tensor_scalar(
                            out=T1[:, :],
                            in0=X[dc][:, tt * 512:tt * 512 + 512],
                            scalar1=mu_s[:, dc:dc + 1], scalar2=-1.0,
                            op0=ALU.mult, op1=ALU.mult),
                            waits=((sT, LS["arena_lo_T"]),), inc=sV)
                        vop(lambda v, dc=dc, tt=tt: v.tensor_tensor(
                            out=T1[:, :],
                            in0=X[dc][:, tt * 512:tt * 512 + 512],
                            in1=T1[:, :], op=ALU.add), inc=sV)
                        vop(lambda v, dc=dc, tt=tt, mu_s=mu_s:
                            v.scalar_tensor_tensor(
                                out=xb_(dc, tt * 512),
                                in0=X[dc][:, 1 + tt * 512:1 + tt * 512 + 512],
                                scalar=mu_s[:, dc:dc + 1], in1=T1[:, :],
                                op0=ALU.mult, op1=ALU.add), inc=sV)

            blends(muk_s)
            xkb_v = sV.n

            for hc in range(NDC):
                for dc in range(NDC):
                    stB.stage(w128_(dc),
                              wk0[dc * 128:(dc + 1) * 128,
                                  hc * 128:(hc + 1) * 128], 128,
                              reader_gate=((sT, LS["arena_md_T"]),))
                stB.flush()
                wc_v = sV.n
                for tt in range(NTT):
                    bk = (hc % 2) * 2 + tt % 2
                    bank = pb[bk]
                    for dc in range(NDC):
                        mm(bank[:, :], w128_(dc), xb_(dc, tt * 512),
                           start=(dc == 0), stop=(dc == NDC - 1),
                           waits=((sV, wc_v), (sV, xkb_v),
                                  (sC, LS.get(f"f0e{bk}", 0)))
                           if dc == 0 else (),
                           inc=sT if dc == NDC - 1 else None)
                    LS["arena_md_T"] = sT.n
                    LS["arena_lo_T"] = sT.n
                    act(T1[:, :], bank[:, :], AF.Relu, waits=((sT, sT.n),),
                        inc=sC)
                    LS[f"f0e{bk}"] = sC.n
                    vop(lambda v, hc=hc, tt=tt: v.tensor_tensor(
                        out=kact_(hc, tt * 512), in0=T1[:, :], in1=T1[:, :],
                        op=ALU.mult),
                        waits=((sC, sC.n), (sT, LS["arena_lo_T"])),
                        inc=sV)
            kact_v = sV.n

            blends(mur_s)
            xrb_v = sV.n

            for mc in range(NDC):
                for dc in range(NDC):
                    stB.stage(w128_(dc),
                              wr0[dc * 128:(dc + 1) * 128,
                                  mc * 128:(mc + 1) * 128], 128,
                              reader_gate=((sT, LS["arena_md_T"]),))
                stB.flush()
                wr_v = sV.n
                for kc in range(NDC):
                    stB.stage(w128_(8 + kc),
                              wv0[kc * 128:(kc + 1) * 128,
                                  mc * 128:(mc + 1) * 128], 128,
                              reader_gate=((sT, LS["arena_md_T"]),))
                stB.flush()
                wv_v = sV.n
                for tt in range(NTT):
                    bank = pb[tt % 2]
                    for dc in range(NDC):
                        mm(bank[:, :], w128_(dc), xb_(dc, tt * 512),
                           start=(dc == 0), stop=(dc == NDC - 1),
                           waits=((sV, wr_v), (sV, xrb_v),
                                  (sC, LS.get(f"f0r{tt % 2}", 0)))
                           if dc == 0 else (),
                           inc=sT if dc == NDC - 1 else None)
                    LS["arena_md_T"] = sT.n
                    LS["arena_lo_T"] = sT.n
                    act(RT[:, :], bank[:, :], AF.Sigmoid,
                        waits=((sT, sT.n), (sV, LS.get(f"f0v{tt % 2}", 0))),
                        inc=sC)
                    LS[f"f0r{tt % 2}"] = sC.n
                    rsig_c = sC.n
                    bank2 = pb[2 + tt % 2]
                    for kc in range(NDC):
                        mm(bank2[:, :], w128_(8 + kc), kact_(kc, tt * 512),
                           start=(kc == 0), stop=(kc == NDC - 1),
                           waits=((sV, wv_v), (sV, kact_v),
                                  (sV, LS.get(f"f0o{tt % 2}", 0)))
                           if kc == 0 else (),
                           inc=sT if kc == NDC - 1 else None)
                    LS["arena_md_T"] = sT.n
                    vop(lambda v, bank2=bank2: v.tensor_tensor(
                        out=T1[:, :], in0=bank2[:, :],
                        in1=RT[:, :], op=ALU.mult),
                        waits=((sT, sT.n), (sC, rsig_c)), inc=sV)
                    LS[f"f0v{tt % 2}"] = sV.n
                    vop(lambda v, mc=mc, tt=tt: v.tensor_tensor(
                        out=X[mc][:, 1 + tt * 512:1 + (tt + 1) * 512],
                        in0=T1[:, :],
                        in1=X[mc][:, 1 + tt * 512:1 + (tt + 1) * 512],
                        op=ALU.add), inc=sV)
                    LS[f"f0o{tt % 2}"] = sV.n
            LS["arena_lo_T"] = sT.n
            LS["arena_md_T"] = sT.n

        def emit_ffn1():
            xb_v = LS["lnap"]

            for dc in range(NDC):
                vop(lambda v, dc=dc: v.tensor_scalar(
                    out=XBT[:, 0:16], in0=XIMP[:, dc * 16:(dc + 1) * 16],
                    scalar1=pv(PV_BL), scalar2=None, op0=ALU.mult),
                    waits=((sV, xb_v),), inc=sV)
                vop(lambda v, dc=dc: v.scalar_tensor_tensor(
                    out=LTB[dc][:, :], in0=X[dc][:, 2:18],
                    scalar=pv(PV_AL), in1=XBT[:, 0:16],
                    op0=ALU.mult, op1=ALU.add), inc=sV)
            ltb_v = sV.n
            vop(lambda v: v.memset(LTEM[:, :], 0.0),
                waits=((sT, LS["arena_md_T"]),), inc=sV)
            vop(lambda v: v.memset(LTJM[:, :], 0.0), inc=sV)
            for dc in range(NDC):
                P.emit("tensor",
                       lambda t, dc=dc: t.transpose(
                           pbt[0:16, 0:128], LTB[dc][:, :], idm_b[:, :]),
                       waits=((sV, ltb_v), (sV, LS.get("ltt_ev", 0))), inc=sT)
                vop(lambda v, dc=dc: v.tensor_copy(
                    out=LTT[:, dc * 128:(dc + 1) * 128], in_=pbt[0:16, 0:128]),
                    waits=((sT, sT.n),), inc=sV)
                LS["ltt_ev"] = sV.n
                P.emit("tensor",
                       lambda t, dc=dc: t.transpose(
                           pbt[0:4, 0:128],
                           LTB[dc].rearrange("p (a b) -> p a b", b=2)[:, 0:4, 0],
                           idm_b[:, :]),
                       waits=((sV, LS["ltt_ev"]),), inc=sT)
                for i in range(4):
                    vop(lambda v, dc=dc, i=i: v.tensor_scalar(
                        out=LTEM[0:4, i * 1024 + dc * 128:
                                 i * 1024 + (dc + 1) * 128],
                        in0=pbt[0:4, 0:128], scalar1=emask_s[:, i:i + 1],
                        scalar2=None, op0=ALU.mult),
                        waits=((sT, sT.n),), inc=sV)
                LS["ltt_ev"] = sV.n
            ltt_v = sV.n
            for i in range(4):
                P.dma("gpsimd", scrb[i, :], LTT[8 + 2 * i:9 + 2 * i, :], sGm,
                      waits=((sV, ltt_v),))
            sc_done = sGm.n
            for i in range(4):
                for j in range(4):
                    for half in range(2):
                        P.dma("gpsimd",
                              LTJM[i:i + 1, i * 1024 + j * 256 + half * 128:
                                   i * 1024 + j * 256 + half * 128 + 128],
                              scrb[i:i + 1,
                                   512 * half + j: 512 * half + j + 509:4],
                              sGm, waits=((sGm, sc_done), (sV, ltt_v)),
                              slow=True)
            ltj_done = sGm.n

            for tt in range(NTT):
                for dc in range(NDC):
                    mm(pb[4][0:4, :],
                       LTB[dc].rearrange("p (a b) -> p a b", b=2)[:, 0:4, 1],
                       xb_(dc, tt * 512),
                       start=(dc == 0), stop=(dc == NDC - 1),
                       waits=((sV, ltb_v), (sV, xb_v),
                              (sC, LS.get("a4ev", 0)))
                       if dc == 0 else (),
                       inc=sT if dc == NDC - 1 else None)
                act(A4[0:4, tt * 512:(tt + 1) * 512], pb[4][0:4, :],
                    AF.Identity, scale=scal_bc[0:4, 0:1],
                    waits=((sT, sT.n),), inc=sC)
                LS["a4ev"] = sC.n
            a4_c = sC.n
            LS["arena_lo_T"] = sT.n

            wbuf_gate = [LS["arena_md_T"], LS["arena_md_T"]]
            w2buf_gate = [LS["arena_md_T"], LS["arena_md_T"]]
            for tt in range(NTT):
                for hc in range(32):
                    bufw = hc % 2
                    for dc in range(NDC):
                        P.dma("sync", w1buf_(bufw, dc),
                              w1bf[dc * 128:(dc + 1) * 128,
                                   hc * 128:(hc + 1) * 128], sW1[bufw],
                              waits=((sT, wbuf_gate[bufw]),
                                     (sGg, LS.get("pcst_dma", 0)))
                              if dc == 0 else ())
                    w_rdy = sW1[bufw].n
                    bank = pb[hc % 2]
                    i4 = hc // 8
                    for dc in range(NDC):
                        mm(bank[:, :], w1buf_(bufw, dc), xb_(dc, tt * 512),
                           start=(dc == 0), stop=False,
                           waits=((sW1[bufw], w_rdy), (sV, xb_v),
                                  (sC, LS.get(f"f1h{hc % 2}", 0)))
                           if dc == 0 else ())
                    mm(bank[:, :],
                       LTEM[0:4, i4 * 1024 + (hc % 8) * 128:
                            i4 * 1024 + (hc % 8) * 128 + 128],
                       A4[0:4, tt * 512:(tt + 1) * 512],
                       start=False, stop=True,
                       waits=((sV, ltt_v), (sC, a4_c)), inc=sT)
                    wbuf_gate[bufw] = sT.n
                    act(h1_(hc), bank[:, :], AF.Gelu, bias=b1_s[:, hc:hc + 1],
                        waits=((sT, sT.n),), inc=sC)
                    LS[f"f1h{hc % 2}"] = sC.n
                h_c = sC.n
                LS["arena_lo_T"] = sT.n

                for j in range(4):
                    for dc in range(NDC):
                        hc = j * 8 + dc
                        mm(pb[4][0:4, :],
                           LTB[dc].rearrange("p (a b) -> p a b", b=2)[:, 4:8, 1],
                           h1_(hc), start=(dc == 0), stop=(dc == NDC - 1),
                           waits=((sC, h_c), (sC, LS.get("b4c", 0)))
                           if dc == 0 else (),
                           inc=sT if dc == NDC - 1 else None)
                    act(B4[j][:, :], pb[4][0:4, :], AF.Identity,
                        scale=scal_bc[0:4, 0:1], waits=((sT, sT.n),), inc=sC)
                    LS["b4c"] = sC.n
                b4_c = sC.n

                for mc in range(NDC):
                    bufw = mc % 2
                    i4 = mc // 2
                    half = mc % 2
                    for hc in range(32):
                        P.dma("sync", w2buf_(bufw, hc),
                              w2bf[hc * 128:(hc + 1) * 128,
                                   mc * 128:(mc + 1) * 128], sW2[bufw],
                              waits=((sT, w2buf_gate[bufw]),
                                     (sGg, LS.get("pcst_dma", 0)))
                              if hc == 0 else ())
                    w_rdy = sW2[bufw].n
                    bank = pb[2 + mc % 2]
                    for hc in range(32):
                        mm(bank[:, :], w2buf_(bufw, hc), h1_(hc),
                           start=(hc == 0), stop=False,
                           waits=((sW2[bufw], w_rdy), (sC, h_c),
                                  (sV, LS.get(f"f1o{mc % 2}", 0)))
                           if hc == 0 else ())
                    for j in range(4):
                        mm(bank[:, :],
                           LTJM[0:4, i4 * 1024 + j * 256 + half * 128:
                                i4 * 1024 + j * 256 + half * 128 + 128],
                           B4[j][0:4, :],
                           start=False, stop=(j == 3),
                           waits=((sC, b4_c), (sGm, ltj_done)),
                           inc=sT if j == 3 else None)
                    w2buf_gate[bufw] = sT.n
                    act(T1[:, :], bank[:, :], AF.Identity,
                        bias=b2_s[:, mc:mc + 1], waits=((sT, sT.n),), inc=sC)
                    vop(lambda v, mc=mc, tt=tt: v.tensor_tensor(
                        out=X[mc][:, 1 + tt * 512:1 + (tt + 1) * 512],
                        in0=T1[:, :],
                        in1=X[mc][:, 1 + tt * 512:1 + (tt + 1) * 512],
                        op=ALU.add), waits=((sC, sC.n),), inc=sV)
                    LS[f"f1o{mc % 2}"] = sV.n
            LS["arena_lo_T"] = sT.n
            LS["arena_md_T"] = sT.n

        def emit_layer(L):
            is0 = (L == 0)

            src_gate = ((sW, x_loaded),) if is0 else ()
            for dc in range(NDC):
                stgb = ar(45312 + (dc % 2) * 2048, 2048)
                vop(lambda v, dc=dc, stgb=stgb: v.tensor_copy(
                    out=stgb, in_=X[dc][:, 1:2049]),
                    waits=src_gate + ((sT, LS["arena_hi_T"]),
                                      (sGg, LS.get("stgb_dma", 0))), inc=sV)
                P.dma("gpsimd", g_in[dc * 128:(dc + 1) * 128, :], stgb, sGg,
                      waits=((sV, sV.n),))
                LS["stgb_dma"] = sGg.n
            gin_done = sGg.n
            gop(lambda g: g.collective_compute(
                "AllGather", ALU.bypass,
                replica_groups=[[0, 1], [2, 3], [4, 5], [6, 7]],
                ins=[g_in[0:512, :].opt()], outs=[g_out[0].opt()]),
                waits=((sGg, gin_done),), inc=sCC)
            xgather_h1 = sCC.n
            gop(lambda g: g.collective_compute(
                "AllGather", ALU.bypass,
                replica_groups=[[0, 1], [2, 3], [4, 5], [6, 7]],
                ins=[g_in[512:1024, :].opt()],
                outs=[g_out[1].opt()]), inc=sCC)
            xgather = sCC.n

            mdgate = ((sT, LS["arena_md_T"]),)
            for dc in range(NDC):
                stA.stage(wqb_(dc), wq[L][dc * 128:(dc + 1) * 128, :], 512,
                          reader_gate=mdgate)
                stA.stage(wkb_(dc), wk[L][dc * 128:(dc + 1) * 128, :], 512,
                          reader_gate=mdgate)
                stA.stage(wvb_(dc), wv[L][dc * 128:(dc + 1) * 128, :], 512,
                          reader_gate=mdgate)
            stA.flush()
            qkvw_v = sV.n

            for tc in range(32):
                vop(lambda v, tc=tc: v.memset(
                    V_(tc).rearrange("p (h w) -> p h w", w=65)[:, :, 64:65],
                    1.0), waits=((sT, LS["arena_lo_T"]),) if tc == 0 else ())
            vop(lambda v: v.memset(IMPCTX[:, :], 0.0),
                waits=((sGsc, LS.get("contrib_dma", 0)),), inc=sV)
            vzero_v = sV.n

            for tt8 in range(8):
                buf = tt8 % 2
                for dc in range(NDC):
                    P.dma("gpsimd", xg_(buf, dc), xg_src(dc, tt8 * 512, 512),
                          sGx, waits=((sCC, xgather_h1 if dc < 4 else xgather),
                                      (sT, LS["xg_gate"])))
                tile_rdy = sGx.n
                for pc in range(NPC):
                    bank = pb[pc % 2]
                    for dc in range(NDC):
                        mm(bank[:, :], wkb_(dc)[:, pc * 128:(pc + 1) * 128],
                           xg_(buf, dc), start=(dc == 0), stop=(dc == NDC - 1),
                           waits=((sGx, tile_rdy), (sV, qkvw_v),
                                  (sV, vzero_v),
                                  (sV, LS.get(f"kev{pc % 2}", 0)))
                           if dc == 0 else (),
                           inc=sT if dc == NDC - 1 else None)
                    if is0:
                        vop(lambda v, pc=pc, tt8=tt8, bank=bank: v.tensor_scalar(
                            out=K_(pc)[:, tt8 * 512:(tt8 + 1) * 512],
                            in0=bank[:, :], scalar1=kb_s[:, pc:pc + 1],
                            scalar2=None, op0=ALU.add),
                            waits=((sT, sT.n),), inc=sV)
                    else:
                        vop(lambda v, pc=pc, tt8=tt8, bank=bank: v.tensor_copy(
                            out=K_(pc)[:, tt8 * 512:(tt8 + 1) * 512],
                            in_=bank[:, :]), waits=((sT, sT.n),), inc=sV)
                    LS[f"kev{pc % 2}"] = sV.n
                for st in range(4):
                    tc = tt8 * 4 + st
                    bank = pb[2 + st % 2]
                    for dc in range(NDC):
                        mm(bank[:, :], xg_(buf, dc)[:, st * 128:(st + 1) * 128],
                           wvb_(dc), start=(dc == 0),
                           stop=(dc == NDC - 1 and not is0),
                           waits=((sV, LS.get(f"vev{st % 2}", 0)),)
                           if dc == 0 else (),
                           inc=sT if (dc == NDC - 1 and not is0) else None)
                    if is0:
                        mm(bank[:, :], ones128f[:, :], vb_s[:, :],
                           start=False, stop=True, inc=sT)
                    vop(lambda v, tc=tc, bank=bank: v.tensor_copy(
                        out=V_(tc).rearrange("p (h w) -> p h w",
                                             w=65)[:, :, 0:64],
                        in_=bank[:, :].rearrange("p (h w) -> p h w", w=64)),
                        waits=((sT, sT.n),), inc=sV)
                    LS[f"vev{st % 2}"] = sV.n
                LS["xg_gate"] = sT.n
            kproj_v = sV.n
            LS["arena_md_T"] = sT.n

            for pc in range(NPC):
                for i8 in range(8):
                    vop(lambda v, pc=pc, i8=i8: v.tensor_tensor(
                        out=qsqv[:, :], in0=K_(pc)[:, i8 * 512:(i8 + 1) * 512],
                        in1=K_(pc)[:, i8 * 512:(i8 + 1) * 512], op=ALU.mult),
                        waits=((sV, kproj_v), (sT, LS.get("nrmT", 0))),
                        inc=sV)
                    mm(pb[4][0:2, :], e2c_b[:, :], qsqv[:, :],
                       start=(i8 == 0), stop=(i8 == 7),
                       waits=((sV, sV.n), (sV, LS.get("ctxev", 0))),
                       inc=sT)
                    LS["nrmT"] = sT.n
                vop(lambda v, pc=pc: v.reduce_max(
                    out=bkm[:, pc:pc + 1], in_=pb[4][0:2, :], axis=AX.X),
                    waits=((sT, sT.n),), inc=sV)
            knorm_v = sV.n

            s_gate = [0, 0, 0, 0]
            exp_gate = [0, 0, 0, 0]
            qev_gate = [0, 0]
            for pair in range(4):
                for half2 in range(2):
                    qb = pair + half2 * 4
                    for dc in range(NDC):
                        P.dma("gpsimd", xg_(1, dc), xg_src(dc, qb * 512, 512),
                              sGx, waits=((sT, LS["xg_gate"]),))
                    qtile_rdy = sGx.n
                    for pc in range(NPC):
                        bank = pb[pc % 2]
                        for dc in range(NDC):
                            mm(bank[:, :],
                               wqb_(dc)[:, pc * 128:(pc + 1) * 128],
                               xg_(1, dc), start=(dc == 0),
                               stop=(dc == NDC - 1),
                               waits=((sGx, qtile_rdy),
                                      (sV, qev_gate[pc % 2]),
                                      (sC, s_gate[pc % 2]))
                               if dc == 0 else (),
                               inc=sT if dc == NDC - 1 else None)
                        if is0:
                            vop(lambda v, pc=pc, bank=bank: v.tensor_scalar(
                                out=Q_(pc), in0=bank[:, :],
                                scalar1=qb_s[:, pc:pc + 1], scalar2=None,
                                op0=ALU.add), waits=((sT, sT.n),), inc=sV)
                        else:
                            vop(lambda v, pc=pc, bank=bank: v.tensor_copy(
                                out=Q_(pc), in_=bank[:, :]),
                                waits=((sT, sT.n),), inc=sV)
                        qev_gate[pc % 2] = sV.n
                    LS["xg_gate"] = sT.n
                    qproj_v = sV.n

                    for pc in range(NPC):
                        vop(lambda v, pc=pc: v.tensor_tensor(
                            out=qsqv[:, :], in0=Q_(pc), in1=Q_(pc),
                            op=ALU.mult),
                            waits=((sV, qproj_v), (sT, LS.get("nrmT", 0))),
                            inc=sV)
                        mm(pb[4][0:2, :], e2c_b[:, :], qsqv[:, :],
                           start=True, stop=True,
                           waits=((sV, sV.n), (sV, LS.get("ctxev", 0))),
                           inc=sT)
                        LS["nrmT"] = sT.n
                        vop(lambda v, pc=pc: v.reduce_max(
                            out=bqm[:, pc:pc + 1], in_=pb[4][0:2, :],
                            axis=AX.X), waits=((sT, sT.n),), inc=sV)
                    vop(lambda v: v.tensor_tensor(
                        out=nbound[:, :], in0=bqm[:, :], in1=bkm[:, :],
                        op=ALU.mult),
                        waits=((sV, knorm_v), (sGm, LS.get("nb_dma", 0))),
                        inc=sV)
                    nbm_v = sV.n
                    P.emit("tensor",
                           lambda t: t.transpose(pb[6][0:4, 0:2],
                                                 nbound[:, :], idmf[0:2, 0:2]),
                           waits=((sV, nbm_v), (sV, LS.get("ctxev", 0))),
                           inc=sT)
                    vop(lambda v: v.reduce_max(out=nb4[:, :],
                                               in_=pb[6][0:4, 0:2],
                                               axis=AX.X),
                        waits=((sT, sT.n),), inc=sV)
                    act(nb4[:, :], nb4[:, :], AF.Sqrt, waits=((sV, sV.n),),
                        inc=sC)
                    vop(lambda v: v.tensor_scalar(
                        out=nb4[:, :], in0=nb4[:, :], scalar1=-0.125,
                        scalar2=44.0, op0=ALU.mult, op1=ALU.add),
                        waits=((sC, sC.n),), inc=sV)
                    nb_v = sV.n
                    P.dma("gpsimd", scrf[0, 0:4], nb4[:, :], sGm,
                          waits=((sV, nb_v), (sGm, LS.get("nb_dma", 0))))
                    P.dma("gpsimd", nb8[:, 0:4], scrf[0, 0:4], sGm,
                          waits=((sGm, sGm.n), (sT, LS.get("nb8_T", 0))))
                    LS["nb_dma"] = sGm.n
                    mm(pb[4][:, 0:4], ones128f[:, :], nb8[:, 0:4], start=True,
                       stop=True, waits=((sGm, LS["nb_dma"]),
                                         (sV, LS.get("ctxev", 0))), inc=sT)
                    LS["nb8_T"] = sT.n
                    LS["xg_gate"] = max(LS["xg_gate"], sT.n)
                    vop(lambda v: v.tensor_copy(out=bB[:, 0:4],
                                                in_=pb[4][:, 0:4]),
                        waits=((sT, sT.n), (sC, LS.get("expC", 0))), inc=sV)
                    bB_v = sV.n

                    nkc = 4 * (qb + 1)
                    for pc in range(NPC):
                        for kc in range(nkc):
                            rel = kc - 4 * qb
                            c0 = 0 if rel < 0 else 128 * rel
                            smm_t = {}
                            for e in range(2):
                                sb_i = (kc % 2) * 2 + e
                                sbank = pb[sb_i]
                                mm(sbank[:, c0:512],
                                   K_(pc)[e * 64:(e + 1) * 64,
                                          kc * 128:(kc + 1) * 128],
                                   Q_(pc)[e * 64:(e + 1) * 64, c0:512],
                                   start=True, stop=True,
                                   waits=((sC, s_gate[sb_i]), (sV, bB_v)),
                                   inc=sT, tp=(e * 64, 0))
                                smm_t[e] = sT.n
                            expv = {}
                            trig = {}
                            if rel < 0:
                                par = kc % 2
                                act(ar(51456 + par * 1024, 1024),
                                    pbS[par][:, :], AF.Exp,
                                    bias=bB[:, pc:pc + 1], scale=0.125,
                                    waits=((sT, smm_t[1]),
                                           (sT, max(exp_gate[par * 2],
                                                    exp_gate[par * 2 + 1]))),
                                    inc=sC)
                                s_gate[par * 2] = s_gate[par * 2 + 1] = sC.n
                                LS["expC"] = sC.n
                                expv[0] = expv[1] = sC.n
                                trig[0] = trig[1] = 0
                            else:
                                for e in range(2):
                                    sb_i = (kc % 2) * 2 + e
                                    ebuf = sb_i
                                    act(expS_(ebuf)[:, c0:512],
                                        pb[sb_i][:, c0:512], AF.Exp,
                                        bias=bB[:, pc:pc + 1],
                                        scale=0.125,
                                        waits=((sT, smm_t[e]),
                                               (sT, exp_gate[ebuf])),
                                        inc=sC)
                                    s_gate[sb_i] = sC.n
                                    LS["expC"] = sC.n
                                    expv[e] = sC.n
                                    trig[e] = 0
                                    vop(lambda v, ebuf=ebuf, c0=c0:
                                        v.tensor_tensor(
                                            out=expS_(ebuf)[:, c0:c0 + 128],
                                            in0=expS_(ebuf)[:, c0:c0 + 128],
                                            in1=tri_b[:, :], op=ALU.mult),
                                        waits=((sC, expv[e]),), inc=sV)
                                    trig[e] = sV.n
                            for e in range(2):
                                sb_i = (kc % 2) * 2 + e
                                ebuf = sb_i
                                avw = [(sC, expv[e])]
                                if trig[e]:
                                    avw.append((sV, trig[e]))
                                if kc == 0:
                                    avw.append((sV, max(LS.get("ctxev", 0),
                                                        bB_v)))
                                mm(pb[4 + e][0:65, c0:512],
                                   V_(kc).rearrange("p (h w) -> p h w",
                                                    w=65)[:, pc * 2 + e, :],
                                   expS_(ebuf)[:, c0:512],
                                   start=(kc == 0), stop=(kc == nkc - 1),
                                   waits=tuple(avw), inc=sT)
                                exp_gate[ebuf] = sT.n
                        for e in range(2):
                            cbank = pb[4 + e]
                            av_done = exp_gate[(nkc - 1) % 2 * 2 + e]
                            vop(lambda v, cbank=cbank: v.tensor_scalar(
                                out=rrow[:, :], in0=cbank[64:65, :],
                                scalar1=1e-30, scalar2=None, op0=ALU.add),
                                waits=((sT, av_done),), inc=sV)
                            vop(lambda v: v.reciprocal(
                                out=rrow[:, :], in_=rrow[:, :]), inc=sV)
                            vop(lambda v: v.tensor_copy(
                                out=rrow_b[:, :], in_=rrow[:, :]), inc=sV)
                            rr_v = sV.n
                            mm(pb[6][0:64, :], onesrow_b[:, :], rrow_b[:, :],
                               start=True, stop=True,
                               waits=((sV, rr_v),
                                      (sV, LS.get("ctxev", 0))), inc=sT)
                            bc_t = sT.n
                            vop(lambda v, cbank=cbank: v.tensor_copy(
                                out=T1[0:64, :], in_=cbank[0:64, :]),
                                waits=((sT, bc_t),), inc=sV)
                            vop(lambda v, pc=pc, e=e: v.tensor_tensor(
                                out=ctxqb_(pc)[e * 64:(e + 1) * 64, :],
                                in0=T1[0:64, :], in1=pb[6][0:64, :],
                                op=ALU.mult), inc=sV)
                            LS["ctxev"] = sV.n
                    ctx_v = sV.n
                    if is0:
                        emit_precast(10)

                    if qb == 0:
                        for pc in range(NPC):
                            vop(lambda v, pc=pc: v.tensor_copy(
                                out=IMPCTX[:, pc * 17 + 1:pc * 17 + 17],
                                in_=ctxqb_(pc)[:, 1:17]), inc=sV)
                    if qb == 3:
                        for pc in range(NPC):
                            vop(lambda v, pc=pc: v.tensor_copy(
                                out=IMPCTX[:, pc * 17:pc * 17 + 1],
                                in_=ctxqb_(pc)[:, 511:512]), inc=sV)

                    if half2 == 0:
                        for pc in range(NPC):
                            vop(lambda v, pc=pc, pair=pair: v.tensor_scalar(
                                out=ctxself_(pc)[:,
                                                 pair * 512:(pair + 1) * 512],
                                in0=ctxqb_(pc), scalar1=pv(PV_ASELF),
                                scalar2=None, op0=ALU.mult),
                                waits=((sT, LS["arena_md_T"]),), inc=sV)
                            vop(lambda v, pc=pc: v.tensor_scalar(
                                out=contrib_(pc), in0=ctxqb_(pc),
                                scalar1=pv(PV_ACTX), scalar2=None,
                                op0=ALU.mult),
                                waits=((sGsc, LS.get("contrib_dma", 0)),),
                                inc=sV)
                    else:
                        for pc in range(NPC):
                            vop(lambda v, pc=pc, pair=pair:
                                v.scalar_tensor_tensor(
                                    out=ctxself_(pc)[:, pair * 512:
                                                     (pair + 1) * 512],
                                    in0=ctxqb_(pc), scalar=pv(PV_BSELF),
                                    in1=ctxself_(pc)[:, pair * 512:
                                                     (pair + 1) * 512],
                                    op0=ALU.mult, op1=ALU.add), inc=sV)
                            vop(lambda v, pc=pc: v.scalar_tensor_tensor(
                                out=contrib_(pc), in0=ctxqb_(pc),
                                scalar=pv(PV_BCTX), in1=contrib_(pc),
                                op0=ALU.mult, op1=ALU.add), inc=sV)
                        blend_v = sV.n
                        for pc in range(NPC):
                            P.dma("gpsimd",
                                  c_in[pc, :, pair * 512:(pair + 1) * 512],
                                  contrib_(pc), sGsc,
                                  waits=((sV, blend_v),
                                         (sCC, LS["cin_free_cc"])))
                        LS["contrib_dma"] = sGsc.n
            imp_v = sV.n
            for pc in range(NPC):
                P.dma("gpsimd", c_in[pc, :, 2048:2065],
                      IMPCTX[:, pc * 17:(pc + 1) * 17], sGsc,
                      waits=((sV, imp_v),))
            LS["contrib_dma"] = sGsc.n

            gop(lambda g: g.collective_compute(
                "AllGather", ALU.bypass,
                replica_groups=[[0, 1], [2, 3], [4, 5], [6, 7]],
                ins=[c_in[0:2].opt()], outs=[c_out[0].opt()]),
                waits=((sGsc, LS["contrib_dma"]),), inc=sCC)
            cgather_h1 = sCC.n
            gop(lambda g: g.collective_compute(
                "AllGather", ALU.bypass,
                replica_groups=[[0, 1], [2, 3], [4, 5], [6, 7]],
                ins=[c_in[2:4].opt()], outs=[c_out[1].opt()]), inc=sCC)
            cgather = sCC.n
            LS["cin_free_cc"] = cgather
            LS["gin_free_cc"] = cgather
            attn_T = sT.n

            for si in range(2):
                for pc in range(NPC):
                    P.dma("gpsimd", slot_(si, pc),
                          c_out[pc // 2, si, pc % 2, :, :], sGsc,
                          waits=((sCC, cgather_h1 if pc < 2 else cgather),
                                 (sT, attn_T)))
            for dc in range(NDC):
                P.dma("gpsimd", IMPRES[:, dc * 17:dc * 17 + 1],
                      xg_src(dc, 2047, 1), sGsc, waits=((sCC, cgather),),
                      slow=True)
                P.dma("gpsimd", IMPRES[:, dc * 17 + 1:dc * 17 + 17],
                      xg_src(dc, 1, 16), sGsc)
            impres_rdy = sGsc.n
            slots_rdy = sGsc.n

            for mc in range(NDC):
                for idx in range(12):
                    ci, c4 = idx // 4, idx % 4
                    stB.stage(w128_(idx),
                              wo[L][ci, c4 * 128:(c4 + 1) * 128,
                                    mc * 128:(mc + 1) * 128], 128,
                              reader_gate=((sT, LS.get("owT", attn_T)),))
                stB.flush()
                owc = sV.n
                LS["owT"] = 0
                for tt in range(NTT):
                    bank = pb[tt % 2]
                    nmm = 0
                    for ci in range(3):
                        for c4 in range(NPC):
                            wt = w128_(ci * 4 + c4)
                            if ci == 0:
                                rhs = ctxself_(c4)[:, tt * 512:(tt + 1) * 512]
                            else:
                                rhs = slot_(ci - 1, c4)[:,
                                                        tt * 512:(tt + 1) * 512]
                            nmm += 1
                            mm(bank[:, :], wt, rhs, start=(nmm == 1),
                               stop=(nmm == 12),
                               waits=((sV, owc), (sGsc, slots_rdy),
                                      (sV, LS.get(f"ores{tt % 2}", 0)))
                               if nmm == 1 else (),
                               inc=sT if nmm == 12 else None)
                    LS["owT"] = max(LS["owT"], sT.n)
                    if is0:
                        act(T1[:, :], bank[:, :], AF.Identity,
                            bias=ob_s[:, mc:mc + 1], waits=((sT, sT.n),),
                            inc=sC)
                        vop(lambda v, mc=mc, tt=tt: v.tensor_tensor(
                            out=X[mc][:, 1 + tt * 512:1 + (tt + 1) * 512],
                            in0=T1[:, :],
                            in1=X[mc][:, 1 + tt * 512:1 + (tt + 1) * 512],
                            op=ALU.add), waits=((sC, sC.n),), inc=sV)
                    else:
                        vop(lambda v, mc=mc, tt=tt, bank=bank: v.tensor_tensor(
                            out=X[mc][:, 1 + tt * 512:1 + (tt + 1) * 512],
                            in0=bank[:, :],
                            in1=X[mc][:, 1 + tt * 512:1 + (tt + 1) * 512],
                            op=ALU.add), waits=((sT, sT.n),), inc=sV)
                    LS[f"ores{tt % 2}"] = sV.n
                bank = pb[6]
                nmm = 0
                for ci in range(3):
                    for c4 in range(NPC):
                        wt = w128_(ci * 4 + c4)
                        if ci == 0:
                            rhs = IMPCTX[:, c4 * 17:(c4 + 1) * 17]
                        else:
                            rhs = slot_(ci - 1, c4)[:, 2048:2065]
                        nmm += 1
                        mm(bank[:, 0:17], wt, rhs, start=(nmm == 1),
                           stop=(nmm == 12),
                           waits=((sV, LS.get("impev", 0)),)
                           if nmm == 1 else (),
                           inc=sT if nmm == 12 else None)
                LS["owT"] = max(LS["owT"], sT.n)
                if is0:
                    act(IMPX[:, mc * 17:(mc + 1) * 17], bank[:, 0:17],
                        AF.Identity, bias=ob_s[:, mc:mc + 1],
                        waits=((sT, sT.n),), inc=sC)
                else:
                    act(IMPX[:, mc * 17:(mc + 1) * 17], bank[:, 0:17],
                        AF.Identity, waits=((sT, sT.n),), inc=sC)
                vop(lambda v, mc=mc: v.tensor_tensor(
                    out=IMPX[:, mc * 17:(mc + 1) * 17],
                    in0=IMPX[:, mc * 17:(mc + 1) * 17],
                    in1=IMPRES[:, mc * 17:(mc + 1) * 17], op=ALU.add),
                    waits=((sC, sC.n), (sGsc, impres_rdy)), inc=sV)
                LS["impev"] = sV.n
            LS["arena_md_T"] = sT.n
            LS["arena_lo_T"] = sT.n

            for mc in range(NDC):
                vop(lambda v, mc=mc: v.tensor_copy(
                    out=IMPB[:, :], in_=IMPX[:, mc * 17:(mc + 1) * 17]),
                    waits=((sT, LS.get("impstT", 0)),), inc=sV)
                vop(lambda v: v.tensor_tensor(
                    out=IMPSQ[:, :], in0=IMPB[:, :], in1=IMPB[:, :],
                    op=ALU.mult), inc=sV)
                mm(pb[4][0:1, 0:17], ones128b[:, :], IMPB[:, :],
                   start=(mc == 0), stop=(mc == NDC - 1),
                   waits=((sV, sV.n),), inc=sT)
                mm(pb[5][0:1, 0:17], ones128b[:, :], IMPSQ[:, :],
                   start=(mc == 0), stop=(mc == NDC - 1),
                   inc=sT)
                LS["impstT"] = sT.n
            ist_t = sT.n
            vop(lambda v: v.tensor_scalar(out=i1row[:, :],
                                          in0=pb[4][0:1, 0:17],
                                          scalar1=1.0 / D, scalar2=None,
                                          op0=ALU.mult),
                waits=((sT, ist_t),), inc=sV)
            vop(lambda v: v.tensor_scalar(out=i2row[:, :],
                                          in0=pb[5][0:1, 0:17],
                                          scalar1=1.0 / D, scalar2=None,
                                          op0=ALU.mult), inc=sV)
            vop(lambda v: v.tensor_tensor(out=iarow[:, :], in0=i1row[:, :],
                                          in1=i1row[:, :], op=ALU.mult),
                inc=sV)
            vop(lambda v: v.tensor_tensor(out=iarow[:, :], in0=i2row[:, :],
                                          in1=iarow[:, :], op=ALU.subtract),
                inc=sV)
            vop(lambda v: v.tensor_scalar(out=iarow[:, :], in0=iarow[:, :],
                                          scalar1=EPS, scalar2=None,
                                          op0=ALU.add), inc=sV)
            vop(lambda v: v.reciprocal(out=iarow[:, :], in_=iarow[:, :]),
                inc=sV)
            act(iarow[:, :], iarow[:, :], AF.Sqrt, waits=((sV, sV.n),),
                inc=sC)
            vop(lambda v: v.tensor_tensor(out=ibrow[:, :], in0=i1row[:, :],
                                          in1=iarow[:, :], op=ALU.mult),
                waits=((sC, sC.n),), inc=sV)
            iab_v = sV.n
            mm(pb[4][:, 0:17], ones128f[:, :], iarow[:, :], start=True,
               stop=True, waits=((sV, iab_v),), inc=sT)
            mm(pb[5][:, 0:17], ones128f[:, :], ibrow[:, :], start=True,
               stop=True, inc=sT)
            ibc_t = sT.n
            for mc in range(NDC):
                vop(lambda v, mc=mc: v.tensor_tensor(
                    out=IMPX[:, mc * 17:(mc + 1) * 17],
                    in0=IMPX[:, mc * 17:(mc + 1) * 17],
                    in1=pb[4][:, 0:17], op=ALU.mult),
                    waits=((sT, ibc_t),), inc=sV)
                vop(lambda v, mc=mc: v.tensor_tensor(
                    out=IMPX[:, mc * 17:(mc + 1) * 17],
                    in0=IMPX[:, mc * 17:(mc + 1) * 17],
                    in1=pb[5][:, 0:17], op=ALU.subtract), inc=sV)
                vop(lambda v, mc=mc: v.tensor_scalar(
                    out=X[mc][:, 0:1], in0=IMPX[:, mc * 17:mc * 17 + 1],
                    scalar1=pv(PV_G), scalar2=None, op0=ALU.mult), inc=sV)
                vop(lambda v, mc=mc: v.tensor_copy(
                    out=XIMP[:, mc * 16:(mc + 1) * 16],
                    in_=IMPX[:, mc * 17 + 1:(mc + 1) * 17]), inc=sV)

            if is0:
                emit_precast(len(PRE_TASKS))
            emit_ln(make_xb=(not is0))

            if is0:
                emit_ffn0()
            else:
                emit_ffn1()

            emit_ln(make_xb=False)
            LS["arena_hi_T"] = sT.n

        emit_layer(0)
        emit_layer(1)

        out_v = LS["lnap"]
        for dc in range(NDC):
            P.dma("gpsimd", yout[dc * 128:(dc + 1) * 128, :],
                  X[dc][:, 1:2049], sGm, waits=((sV, out_v),))
        P.wait("gpsimd", (sGm, sGm.n), (sGx, sGx.n), (sGsc, sGsc.n),
               (sGg, sGg.n))
        P.wait("sync", (sW, sW.n), (sWa, sWa.n), (sWb, sWb.n),
               (sW1[0], sW1[0].n), (sW1[1], sW1[1].n),
               (sW2[0], sW2[0].n), (sW2[1], sW2[1].n))

        with nc.Block() as block:
            @block.sync
            def _(sync):
                for f in P.q["sync"]:
                    f(sync)

            @block.tensor
            def _(tensor):
                for f in P.q["tensor"]:
                    f(tensor)

            @block.vector
            def _(vector):
                for f in P.q["vector"]:
                    f(vector)

            @block.scalar
            def _(scalar):
                for f in P.q["scalar"]:
                    f(scalar)

            @block.gpsimd
            def _(gpsimd):
                for f in P.q["gpsimd"]:
                    f(gpsimd)

    return nc


def kernel(**inp):
    f32 = np.float32

    def TT(a):
        return np.ascontiguousarray(np.asarray(a, dtype=f32).T)

    nc = build_nc()
    in_maps = []
    for c in range(8):
        b, g = c // 2, c % 2
        m = {}
        x = np.asarray(inp["L0_attn_x"], f32)
        m["x0"] = np.ascontiguousarray(x[b, g * T:(g + 1) * T, :].T)

        hsl = slice(g * 8, g * 8 + 8)
        psl = slice((1 - g) * 8, (1 - g) * 8 + 8)
        for nm, w in (("wq", "L0_attn_Q_w"), ("wk", "L0_attn_K_w"),
                      ("wv", "L0_attn_V_w")):
            W = np.asarray(inp[w], f32)[hsl]
            m[f"L0_{nm}"] = np.ascontiguousarray(
                W.transpose(1, 0, 2).reshape(D, HD))
        OW = np.asarray(inp["L0_attn_O_w"], f32)
        zz = np.zeros((HD, D), f32)
        part = np.ascontiguousarray(OW[psl].reshape(HD, D))
        m["L0_wo"] = np.stack([
            np.ascontiguousarray(OW[hsl].reshape(HD, D)),
            part if g == 1 else zz,
            part if g == 0 else zz])
        m["qbias"] = np.ascontiguousarray(
            np.asarray(inp["L0_attn_Q_b"], f32)[hsl].reshape(4, 128).T)
        m["kbias"] = np.ascontiguousarray(
            np.asarray(inp["L0_attn_K_b"], f32)[hsl].reshape(4, 128).T)
        m["vbias"] = np.asarray(
            inp["L0_attn_V_b"], f32)[hsl].reshape(1, HD).copy()
        m["obias"] = np.ascontiguousarray(
            np.asarray(inp["L0_attn_O_b"], f32).reshape(8, 128).T)

        for nm, w in (("wq", "L1_attn_w_q"), ("wk", "L1_attn_w_k"),
                      ("wv", "L1_attn_w_v")):
            W = np.asarray(inp[w], f32)[hsl]
            m[f"L1_{nm}"] = np.ascontiguousarray(W.reshape(HD, D).T)
        WO = np.asarray(inp["L1_attn_w_o"], f32)
        part = np.ascontiguousarray(WO[(1 - g) * HD:(2 - g) * HD])
        m["L1_wo"] = np.stack([
            np.ascontiguousarray(WO[g * HD:(g + 1) * HD]),
            part if g == 1 else zz,
            part if g == 0 else zz])

        m["wr0"] = TT(inp["L0_ffn_W_r"])
        m["wk0"] = TT(inp["L0_ffn_W_k"])
        m["wv0"] = TT(inp["L0_ffn_W_v"])
        m["mur"] = np.ascontiguousarray(
            np.asarray(inp["L0_ffn_mu_r"], f32).reshape(8, 128).T)
        m["muk"] = np.ascontiguousarray(
            np.asarray(inp["L0_ffn_mu_k"], f32).reshape(8, 128).T)

        m["w1t"] = TT(inp["L1_ffn_W1"])
        m["b1c"] = np.ascontiguousarray(
            np.asarray(inp["L1_ffn_b1"], f32).reshape(32, 128).T)
        m["w2t"] = TT(inp["L1_ffn_W2"])
        m["b2c"] = np.ascontiguousarray(
            np.asarray(inp["L1_ffn_b2"], f32).reshape(8, 128).T)
        m["scal"] = np.asarray(inp["L1_ffn_scale"], f32).reshape(1, 1)

        if g == 0:
            pvv = [0.0, 1.0, 1.0, 0.0, 0.0, 1.0, 0.0, 0.0]
        else:
            pvv = [1.0, 0.0, 0.0, 1.0, 1.0, 0.0, 1.0, 0.0]
        m["pvec"] = np.asarray(pvv, f32).reshape(1, 8)

        m["tric"] = np.ascontiguousarray(np.triu(np.ones((128, 128), f32)))
        m["idmc"] = np.eye(128, dtype=f32)
        e2 = np.zeros((128, 2), f32)
        e2[0:64, 0] = 1.0
        e2[64:128, 1] = 1.0
        m["e2cc"] = e2
        m["emc"] = np.eye(4, dtype=f32)
        in_maps.append(m)

    res = run_bass_kernel_spmd(nc, in_maps, core_ids=list(range(8)))
    out = np.empty((B, S, D), np.float32)
    for c in range(8):
        b, g = c // 2, c % 2
        out[b, g * T:(g + 1) * T, :] = res.results[c]["y"].T
    return out


if __name__ == "__main__":
    import reference as R
    inp = R.setup_inputs()
    got = kernel(**{k: np.asarray(v) for k, v in inp.items()})
    want = np.asarray(R.reference(**inp))
    num = float(((got - want) ** 2).mean())
    den = float((want ** 2).mean())
    print("resvar:", num / den)


# version 06
# speedup vs baseline: 1.2295x; 1.2295x over previous
import contextlib
import numpy as np

import concourse.bass as bass
import concourse.mybir as mybir
from concourse.bass_utils import run_bass_kernel_spmd

F32 = mybir.dt.float32
BF16 = mybir.dt.bfloat16
AF = mybir.ActivationFunctionType
ALU = mybir.AluOpType
AX = mybir.AxisListType

B, S, D = 4, 4096, 1024
T = 2048
HD = 512
NPC = 4
NDC = 8
NTT = 4
FH = 4096
EPS = 1e-5

ENGS = ("sync", "tensor", "vector", "scalar", "gpsimd")


class Sem:
    def __init__(self, h):
        self.h = h
        self.n = 0


class Prog:
    def __init__(self):
        self.q = {e: [] for e in ENGS}
        self.hw = {}

    def emit(self, eng, fn, waits=(), inc=None, inc_by=1):
        ss = getattr(self, "selfsem", {}).get(eng)
        if ss is not None:
            waits = tuple(waits) + ((ss, ss.n),)
            if inc is None:
                inc, inc_by = ss, 1
        xs = getattr(self, "crosssem", {}).get(eng)
        if xs is not None:
            waits = tuple(waits) + ((xs, xs.n),)
        eff = []
        for s, v in waits:
            k = (eng, id(s))
            if v > 0 and v > self.hw.get(k, 0):
                self.hw[k] = v
                eff.append((s, v))
        eff = tuple(eff)

        def g(obj, fn=fn, waits=eff, inc=inc, inc_by=inc_by):
            for s, v in waits:
                obj.wait_ge(s.h, v)
            r = fn(obj)
            if inc is not None:
                r.then_inc(inc.h, inc_by)

        self.q[eng].append(g)
        if inc is not None:
            inc.n += inc_by

    def dma(self, eng, out, in_, sem, waits=(), slow=False):
        eff = tuple((s, v) for (s, v) in waits if v > 0)
        for s, v in eff:
            k = (eng, id(s))
            self.hw[k] = max(self.hw.get(k, 0), v)

        def g(obj, out=out, in_=in_, slow=slow, eff=eff, sem=sem):
            for s, v in eff:
                obj.wait_ge(s.h, v)
            obj.dma_start(out=out, in_=in_,
                          allow_slow_non_contiguous=slow).then_inc(sem.h, 16)

        self.q[eng].append(g)
        sem.n += 16

    def wait(self, eng, *pairs):
        eff = []
        for s, v in pairs:
            k = (eng, id(s))
            if v > 0 and v > self.hw.get(k, 0):
                self.hw[k] = v
                eff.append((s, v))
        eff = tuple(eff)

        def g(obj):
            for s, v in eff:
                obj.wait_ge(s.h, v)

        self.q[eng].append(g)


def build_nc():
    nc = bass.Bass("TRN2", target_bir_lowering=False)

    def din(name, shape):
        return nc.dram_tensor(name, shape, F32, kind="ExternalInput").ap()

    x0 = din("x0", [D, T])
    wq, wk, wv, wo = {}, {}, {}, {}
    for L in (0, 1):
        wq[L] = din(f"L{L}_wq", [D, HD])
        wk[L] = din(f"L{L}_wk", [D, HD])
        wv[L] = din(f"L{L}_wv", [D, HD])
        wo[L] = din(f"L{L}_wo", [3, HD, D])
    qbias = din("qbias", [128, NPC])
    kbias = din("kbias", [128, NPC])
    vbias = din("vbias", [1, HD])
    obias = din("obias", [128, NDC])
    wr0 = din("wr0", [D, D])
    wk0 = din("wk0", [D, D])
    wv0 = din("wv0", [D, D])
    mur = din("mur", [128, NDC])
    muk = din("muk", [128, NDC])
    w1t = din("w1t", [D, FH])
    b1c = din("b1c", [128, 32])
    w2t = din("w2t", [FH, D])
    b2c = din("b2c", [128, NDC])
    scal = din("scal", [1, 1])
    pvec = din("pvec", [1, 8])
    tric = din("tric", [128, 128])
    idmc = din("idmc", [128, 128])
    e2cc = din("e2cc", [128, 2])
    emc = din("emc", [4, 4])

    yout = nc.dram_tensor("y", [D, T], F32, kind="ExternalOutput").ap()

    g_in = nc.dram_tensor("g_in", [D, T], BF16).ap()
    g_out = nc.dram_tensor("g_out", [2, 2, 512, T], BF16).ap()
    c_in = nc.dram_tensor("c_in", [NPC, 128, 2065], BF16).ap()
    c_out = nc.dram_tensor("c_out", [2, 2, 2, 128, 2065], BF16).ap()
    w1bf = nc.dram_tensor("w1bf", [D, FH], BF16).ap()
    w2bf = nc.dram_tensor("w2bf", [FH, D], BF16).ap()
    scrb = nc.dram_tensor("scrb", [4, 1024], BF16).ap()
    scrf = nc.dram_tensor("scrf", [1, 16], F32).ap()

    P = Prog()
    LS = {}

    with contextlib.ExitStack() as es:
        A = lambda *a: es.enter_context(nc.sbuf_tensor(*a))
        PSA = lambda *a: es.enter_context(nc.psum_tensor(*a))

        X = [A(f"X{i}", [128, 2049], F32) for i in range(NDC)]
        XIMP = A("XIMP", [128, NDC * 16], F32)
        IMPX = A("IMPX", [128, NDC * 17], F32)
        AR = A("AR", [128, 58112], BF16)

        def ar(off, n):
            return AR[:, off:off + n]

        K_ = lambda pc: ar(pc * 4096, 4096)
        V_ = lambda tc: ar(16384 + tc * 520, 520)
        slot_ = lambda s, pc: ar((s * NPC + pc) * 2065, 2065)
        wqb_ = lambda dc: ar(33024 + dc * 512, 512)
        wkb_ = lambda dc: ar(37120 + dc * 512, 512)
        wvb_ = lambda dc: ar(41216 + dc * 512, 512)
        ctxself_ = lambda pc: ar(37120 + pc * 2048, 2048)
        xg_ = lambda buf, dc: ar(45312 + dc * 512, 512)
        Q_ = lambda pc: ar(49408 + pc * 512, 512)
        expS_ = lambda i: ar(51456 + i * 512, 512)
        ctxqb_ = lambda pc: ar(53504 + pc * 512, 512)
        contrib_ = lambda pc: ar(55552 + pc * 512, 512)
        qsqv = ar(57600, 512)
        xb_ = lambda dc, off=0: ar(dc * 2048 + off, 512)
        kact_ = lambda hc, off=0: ar(16384 + hc * 2048 + off, 512)
        h1_ = lambda hc: ar(16384 + hc * 512, 512)
        w1buf_ = lambda buf, dc: ar(53504 + buf * 1024 + dc * 128, 128)
        w2buf_ = lambda buf, hc: ar(33024 + buf * 4096 + hc * 128, 128)
        w128_ = lambda i: ar(51456 + i * 128, 128)

        LTB = [A(f"LTB{dc}", [128, 16], BF16) for dc in range(NDC)]
        LTT = AR[0:16, 41216:42240]
        LTEM = AR[0:4, 42240:46336]
        LTJM = AR[0:4, 46336:50432]
        A4 = AR[0:4, 50432:52480]
        B4 = [AR[0:4, 55552 + j * 512: 56064 + j * 512] for j in range(4)]
        IMPCTX = A("IMPCTX", [128, NPC * 17], BF16)
        IMPRES = A("IMPRES", [128, NDC * 17], BF16)
        IMPB = A("IMPB", [128, 17], BF16)
        IMPSQ = A("IMPSQ", [128, 17], BF16)

        WST = A("WST", [128, 1024], F32)
        PCST = A("PCST", [128, 1024], BF16)
        WST2 = A("WST2", [128, 1664], F32)
        T1 = A("T1", [128, 512], F32)
        RT = A("RT", [128, 512], F32)
        XBT = PCST[:, 0:512]
        XSQT = PCST[:, 512:1024]

        tri_b = A("tri_b", [128, 128], BF16)
        idm_b = A("idm_b", [128, 128], BF16)
        idmf = A("idmf", [128, 128], F32)
        nb4 = A("nb4", [4, 1], F32)
        e2c_b = A("e2c_b", [128, 2], BF16)
        onesrow_b = A("onesrow_b", [1, 64], BF16)
        rrow_b = A("rrow_b", [1, 512], BF16)
        emask_s = A("emask_s", [4, 4], F32)
        nb8 = A("nb8", [1, 8], F32)
        ones128f = A("ones128f", [1, 128], F32)
        ones128b = A("ones128b", [128, 1], BF16)
        pvec_s = A("pvec_s", [1, 8], F32)
        pbc = A("pbc", [128, 8], F32)
        scal_s = A("scal_s", [1, 1], F32)
        scal_bc = A("scal_bc", [128, 1], F32)
        qb_s = A("qb_s", [128, NPC], F32)
        kb_s = A("kb_s", [128, NPC], F32)
        ob_s = A("ob_s", [128, NDC], F32)
        vb_s = A("vb_s", [1, HD], F32)
        mur_s = A("mur_s", [128, NDC], F32)
        muk_s = A("muk_s", [128, NDC], F32)
        b1_s = A("b1_s", [128, 32], F32)
        b2_s = A("b2_s", [128, NDC], F32)
        bkm = A("bkm", [2, NPC], F32)
        bqm = A("bqm", [2, NPC], F32)
        nbound = A("nbound", [2, NPC], F32)
        bB = A("bB", [128, 8], F32)
        lrow = A("lrow", [1, 512], F32)
        rrow = A("rrow", [1, 512], F32)
        m1row = A("m1row", [1, 512], F32)
        m2row = lrow
        arow = rrow
        brow = lrow
        i1row = A("i1row", [1, 17], F32)
        i2row = A("i2row", [1, 17], F32)
        iarow = A("iarow", [1, 17], F32)
        ibrow = A("ibrow", [1, 17], F32)

        pbS = [PSA("pbS0", [128, 1024], F32), PSA("pbS1", [128, 1024], F32)]
        pb = [pbS[0][:, 0:512], pbS[0][:, 512:1024],
              pbS[1][:, 0:512], pbS[1][:, 512:1024]] + \
             [PSA(f"pb{i}", [128, 512], F32) for i in range(4, 7)]
        pbt = PSA("pbt", [128, 1024], BF16)

        sW = Sem(es.enter_context(nc.semaphore("sW")))
        sWa = Sem(es.enter_context(nc.semaphore("sWa")))
        sWb = Sem(es.enter_context(nc.semaphore("sWb")))
        sW1 = [Sem(es.enter_context(nc.semaphore(f"sW1{i}"))) for i in range(2)]
        sW2 = [Sem(es.enter_context(nc.semaphore(f"sW2{i}"))) for i in range(2)]
        sGx = Sem(es.enter_context(nc.semaphore("sGx")))
        sGsc = Sem(es.enter_context(nc.semaphore("sGsc")))
        sGg = Sem(es.enter_context(nc.semaphore("sGg")))
        sGm = Sem(es.enter_context(nc.semaphore("sGm")))
        sT = Sem(es.enter_context(nc.semaphore("sT")))
        sV = Sem(es.enter_context(nc.semaphore("sV")))
        sC = Sem(es.enter_context(nc.semaphore("sC")))
        sP = Sem(es.enter_context(nc.semaphore("sP")))
        sCC = Sem(es.enter_context(nc.semaphore("sCC")))
        P.selfsem = {"vector": sV, "scalar": sC}
        P.crosssem = {"vector": sC, "scalar": sV}

        def mm(out, lhsT, rhs, start, stop, waits=(), inc=None, tp=None):
            P.emit("tensor",
                   lambda t, out=out, lhsT=lhsT, rhs=rhs, s0=start, s1=stop, tp=tp:
                   t.matmul(out, lhsT, rhs, start=s0, stop=s1, tile_position=tp),
                   waits=waits, inc=inc)

        def act(out, in_, func, bias=None, scale=1.0, waits=(), inc=None):
            kw = {}
            if bias is not None:
                kw["bias"] = bias
            P.emit("scalar",
                   lambda sc, out=out, in_=in_, f=func, scale=scale, kw=kw:
                   sc.activation(out=out, in_=in_, func=f, scale=scale, **kw),
                   waits=waits, inc=inc)

        def vop(fn, waits=(), inc=None):
            P.emit("vector", fn, waits=waits, inc=inc)

        def gop(fn, waits=(), inc=None):
            P.emit("gpsimd", fn, waits=waits, inc=inc)

        class Stager:

            def __init__(self, buf, slot_elems, nslots, sem):
                self.buf, self.se, self.ns = buf, slot_elems, nslots
                self.sem = sem
                self.gates = [0] * nslots
                self.i = 0
                self.pend = []
                self.last = 0

            def stage(self, dst_bf, src, n, reader_gate=()):
                s = self.i % self.ns
                self.i += 1
                stg = self.buf[:, s * self.se: s * self.se + n]
                P.dma("sync", stg, src, self.sem,
                      waits=((sV, max(self.gates[s],
                                      self.last if not self.pend else 0)),))
                self.pend.append((s, stg, dst_bf, tuple(reader_gate)))
                if len(self.pend) == self.ns:
                    self.flush()

            def flush(self):
                if not self.pend:
                    return
                tot = self.sem.n
                for s, stg, dst, rg in self.pend:
                    vop(lambda v, stg=stg, dst=dst: v.tensor_copy(
                        out=dst, in_=stg), waits=((self.sem, tot),) + rg,
                        inc=sV)
                    self.gates[s] = sV.n
                self.pend = []
                self.last = sV.n

        stA = Stager(WST, 512, 2, sWa)
        stB = Stager(WST2, 128, 13, sWb)

        PRE_TASKS = []
        for cc in range(NDC):
            for j in range(8):
                PRE_TASKS.append((w1bf[cc * 128:(cc + 1) * 128,
                                       j * 512:(j + 1) * 512],
                                  w1t[cc * 128:(cc + 1) * 128,
                                      j * 512:(j + 1) * 512]))
        for cc in range(32):
            for j in range(2):
                PRE_TASKS.append((w2bf[cc * 128:(cc + 1) * 128,
                                       j * 512:(j + 1) * 512],
                                  w2t[cc * 128:(cc + 1) * 128,
                                      j * 512:(j + 1) * 512]))
        PRE_STATE = {"i": 0, "gates": [0, 0]}

        def emit_precast(n):
            pout = []
            while n > 0 and PRE_STATE["i"] < len(PRE_TASKS):
                dst, srcw = PRE_TASKS[PRE_STATE["i"]]
                pslot = PRE_STATE["i"] % 2
                PRE_STATE["i"] += 1
                n -= 1
                pcb = PCST[:, pslot * 512:(pslot + 1) * 512]
                stA.stage(pcb, srcw, 512,
                          reader_gate=((sGg, PRE_STATE["gates"][pslot]),))
                pout.append((pslot, dst, pcb))
                if len(pout) == 2:
                    stA.flush()
                    cv = sV.n
                    for ps, d, pc_ in pout:
                        P.dma("gpsimd", d, pc_, sGg, waits=((sV, cv),))
                    PRE_STATE["gates"][0] = PRE_STATE["gates"][1] = sGg.n
                    pout.clear()
                    LS["pcst_dma"] = sGg.n
            assert not pout

        def xg_src(dc, col0, ncol):
            s = col0 // T
            assert (col0 + ncol - 1) // T == s
            o = col0 - s * T
            h = dc // 4
            return g_out[h, s, (dc % 4) * 128:(dc % 4 + 1) * 128, o:o + ncol]

        for dc in range(NDC):
            P.dma("sync", X[dc][:, 1:2049], x0[dc * 128:(dc + 1) * 128, :], sW)
        for (dst, src) in ((qb_s, qbias), (kb_s, kbias), (ob_s, obias),
                           (vb_s, vbias), (mur_s, mur), (muk_s, muk),
                           (b1_s, b1c), (b2_s, b2c), (pvec_s, pvec),
                           (scal_s, scal)):
            P.dma("sync", dst[:, :], src[:, :], sW)
        P.dma("sync", WST2[:, 0:128], tric[:, :], sW)
        P.dma("sync", WST2[:, 128:256], idmc[:, :], sW)
        P.dma("sync", WST[:, 0:2], e2cc[:, :], sW)
        P.dma("sync", emask_s[:, :], emc[:, :], sW)
        consts_l = sW.n
        x_loaded = sW.n
        vop(lambda v: v.tensor_copy(out=tri_b[:, :], in_=WST2[:, 0:128]),
            waits=((sW, consts_l),), inc=sV)
        vop(lambda v: v.tensor_copy(out=idm_b[:, :], in_=WST2[:, 128:256]),
            inc=sV)
        vop(lambda v: v.tensor_copy(out=idmf[:, :], in_=WST2[:, 128:256]),
            inc=sV)
        vop(lambda v: v.tensor_copy(out=e2c_b[:, :], in_=WST[:, 0:2]), inc=sV)
        vop(lambda v: v.memset(ones128f[:, :], 1.0), inc=sV)
        vop(lambda v: v.memset(ones128b[:, :], 1.0), inc=sV)
        vop(lambda v: v.memset(onesrow_b[:, :], 1.0), inc=sV)
        stA.gates[0] = sV.n
        stB.gates[0] = stB.gates[1] = sV.n
        consts_v = sV.n
        mm(pb[0][:, 0:8], ones128f[:, :], pvec_s[:, :], start=True, stop=True,
           waits=((sW, consts_l), (sV, consts_v)), inc=sT)
        mm(pb[0][:, 8:9], ones128f[:, :], scal_s[:, :], start=True, stop=True,
           inc=sT)
        vop(lambda v: v.tensor_copy(out=pbc[:, :], in_=pb[0][:, 0:8]),
            waits=((sT, sT.n),), inc=sV)
        vop(lambda v: v.tensor_copy(out=scal_bc[:, :], in_=pb[0][:, 8:9]),
            inc=sV)
        LS["pbc_v"] = sV.n

        PV_ACTX, PV_BCTX, PV_ASELF, PV_BSELF, PV_G, PV_AL, PV_BL = range(7)

        def pv(i):
            return pbc[:, i:i + 1]

        LS["xg_gate"] = 0
        LS["gin_free_cc"] = 0
        LS["cin_free_cc"] = 0
        LS["arena_lo_T"] = 0
        LS["arena_md_T"] = 0
        LS["arena_hi_T"] = 0

        def emit_ln(make_xb):
            for tt in range(NTT):
                for dc in range(NDC):
                    vop(lambda v, dc=dc, tt=tt: v.tensor_copy(
                        out=XBT[:, :],
                        in_=X[dc][:, 1 + tt * 512:1 + (tt + 1) * 512]),
                        waits=((sT, LS.get("lnst", 0)),
                               (sGg, LS.get("pcst_dma", 0))), inc=sV)
                    vop(lambda v: v.tensor_tensor(
                        out=XSQT[:, :], in0=XBT[:, :], in1=XBT[:, :],
                        op=ALU.mult), inc=sV)
                    mm(pb[2][0:1, :], ones128b[:, :], XBT[:, :],
                       start=(dc == 0), stop=(dc == NDC - 1),
                       waits=((sV, sV.n), (sV, LS.get("lnev", 0))),
                       inc=sT)
                    mm(pb[3][0:1, :], ones128b[:, :], XSQT[:, :],
                       start=(dc == 0), stop=(dc == NDC - 1),
                       inc=sT)
                    LS["lnst"] = sT.n
                st_t = sT.n
                vop(lambda v: v.tensor_scalar(
                    out=m1row[:, :], in0=pb[2][0:1, :], scalar1=1.0 / D,
                    scalar2=None, op0=ALU.mult), waits=((sT, st_t),), inc=sV)
                vop(lambda v: v.tensor_scalar(
                    out=m2row[:, :], in0=pb[3][0:1, :], scalar1=1.0 / D,
                    scalar2=None, op0=ALU.mult), inc=sV)
                LS["lnev"] = sV.n
                vop(lambda v: v.tensor_tensor(out=arow[:, :], in0=m1row[:, :],
                                              in1=m1row[:, :], op=ALU.mult),
                    inc=sV)
                vop(lambda v: v.tensor_tensor(out=arow[:, :], in0=m2row[:, :],
                                              in1=arow[:, :],
                                              op=ALU.subtract), inc=sV)
                vop(lambda v: v.tensor_scalar(out=arow[:, :], in0=arow[:, :],
                                              scalar1=EPS, scalar2=None,
                                              op0=ALU.add), inc=sV)
                vop(lambda v: v.reciprocal(out=arow[:, :], in_=arow[:, :]),
                    inc=sV)
                act(arow[:, :], arow[:, :], AF.Sqrt, waits=((sV, sV.n),),
                    inc=sC)
                vop(lambda v: v.tensor_tensor(out=brow[:, :], in0=m1row[:, :],
                                              in1=arow[:, :], op=ALU.mult),
                    waits=((sC, sC.n),), inc=sV)
                ab_v = sV.n
                mm(pb[2][:, :], ones128f[:, :], arow[:, :], start=True,
                   stop=True, waits=((sV, ab_v), (sV, LS.get("lnap", 0))),
                   inc=sT)
                mm(pb[3][:, :], ones128f[:, :], brow[:, :], start=True,
                   stop=True, inc=sT)
                bc_t = sT.n
                for dc in range(NDC):
                    vop(lambda v, dc=dc, tt=tt: v.tensor_tensor(
                        out=T1[:, :],
                        in0=X[dc][:, 1 + tt * 512:1 + (tt + 1) * 512],
                        in1=pb[2][:, :], op=ALU.mult),
                        waits=((sT, bc_t),), inc=sV)
                    vop(lambda v, dc=dc, tt=tt: v.tensor_tensor(
                        out=X[dc][:, 1 + tt * 512:1 + (tt + 1) * 512],
                        in0=T1[:, :], in1=pb[3][:, :], op=ALU.subtract),
                        inc=sV)
                    if make_xb:
                        vop(lambda v, dc=dc, tt=tt: v.tensor_copy(
                            out=xb_(dc, tt * 512),
                            in_=X[dc][:, 1 + tt * 512:1 + (tt + 1) * 512]),
                            waits=((sT, LS["arena_lo_T"]),), inc=sV)
                LS["lnap"] = sV.n

        def emit_ffn0():
            def blends(mu_s):
                for dc in range(NDC):
                    for tt in range(NTT):
                        vop(lambda v, dc=dc, tt=tt, mu_s=mu_s: v.tensor_scalar(
                            out=T1[:, :],
                            in0=X[dc][:, tt * 512:tt * 512 + 512],
                            scalar1=mu_s[:, dc:dc + 1], scalar2=-1.0,
                            op0=ALU.mult, op1=ALU.mult),
                            waits=((sT, LS["arena_lo_T"]),), inc=sV)
                        vop(lambda v, dc=dc, tt=tt: v.tensor_tensor(
                            out=T1[:, :],
                            in0=X[dc][:, tt * 512:tt * 512 + 512],
                            in1=T1[:, :], op=ALU.add), inc=sV)
                        vop(lambda v, dc=dc, tt=tt, mu_s=mu_s:
                            v.scalar_tensor_tensor(
                                out=xb_(dc, tt * 512),
                                in0=X[dc][:, 1 + tt * 512:1 + tt * 512 + 512],
                                scalar=mu_s[:, dc:dc + 1], in1=T1[:, :],
                                op0=ALU.mult, op1=ALU.add), inc=sV)

            blends(muk_s)
            xkb_v = sV.n

            for hc in range(NDC):
                for dc in range(NDC):
                    stB.stage(w128_(dc),
                              wk0[dc * 128:(dc + 1) * 128,
                                  hc * 128:(hc + 1) * 128], 128,
                              reader_gate=((sT, LS["arena_md_T"]),))
                stB.flush()
                wc_v = sV.n
                for tt in range(NTT):
                    bk = (hc % 2) * 2 + tt % 2
                    bank = pb[bk]
                    for dc in range(NDC):
                        mm(bank[:, :], w128_(dc), xb_(dc, tt * 512),
                           start=(dc == 0), stop=(dc == NDC - 1),
                           waits=((sV, wc_v), (sV, xkb_v),
                                  (sC, LS.get(f"f0e{bk}", 0)))
                           if dc == 0 else (),
                           inc=sT if dc == NDC - 1 else None)
                    LS["arena_md_T"] = sT.n
                    LS["arena_lo_T"] = sT.n
                    act(T1[:, :], bank[:, :], AF.Relu, waits=((sT, sT.n),),
                        inc=sC)
                    LS[f"f0e{bk}"] = sC.n
                    vop(lambda v, hc=hc, tt=tt: v.tensor_tensor(
                        out=kact_(hc, tt * 512), in0=T1[:, :], in1=T1[:, :],
                        op=ALU.mult),
                        waits=((sC, sC.n), (sT, LS["arena_lo_T"])),
                        inc=sV)
            kact_v = sV.n

            blends(mur_s)
            xrb_v = sV.n

            for mc in range(NDC):
                for dc in range(NDC):
                    stB.stage(w128_(dc),
                              wr0[dc * 128:(dc + 1) * 128,
                                  mc * 128:(mc + 1) * 128], 128,
                              reader_gate=((sT, LS["arena_md_T"]),))
                stB.flush()
                wr_v = sV.n
                for kc in range(NDC):
                    stB.stage(w128_(8 + kc),
                              wv0[kc * 128:(kc + 1) * 128,
                                  mc * 128:(mc + 1) * 128], 128,
                              reader_gate=((sT, LS["arena_md_T"]),))
                stB.flush()
                wv_v = sV.n
                for tt in range(NTT):
                    bank = pb[tt % 2]
                    for dc in range(NDC):
                        mm(bank[:, :], w128_(dc), xb_(dc, tt * 512),
                           start=(dc == 0), stop=(dc == NDC - 1),
                           waits=((sV, wr_v), (sV, xrb_v),
                                  (sC, LS.get(f"f0r{tt % 2}", 0)))
                           if dc == 0 else (),
                           inc=sT if dc == NDC - 1 else None)
                    LS["arena_md_T"] = sT.n
                    LS["arena_lo_T"] = sT.n
                    act(RT[:, :], bank[:, :], AF.Sigmoid,
                        waits=((sT, sT.n), (sV, LS.get(f"f0v{tt % 2}", 0))),
                        inc=sC)
                    LS[f"f0r{tt % 2}"] = sC.n
                    rsig_c = sC.n
                    bank2 = pb[2 + tt % 2]
                    for kc in range(NDC):
                        mm(bank2[:, :], w128_(8 + kc), kact_(kc, tt * 512),
                           start=(kc == 0), stop=(kc == NDC - 1),
                           waits=((sV, wv_v), (sV, kact_v),
                                  (sV, LS.get(f"f0o{tt % 2}", 0)))
                           if kc == 0 else (),
                           inc=sT if kc == NDC - 1 else None)
                    LS["arena_md_T"] = sT.n
                    vop(lambda v, bank2=bank2: v.tensor_tensor(
                        out=T1[:, :], in0=bank2[:, :],
                        in1=RT[:, :], op=ALU.mult),
                        waits=((sT, sT.n), (sC, rsig_c)), inc=sV)
                    LS[f"f0v{tt % 2}"] = sV.n
                    vop(lambda v, mc=mc, tt=tt: v.tensor_tensor(
                        out=X[mc][:, 1 + tt * 512:1 + (tt + 1) * 512],
                        in0=T1[:, :],
                        in1=X[mc][:, 1 + tt * 512:1 + (tt + 1) * 512],
                        op=ALU.add), inc=sV)
                    LS[f"f0o{tt % 2}"] = sV.n
            LS["arena_lo_T"] = sT.n
            LS["arena_md_T"] = sT.n

        def emit_ffn1():
            xb_v = LS["lnap"]

            for dc in range(NDC):
                vop(lambda v, dc=dc: v.tensor_scalar(
                    out=XBT[:, 0:16], in0=XIMP[:, dc * 16:(dc + 1) * 16],
                    scalar1=pv(PV_BL), scalar2=None, op0=ALU.mult),
                    waits=((sV, xb_v),), inc=sV)
                vop(lambda v, dc=dc: v.scalar_tensor_tensor(
                    out=LTB[dc][:, :], in0=X[dc][:, 2:18],
                    scalar=pv(PV_AL), in1=XBT[:, 0:16],
                    op0=ALU.mult, op1=ALU.add), inc=sV)
            ltb_v = sV.n
            vop(lambda v: v.memset(LTEM[:, :], 0.0),
                waits=((sT, LS["arena_md_T"]),), inc=sV)
            vop(lambda v: v.memset(LTJM[:, :], 0.0), inc=sV)
            for dc in range(NDC):
                P.emit("tensor",
                       lambda t, dc=dc: t.transpose(
                           pbt[0:16, 0:128], LTB[dc][:, :], idm_b[:, :]),
                       waits=((sV, ltb_v), (sV, LS.get("ltt_ev", 0))), inc=sT)
                vop(lambda v, dc=dc: v.tensor_copy(
                    out=LTT[:, dc * 128:(dc + 1) * 128], in_=pbt[0:16, 0:128]),
                    waits=((sT, sT.n),), inc=sV)
                LS["ltt_ev"] = sV.n
                P.emit("tensor",
                       lambda t, dc=dc: t.transpose(
                           pbt[0:4, 0:128],
                           LTB[dc].rearrange("p (a b) -> p a b", b=2)[:, 0:4, 0],
                           idm_b[:, :]),
                       waits=((sV, LS["ltt_ev"]),), inc=sT)
                for i in range(4):
                    vop(lambda v, dc=dc, i=i: v.tensor_scalar(
                        out=LTEM[0:4, i * 1024 + dc * 128:
                                 i * 1024 + (dc + 1) * 128],
                        in0=pbt[0:4, 0:128], scalar1=emask_s[:, i:i + 1],
                        scalar2=None, op0=ALU.mult),
                        waits=((sT, sT.n),), inc=sV)
                LS["ltt_ev"] = sV.n
            ltt_v = sV.n
            for i in range(4):
                P.dma("gpsimd", scrb[i, :], LTT[8 + 2 * i:9 + 2 * i, :], sGm,
                      waits=((sV, ltt_v),))
            sc_done = sGm.n
            for i in range(4):
                for j in range(4):
                    for half in range(2):
                        P.dma("gpsimd",
                              LTJM[i:i + 1, i * 1024 + j * 256 + half * 128:
                                   i * 1024 + j * 256 + half * 128 + 128],
                              scrb[i:i + 1,
                                   512 * half + j: 512 * half + j + 509:4],
                              sGm, waits=((sGm, sc_done), (sV, ltt_v)),
                              slow=True)
            ltj_done = sGm.n

            for tt in range(NTT):
                for dc in range(NDC):
                    mm(pb[4][0:4, :],
                       LTB[dc].rearrange("p (a b) -> p a b", b=2)[:, 0:4, 1],
                       xb_(dc, tt * 512),
                       start=(dc == 0), stop=(dc == NDC - 1),
                       waits=((sV, ltb_v), (sV, xb_v),
                              (sC, LS.get("a4ev", 0)))
                       if dc == 0 else (),
                       inc=sT if dc == NDC - 1 else None)
                act(A4[0:4, tt * 512:(tt + 1) * 512], pb[4][0:4, :],
                    AF.Identity, scale=scal_bc[0:4, 0:1],
                    waits=((sT, sT.n),), inc=sC)
                LS["a4ev"] = sC.n
            a4_c = sC.n
            LS["arena_lo_T"] = sT.n

            wbuf_gate = [LS["arena_md_T"], LS["arena_md_T"]]
            w2buf_gate = [LS["arena_md_T"], LS["arena_md_T"]]
            for tt in range(NTT):
                for hc in range(32):
                    bufw = hc % 2
                    for dc in range(NDC):
                        P.dma("sync", w1buf_(bufw, dc),
                              w1bf[dc * 128:(dc + 1) * 128,
                                   hc * 128:(hc + 1) * 128], sW1[bufw],
                              waits=((sT, wbuf_gate[bufw]),
                                     (sGg, LS.get("pcst_dma", 0)))
                              if dc == 0 else ())
                    w_rdy = sW1[bufw].n
                    bank = pb[hc % 2]
                    i4 = hc // 8
                    for dc in range(NDC):
                        mm(bank[:, :], w1buf_(bufw, dc), xb_(dc, tt * 512),
                           start=(dc == 0), stop=False,
                           waits=((sW1[bufw], w_rdy), (sV, xb_v),
                                  (sC, LS.get(f"f1h{hc % 2}", 0)))
                           if dc == 0 else ())
                    mm(bank[:, :],
                       LTEM[0:4, i4 * 1024 + (hc % 8) * 128:
                            i4 * 1024 + (hc % 8) * 128 + 128],
                       A4[0:4, tt * 512:(tt + 1) * 512],
                       start=False, stop=True,
                       waits=((sV, ltt_v), (sC, a4_c)), inc=sT)
                    wbuf_gate[bufw] = sT.n
                    act(h1_(hc), bank[:, :], AF.Gelu, bias=b1_s[:, hc:hc + 1],
                        waits=((sT, sT.n),), inc=sC)
                    LS[f"f1h{hc % 2}"] = sC.n
                h_c = sC.n
                LS["arena_lo_T"] = sT.n

                for j in range(4):
                    for dc in range(NDC):
                        hc = j * 8 + dc
                        mm(pb[4][0:4, :],
                           LTB[dc].rearrange("p (a b) -> p a b", b=2)[:, 4:8, 1],
                           h1_(hc), start=(dc == 0), stop=(dc == NDC - 1),
                           waits=((sC, h_c), (sC, LS.get("b4c", 0)))
                           if dc == 0 else (),
                           inc=sT if dc == NDC - 1 else None)
                    act(B4[j][:, :], pb[4][0:4, :], AF.Identity,
                        scale=scal_bc[0:4, 0:1], waits=((sT, sT.n),), inc=sC)
                    LS["b4c"] = sC.n
                b4_c = sC.n

                for mc in range(NDC):
                    bufw = mc % 2
                    i4 = mc // 2
                    half = mc % 2
                    for hc in range(32):
                        P.dma("sync", w2buf_(bufw, hc),
                              w2bf[hc * 128:(hc + 1) * 128,
                                   mc * 128:(mc + 1) * 128], sW2[bufw],
                              waits=((sT, w2buf_gate[bufw]),
                                     (sGg, LS.get("pcst_dma", 0)))
                              if hc == 0 else ())
                    w_rdy = sW2[bufw].n
                    bank = pb[2 + mc % 2]
                    for hc in range(32):
                        mm(bank[:, :], w2buf_(bufw, hc), h1_(hc),
                           start=(hc == 0), stop=False,
                           waits=((sW2[bufw], w_rdy), (sC, h_c),
                                  (sV, LS.get(f"f1o{mc % 2}", 0)))
                           if hc == 0 else ())
                    for j in range(4):
                        mm(bank[:, :],
                           LTJM[0:4, i4 * 1024 + j * 256 + half * 128:
                                i4 * 1024 + j * 256 + half * 128 + 128],
                           B4[j][0:4, :],
                           start=False, stop=(j == 3),
                           waits=((sC, b4_c), (sGm, ltj_done)),
                           inc=sT if j == 3 else None)
                    w2buf_gate[bufw] = sT.n
                    act(T1[:, :], bank[:, :], AF.Identity,
                        bias=b2_s[:, mc:mc + 1], waits=((sT, sT.n),), inc=sC)
                    vop(lambda v, mc=mc, tt=tt: v.tensor_tensor(
                        out=X[mc][:, 1 + tt * 512:1 + (tt + 1) * 512],
                        in0=T1[:, :],
                        in1=X[mc][:, 1 + tt * 512:1 + (tt + 1) * 512],
                        op=ALU.add), waits=((sC, sC.n),), inc=sV)
                    LS[f"f1o{mc % 2}"] = sV.n
            LS["arena_lo_T"] = sT.n
            LS["arena_md_T"] = sT.n

        def emit_layer(L):
            is0 = (L == 0)

            src_gate = ((sW, x_loaded),) if is0 else ()
            for dc in range(NDC):
                stgb = ar(45312 + (dc % 2) * 2048, 2048)
                vop(lambda v, dc=dc, stgb=stgb: v.tensor_copy(
                    out=stgb, in_=X[dc][:, 1:2049]),
                    waits=src_gate + ((sT, LS["arena_hi_T"]),
                                      (sGg, LS.get("stgb_dma", 0))), inc=sV)
                P.dma("gpsimd", g_in[dc * 128:(dc + 1) * 128, :], stgb, sGg,
                      waits=((sV, sV.n),))
                LS["stgb_dma"] = sGg.n
            gin_done = sGg.n
            gop(lambda g: g.collective_compute(
                "AllGather", ALU.bypass,
                replica_groups=[[0, 1], [2, 3], [4, 5], [6, 7]],
                ins=[g_in[0:512, :].opt()], outs=[g_out[0].opt()]),
                waits=((sGg, gin_done),), inc=sCC)
            xgather_h1 = sCC.n
            gop(lambda g: g.collective_compute(
                "AllGather", ALU.bypass,
                replica_groups=[[0, 1], [2, 3], [4, 5], [6, 7]],
                ins=[g_in[512:1024, :].opt()],
                outs=[g_out[1].opt()]), inc=sCC)
            xgather = sCC.n

            mdgate = ((sT, LS["arena_md_T"]),)
            for dc in range(NDC):
                stA.stage(wqb_(dc), wq[L][dc * 128:(dc + 1) * 128, :], 512,
                          reader_gate=mdgate)
                stA.stage(wkb_(dc), wk[L][dc * 128:(dc + 1) * 128, :], 512,
                          reader_gate=mdgate)
                stA.stage(wvb_(dc), wv[L][dc * 128:(dc + 1) * 128, :], 512,
                          reader_gate=mdgate)
            stA.flush()
            qkvw_v = sV.n

            for tc in range(32):
                vop(lambda v, tc=tc: v.memset(
                    V_(tc).rearrange("p (h w) -> p h w", w=65)[:, :, 64:65],
                    1.0), waits=((sT, LS["arena_lo_T"]),) if tc == 0 else ())
            vop(lambda v: v.memset(IMPCTX[:, :], 0.0),
                waits=((sGsc, LS.get("contrib_dma", 0)),), inc=sV)
            vzero_v = sV.n

            for tt8 in range(8):
                buf = tt8 % 2
                for dc in range(NDC):
                    P.dma("gpsimd", xg_(buf, dc), xg_src(dc, tt8 * 512, 512),
                          sGx, waits=((sCC, xgather_h1 if dc < 4 else xgather),
                                      (sT, LS["xg_gate"])))
                tile_rdy = sGx.n
                for pc in range(NPC):
                    bank = pb[pc % 2]
                    for dc in range(NDC):
                        mm(bank[:, :], wkb_(dc)[:, pc * 128:(pc + 1) * 128],
                           xg_(buf, dc), start=(dc == 0), stop=(dc == NDC - 1),
                           waits=((sGx, tile_rdy), (sV, qkvw_v),
                                  (sV, vzero_v),
                                  (sV, LS.get(f"kev{pc % 2}", 0)))
                           if dc == 0 else (),
                           inc=sT if dc == NDC - 1 else None)
                    if is0:
                        vop(lambda v, pc=pc, tt8=tt8, bank=bank: v.tensor_scalar(
                            out=K_(pc)[:, tt8 * 512:(tt8 + 1) * 512],
                            in0=bank[:, :], scalar1=kb_s[:, pc:pc + 1],
                            scalar2=None, op0=ALU.add),
                            waits=((sT, sT.n),), inc=sV)
                    else:
                        vop(lambda v, pc=pc, tt8=tt8, bank=bank: v.tensor_copy(
                            out=K_(pc)[:, tt8 * 512:(tt8 + 1) * 512],
                            in_=bank[:, :]), waits=((sT, sT.n),), inc=sV)
                    LS[f"kev{pc % 2}"] = sV.n
                for st in range(4):
                    tc = tt8 * 4 + st
                    bank = pb[2 + st % 2]
                    for dc in range(NDC):
                        mm(bank[:, :], xg_(buf, dc)[:, st * 128:(st + 1) * 128],
                           wvb_(dc), start=(dc == 0),
                           stop=(dc == NDC - 1 and not is0),
                           waits=((sV, LS.get(f"vev{st % 2}", 0)),)
                           if dc == 0 else (),
                           inc=sT if (dc == NDC - 1 and not is0) else None)
                    if is0:
                        mm(bank[:, :], ones128f[:, :], vb_s[:, :],
                           start=False, stop=True, inc=sT)
                    vop(lambda v, tc=tc, bank=bank: v.tensor_copy(
                        out=V_(tc).rearrange("p (h w) -> p h w",
                                             w=65)[:, :, 0:64],
                        in_=bank[:, :].rearrange("p (h w) -> p h w", w=64)),
                        waits=((sT, sT.n),), inc=sV)
                    LS[f"vev{st % 2}"] = sV.n
                LS["xg_gate"] = sT.n
            kproj_v = sV.n
            LS["arena_md_T"] = sT.n

            for pc in range(NPC):
                for i8 in range(8):
                    vop(lambda v, pc=pc, i8=i8: v.tensor_tensor(
                        out=qsqv[:, :], in0=K_(pc)[:, i8 * 512:(i8 + 1) * 512],
                        in1=K_(pc)[:, i8 * 512:(i8 + 1) * 512], op=ALU.mult),
                        waits=((sV, kproj_v), (sT, LS.get("nrmT", 0))),
                        inc=sV)
                    mm(pb[4][0:2, :], e2c_b[:, :], qsqv[:, :],
                       start=(i8 == 0), stop=(i8 == 7),
                       waits=((sV, sV.n), (sV, LS.get("ctxev", 0))),
                       inc=sT)
                    LS["nrmT"] = sT.n
                vop(lambda v, pc=pc: v.reduce_max(
                    out=bkm[:, pc:pc + 1], in_=pb[4][0:2, :], axis=AX.X),
                    waits=((sT, sT.n),), inc=sV)
            knorm_v = sV.n

            s_gate = [0, 0, 0, 0]
            exp_gate = [0, 0, 0, 0]
            qev_gate = [0, 0]
            for pair in range(4):
                for half2 in range(2):
                    qb = pair + half2 * 4
                    for dc in range(NDC):
                        P.dma("gpsimd", xg_(1, dc), xg_src(dc, qb * 512, 512),
                              sGx, waits=((sT, LS["xg_gate"]),))
                    qtile_rdy = sGx.n
                    for pc in range(NPC):
                        bank = pb[pc % 2]
                        for dc in range(NDC):
                            mm(bank[:, :],
                               wqb_(dc)[:, pc * 128:(pc + 1) * 128],
                               xg_(1, dc), start=(dc == 0),
                               stop=(dc == NDC - 1),
                               waits=((sGx, qtile_rdy),
                                      (sV, qev_gate[pc % 2]),
                                      (sC, s_gate[pc % 2]))
                               if dc == 0 else (),
                               inc=sT if dc == NDC - 1 else None)
                        if is0:
                            vop(lambda v, pc=pc, bank=bank: v.tensor_scalar(
                                out=Q_(pc), in0=bank[:, :],
                                scalar1=qb_s[:, pc:pc + 1], scalar2=None,
                                op0=ALU.add), waits=((sT, sT.n),), inc=sV)
                        else:
                            vop(lambda v, pc=pc, bank=bank: v.tensor_copy(
                                out=Q_(pc), in_=bank[:, :]),
                                waits=((sT, sT.n),), inc=sV)
                        qev_gate[pc % 2] = sV.n
                    LS["xg_gate"] = sT.n
                    qproj_v = sV.n

                    for pc in range(NPC):
                        vop(lambda v, pc=pc: v.tensor_tensor(
                            out=qsqv[:, :], in0=Q_(pc), in1=Q_(pc),
                            op=ALU.mult),
                            waits=((sV, qproj_v), (sT, LS.get("nrmT", 0))),
                            inc=sV)
                        mm(pb[4][0:2, :], e2c_b[:, :], qsqv[:, :],
                           start=True, stop=True,
                           waits=((sV, sV.n), (sV, LS.get("ctxev", 0))),
                           inc=sT)
                        LS["nrmT"] = sT.n
                        vop(lambda v, pc=pc: v.reduce_max(
                            out=bqm[:, pc:pc + 1], in_=pb[4][0:2, :],
                            axis=AX.X), waits=((sT, sT.n),), inc=sV)
                    vop(lambda v: v.tensor_tensor(
                        out=nbound[:, :], in0=bqm[:, :], in1=bkm[:, :],
                        op=ALU.mult),
                        waits=((sV, knorm_v), (sGm, LS.get("nb_dma", 0))),
                        inc=sV)
                    nbm_v = sV.n
                    P.emit("tensor",
                           lambda t: t.transpose(pb[6][0:4, 0:2],
                                                 nbound[:, :], idmf[0:2, 0:2]),
                           waits=((sV, nbm_v), (sV, LS.get("ctxev", 0))),
                           inc=sT)
                    vop(lambda v: v.reduce_max(out=nb4[:, :],
                                               in_=pb[6][0:4, 0:2],
                                               axis=AX.X),
                        waits=((sT, sT.n),), inc=sV)
                    act(nb4[:, :], nb4[:, :], AF.Sqrt, waits=((sV, sV.n),),
                        inc=sC)
                    vop(lambda v: v.tensor_scalar(
                        out=nb4[:, :], in0=nb4[:, :], scalar1=-0.125,
                        scalar2=44.0, op0=ALU.mult, op1=ALU.add),
                        waits=((sC, sC.n),), inc=sV)
                    nb_v = sV.n
                    P.dma("gpsimd", scrf[0, 0:4], nb4[:, :], sGm,
                          waits=((sV, nb_v), (sGm, LS.get("nb_dma", 0))))
                    P.dma("gpsimd", nb8[:, 0:4], scrf[0, 0:4], sGm,
                          waits=((sGm, sGm.n), (sT, LS.get("nb8_T", 0))))
                    LS["nb_dma"] = sGm.n
                    mm(pb[4][:, 0:4], ones128f[:, :], nb8[:, 0:4], start=True,
                       stop=True, waits=((sGm, LS["nb_dma"]),
                                         (sV, LS.get("ctxev", 0))), inc=sT)
                    LS["nb8_T"] = sT.n
                    LS["xg_gate"] = max(LS["xg_gate"], sT.n)
                    vop(lambda v: v.tensor_copy(out=bB[:, 0:4],
                                                in_=pb[4][:, 0:4]),
                        waits=((sT, sT.n), (sC, LS.get("expC", 0))), inc=sV)
                    bB_v = sV.n

                    nkc = 4 * (qb + 1)
                    for pc in range(NPC):
                        pend = None

                        def emit_av(pc, kc, c0, expv, trig, nkc):
                            for e in range(2):
                                sb_i = (kc % 2) * 2 + e
                                avw = [(sC, expv[e])]
                                if trig[e]:
                                    avw.append((sV, trig[e]))
                                if kc == 0:
                                    avw.append((sV, max(LS.get("ctxev", 0),
                                                        bB_v)))
                                mm(pb[4 + e][0:65, c0:512],
                                   V_(kc).rearrange("p (h w) -> p h w",
                                                    w=65)[:, pc * 2 + e, :],
                                   expS_(sb_i)[:, c0:512],
                                   start=(kc == 0), stop=(kc == nkc - 1),
                                   waits=tuple(avw), inc=sT)
                                exp_gate[sb_i] = sT.n

                        for kc in range(nkc):
                            rel = kc - 4 * qb
                            c0 = 0 if rel < 0 else 128 * rel
                            smm_t = {}
                            for e in range(2):
                                sb_i = (kc % 2) * 2 + e
                                sbank = pb[sb_i]
                                mm(sbank[:, c0:512],
                                   K_(pc)[e * 64:(e + 1) * 64,
                                          kc * 128:(kc + 1) * 128],
                                   Q_(pc)[e * 64:(e + 1) * 64, c0:512],
                                   start=True, stop=True,
                                   waits=((sC, s_gate[sb_i]), (sV, bB_v)),
                                   inc=sT, tp=(e * 64, 0))
                                smm_t[e] = sT.n
                            expv = {}
                            trig = {}
                            if rel < 0:
                                par = kc % 2
                                act(ar(51456 + par * 1024, 1024),
                                    pbS[par][:, :], AF.Exp,
                                    bias=bB[:, pc:pc + 1], scale=0.125,
                                    waits=((sT, smm_t[1]),
                                           (sT, max(exp_gate[par * 2],
                                                    exp_gate[par * 2 + 1]))),
                                    inc=sC)
                                s_gate[par * 2] = s_gate[par * 2 + 1] = sC.n
                                LS["expC"] = sC.n
                                expv[0] = expv[1] = sC.n
                                trig[0] = trig[1] = 0
                            else:
                                for e in range(2):
                                    sb_i = (kc % 2) * 2 + e
                                    act(expS_(sb_i)[:, c0:512],
                                        pb[sb_i][:, c0:512], AF.Exp,
                                        bias=bB[:, pc:pc + 1], scale=0.125,
                                        waits=((sT, smm_t[e]),
                                               (sT, exp_gate[sb_i])),
                                        inc=sC)
                                    s_gate[sb_i] = sC.n
                                    LS["expC"] = sC.n
                                    expv[e] = sC.n
                                    vop(lambda v, sb_i=sb_i, c0=c0:
                                        v.tensor_tensor(
                                            out=expS_(sb_i)[:, c0:c0 + 128],
                                            in0=expS_(sb_i)[:, c0:c0 + 128],
                                            in1=tri_b[:, :], op=ALU.mult),
                                        waits=((sC, expv[e]),), inc=sV)
                                    trig[e] = sV.n
                            if pend is not None:
                                emit_av(pc, *pend, nkc)
                            pend = (kc, c0, expv, trig)
                        emit_av(pc, *pend, nkc)
                        for e in range(2):
                            cbank = pb[4 + e]
                            av_done = exp_gate[(nkc - 1) % 2 * 2 + e]
                            vop(lambda v, cbank=cbank: v.tensor_scalar(
                                out=rrow[:, :], in0=cbank[64:65, :],
                                scalar1=1e-30, scalar2=None, op0=ALU.add),
                                waits=((sT, av_done),), inc=sV)
                            vop(lambda v: v.reciprocal(
                                out=rrow[:, :], in_=rrow[:, :]), inc=sV)
                            vop(lambda v: v.tensor_copy(
                                out=rrow_b[:, :], in_=rrow[:, :]), inc=sV)
                            rr_v = sV.n
                            mm(pb[6][0:64, :], onesrow_b[:, :], rrow_b[:, :],
                               start=True, stop=True,
                               waits=((sV, rr_v),
                                      (sV, LS.get("ctxev", 0))), inc=sT)
                            bc_t = sT.n
                            vop(lambda v, cbank=cbank: v.tensor_copy(
                                out=T1[0:64, :], in_=cbank[0:64, :]),
                                waits=((sT, bc_t),), inc=sV)
                            vop(lambda v, pc=pc, e=e: v.tensor_tensor(
                                out=ctxqb_(pc)[e * 64:(e + 1) * 64, :],
                                in0=T1[0:64, :], in1=pb[6][0:64, :],
                                op=ALU.mult), inc=sV)
                            LS["ctxev"] = sV.n
                    ctx_v = sV.n
                    if is0:
                        emit_precast(10)

                    if qb == 0:
                        for pc in range(NPC):
                            vop(lambda v, pc=pc: v.tensor_copy(
                                out=IMPCTX[:, pc * 17 + 1:pc * 17 + 17],
                                in_=ctxqb_(pc)[:, 1:17]), inc=sV)
                    if qb == 3:
                        for pc in range(NPC):
                            vop(lambda v, pc=pc: v.tensor_copy(
                                out=IMPCTX[:, pc * 17:pc * 17 + 1],
                                in_=ctxqb_(pc)[:, 511:512]), inc=sV)

                    if half2 == 0:
                        for pc in range(NPC):
                            vop(lambda v, pc=pc, pair=pair: v.tensor_scalar(
                                out=ctxself_(pc)[:,
                                                 pair * 512:(pair + 1) * 512],
                                in0=ctxqb_(pc), scalar1=pv(PV_ASELF),
                                scalar2=None, op0=ALU.mult),
                                waits=((sT, LS["arena_md_T"]),), inc=sV)
                            vop(lambda v, pc=pc: v.tensor_scalar(
                                out=contrib_(pc), in0=ctxqb_(pc),
                                scalar1=pv(PV_ACTX), scalar2=None,
                                op0=ALU.mult),
                                waits=((sGsc, LS.get("contrib_dma", 0)),),
                                inc=sV)
                    else:
                        for pc in range(NPC):
                            vop(lambda v, pc=pc, pair=pair:
                                v.scalar_tensor_tensor(
                                    out=ctxself_(pc)[:, pair * 512:
                                                     (pair + 1) * 512],
                                    in0=ctxqb_(pc), scalar=pv(PV_BSELF),
                                    in1=ctxself_(pc)[:, pair * 512:
                                                     (pair + 1) * 512],
                                    op0=ALU.mult, op1=ALU.add), inc=sV)
                            vop(lambda v, pc=pc: v.scalar_tensor_tensor(
                                out=contrib_(pc), in0=ctxqb_(pc),
                                scalar=pv(PV_BCTX), in1=contrib_(pc),
                                op0=ALU.mult, op1=ALU.add), inc=sV)
                        blend_v = sV.n
                        for pc in range(NPC):
                            P.dma("gpsimd",
                                  c_in[pc, :, pair * 512:(pair + 1) * 512],
                                  contrib_(pc), sGsc,
                                  waits=((sV, blend_v),
                                         (sCC, LS["cin_free_cc"])))
                        LS["contrib_dma"] = sGsc.n
            imp_v = sV.n
            for pc in range(NPC):
                P.dma("gpsimd", c_in[pc, :, 2048:2065],
                      IMPCTX[:, pc * 17:(pc + 1) * 17], sGsc,
                      waits=((sV, imp_v),))
            LS["contrib_dma"] = sGsc.n

            gop(lambda g: g.collective_compute(
                "AllGather", ALU.bypass,
                replica_groups=[[0, 1], [2, 3], [4, 5], [6, 7]],
                ins=[c_in[0:2].opt()], outs=[c_out[0].opt()]),
                waits=((sGsc, LS["contrib_dma"]),), inc=sCC)
            cgather_h1 = sCC.n
            gop(lambda g: g.collective_compute(
                "AllGather", ALU.bypass,
                replica_groups=[[0, 1], [2, 3], [4, 5], [6, 7]],
                ins=[c_in[2:4].opt()], outs=[c_out[1].opt()]), inc=sCC)
            cgather = sCC.n
            LS["cin_free_cc"] = cgather
            LS["gin_free_cc"] = cgather
            attn_T = sT.n

            for si in range(2):
                for pc in range(NPC):
                    P.dma("gpsimd", slot_(si, pc),
                          c_out[pc // 2, si, pc % 2, :, :], sGsc,
                          waits=((sCC, cgather_h1 if pc < 2 else cgather),
                                 (sT, attn_T)))
            for dc in range(NDC):
                P.dma("gpsimd", IMPRES[:, dc * 17:dc * 17 + 1],
                      xg_src(dc, 2047, 1), sGsc, waits=((sCC, cgather),),
                      slow=True)
                P.dma("gpsimd", IMPRES[:, dc * 17 + 1:dc * 17 + 17],
                      xg_src(dc, 1, 16), sGsc)
            impres_rdy = sGsc.n
            slots_rdy = sGsc.n

            for mc in range(NDC):
                for idx in range(12):
                    ci, c4 = idx // 4, idx % 4
                    stB.stage(w128_(idx),
                              wo[L][ci, c4 * 128:(c4 + 1) * 128,
                                    mc * 128:(mc + 1) * 128], 128,
                              reader_gate=((sT, LS.get("owT", attn_T)),))
                stB.flush()
                owc = sV.n
                LS["owT"] = 0
                for tt in range(NTT):
                    bank = pb[tt % 2]
                    nmm = 0
                    for ci in range(3):
                        for c4 in range(NPC):
                            wt = w128_(ci * 4 + c4)
                            if ci == 0:
                                rhs = ctxself_(c4)[:, tt * 512:(tt + 1) * 512]
                            else:
                                rhs = slot_(ci - 1, c4)[:,
                                                        tt * 512:(tt + 1) * 512]
                            nmm += 1
                            mm(bank[:, :], wt, rhs, start=(nmm == 1),
                               stop=(nmm == 12),
                               waits=((sV, owc), (sGsc, slots_rdy),
                                      (sV, LS.get(f"ores{tt % 2}", 0)))
                               if nmm == 1 else (),
                               inc=sT if nmm == 12 else None)
                    LS["owT"] = max(LS["owT"], sT.n)
                    if is0:
                        act(T1[:, :], bank[:, :], AF.Identity,
                            bias=ob_s[:, mc:mc + 1], waits=((sT, sT.n),),
                            inc=sC)
                        vop(lambda v, mc=mc, tt=tt: v.tensor_tensor(
                            out=X[mc][:, 1 + tt * 512:1 + (tt + 1) * 512],
                            in0=T1[:, :],
                            in1=X[mc][:, 1 + tt * 512:1 + (tt + 1) * 512],
                            op=ALU.add), waits=((sC, sC.n),), inc=sV)
                    else:
                        vop(lambda v, mc=mc, tt=tt, bank=bank: v.tensor_tensor(
                            out=X[mc][:, 1 + tt * 512:1 + (tt + 1) * 512],
                            in0=bank[:, :],
                            in1=X[mc][:, 1 + tt * 512:1 + (tt + 1) * 512],
                            op=ALU.add), waits=((sT, sT.n),), inc=sV)
                    LS[f"ores{tt % 2}"] = sV.n
                bank = pb[6]
                nmm = 0
                for ci in range(3):
                    for c4 in range(NPC):
                        wt = w128_(ci * 4 + c4)
                        if ci == 0:
                            rhs = IMPCTX[:, c4 * 17:(c4 + 1) * 17]
                        else:
                            rhs = slot_(ci - 1, c4)[:, 2048:2065]
                        nmm += 1
                        mm(bank[:, 0:17], wt, rhs, start=(nmm == 1),
                           stop=(nmm == 12),
                           waits=((sV, LS.get("impev", 0)),)
                           if nmm == 1 else (),
                           inc=sT if nmm == 12 else None)
                LS["owT"] = max(LS["owT"], sT.n)
                if is0:
                    act(IMPX[:, mc * 17:(mc + 1) * 17], bank[:, 0:17],
                        AF.Identity, bias=ob_s[:, mc:mc + 1],
                        waits=((sT, sT.n),), inc=sC)
                else:
                    act(IMPX[:, mc * 17:(mc + 1) * 17], bank[:, 0:17],
                        AF.Identity, waits=((sT, sT.n),), inc=sC)
                vop(lambda v, mc=mc: v.tensor_tensor(
                    out=IMPX[:, mc * 17:(mc + 1) * 17],
                    in0=IMPX[:, mc * 17:(mc + 1) * 17],
                    in1=IMPRES[:, mc * 17:(mc + 1) * 17], op=ALU.add),
                    waits=((sC, sC.n), (sGsc, impres_rdy)), inc=sV)
                LS["impev"] = sV.n
            LS["arena_md_T"] = sT.n
            LS["arena_lo_T"] = sT.n

            for mc in range(NDC):
                vop(lambda v, mc=mc: v.tensor_copy(
                    out=IMPB[:, :], in_=IMPX[:, mc * 17:(mc + 1) * 17]),
                    waits=((sT, LS.get("impstT", 0)),), inc=sV)
                vop(lambda v: v.tensor_tensor(
                    out=IMPSQ[:, :], in0=IMPB[:, :], in1=IMPB[:, :],
                    op=ALU.mult), inc=sV)
                mm(pb[4][0:1, 0:17], ones128b[:, :], IMPB[:, :],
                   start=(mc == 0), stop=(mc == NDC - 1),
                   waits=((sV, sV.n),), inc=sT)
                mm(pb[5][0:1, 0:17], ones128b[:, :], IMPSQ[:, :],
                   start=(mc == 0), stop=(mc == NDC - 1),
                   inc=sT)
                LS["impstT"] = sT.n
            ist_t = sT.n
            vop(lambda v: v.tensor_scalar(out=i1row[:, :],
                                          in0=pb[4][0:1, 0:17],
                                          scalar1=1.0 / D, scalar2=None,
                                          op0=ALU.mult),
                waits=((sT, ist_t),), inc=sV)
            vop(lambda v: v.tensor_scalar(out=i2row[:, :],
                                          in0=pb[5][0:1, 0:17],
                                          scalar1=1.0 / D, scalar2=None,
                                          op0=ALU.mult), inc=sV)
            vop(lambda v: v.tensor_tensor(out=iarow[:, :], in0=i1row[:, :],
                                          in1=i1row[:, :], op=ALU.mult),
                inc=sV)
            vop(lambda v: v.tensor_tensor(out=iarow[:, :], in0=i2row[:, :],
                                          in1=iarow[:, :], op=ALU.subtract),
                inc=sV)
            vop(lambda v: v.tensor_scalar(out=iarow[:, :], in0=iarow[:, :],
                                          scalar1=EPS, scalar2=None,
                                          op0=ALU.add), inc=sV)
            vop(lambda v: v.reciprocal(out=iarow[:, :], in_=iarow[:, :]),
                inc=sV)
            act(iarow[:, :], iarow[:, :], AF.Sqrt, waits=((sV, sV.n),),
                inc=sC)
            vop(lambda v: v.tensor_tensor(out=ibrow[:, :], in0=i1row[:, :],
                                          in1=iarow[:, :], op=ALU.mult),
                waits=((sC, sC.n),), inc=sV)
            iab_v = sV.n
            mm(pb[4][:, 0:17], ones128f[:, :], iarow[:, :], start=True,
               stop=True, waits=((sV, iab_v),), inc=sT)
            mm(pb[5][:, 0:17], ones128f[:, :], ibrow[:, :], start=True,
               stop=True, inc=sT)
            ibc_t = sT.n
            for mc in range(NDC):
                vop(lambda v, mc=mc: v.tensor_tensor(
                    out=IMPX[:, mc * 17:(mc + 1) * 17],
                    in0=IMPX[:, mc * 17:(mc + 1) * 17],
                    in1=pb[4][:, 0:17], op=ALU.mult),
                    waits=((sT, ibc_t),), inc=sV)
                vop(lambda v, mc=mc: v.tensor_tensor(
                    out=IMPX[:, mc * 17:(mc + 1) * 17],
                    in0=IMPX[:, mc * 17:(mc + 1) * 17],
                    in1=pb[5][:, 0:17], op=ALU.subtract), inc=sV)
                vop(lambda v, mc=mc: v.tensor_scalar(
                    out=X[mc][:, 0:1], in0=IMPX[:, mc * 17:mc * 17 + 1],
                    scalar1=pv(PV_G), scalar2=None, op0=ALU.mult), inc=sV)
                vop(lambda v, mc=mc: v.tensor_copy(
                    out=XIMP[:, mc * 16:(mc + 1) * 16],
                    in_=IMPX[:, mc * 17 + 1:(mc + 1) * 17]), inc=sV)

            if is0:
                emit_precast(len(PRE_TASKS))
            emit_ln(make_xb=(not is0))

            if is0:
                emit_ffn0()
            else:
                emit_ffn1()

            emit_ln(make_xb=False)
            LS["arena_hi_T"] = sT.n

        emit_layer(0)
        emit_layer(1)

        out_v = LS["lnap"]
        for dc in range(NDC):
            P.dma("gpsimd", yout[dc * 128:(dc + 1) * 128, :],
                  X[dc][:, 1:2049], sGm, waits=((sV, out_v),))
        P.wait("gpsimd", (sGm, sGm.n), (sGx, sGx.n), (sGsc, sGsc.n),
               (sGg, sGg.n))
        P.wait("sync", (sW, sW.n), (sWa, sWa.n), (sWb, sWb.n),
               (sW1[0], sW1[0].n), (sW1[1], sW1[1].n),
               (sW2[0], sW2[0].n), (sW2[1], sW2[1].n))

        with nc.Block() as block:
            @block.sync
            def _(sync):
                for f in P.q["sync"]:
                    f(sync)

            @block.tensor
            def _(tensor):
                for f in P.q["tensor"]:
                    f(tensor)

            @block.vector
            def _(vector):
                for f in P.q["vector"]:
                    f(vector)

            @block.scalar
            def _(scalar):
                for f in P.q["scalar"]:
                    f(scalar)

            @block.gpsimd
            def _(gpsimd):
                for f in P.q["gpsimd"]:
                    f(gpsimd)

    return nc


def kernel(**inp):
    f32 = np.float32

    def TT(a):
        return np.ascontiguousarray(np.asarray(a, dtype=f32).T)

    nc = build_nc()
    in_maps = []
    for c in range(8):
        b, g = c // 2, c % 2
        m = {}
        x = np.asarray(inp["L0_attn_x"], f32)
        m["x0"] = np.ascontiguousarray(x[b, g * T:(g + 1) * T, :].T)

        hsl = slice(g * 8, g * 8 + 8)
        psl = slice((1 - g) * 8, (1 - g) * 8 + 8)
        for nm, w in (("wq", "L0_attn_Q_w"), ("wk", "L0_attn_K_w"),
                      ("wv", "L0_attn_V_w")):
            W = np.asarray(inp[w], f32)[hsl]
            m[f"L0_{nm}"] = np.ascontiguousarray(
                W.transpose(1, 0, 2).reshape(D, HD))
        OW = np.asarray(inp["L0_attn_O_w"], f32)
        zz = np.zeros((HD, D), f32)
        part = np.ascontiguousarray(OW[psl].reshape(HD, D))
        m["L0_wo"] = np.stack([
            np.ascontiguousarray(OW[hsl].reshape(HD, D)),
            part if g == 1 else zz,
            part if g == 0 else zz])
        m["qbias"] = np.ascontiguousarray(
            np.asarray(inp["L0_attn_Q_b"], f32)[hsl].reshape(4, 128).T)
        m["kbias"] = np.ascontiguousarray(
            np.asarray(inp["L0_attn_K_b"], f32)[hsl].reshape(4, 128).T)
        m["vbias"] = np.asarray(
            inp["L0_attn_V_b"], f32)[hsl].reshape(1, HD).copy()
        m["obias"] = np.ascontiguousarray(
            np.asarray(inp["L0_attn_O_b"], f32).reshape(8, 128).T)

        for nm, w in (("wq", "L1_attn_w_q"), ("wk", "L1_attn_w_k"),
                      ("wv", "L1_attn_w_v")):
            W = np.asarray(inp[w], f32)[hsl]
            m[f"L1_{nm}"] = np.ascontiguousarray(W.reshape(HD, D).T)
        WO = np.asarray(inp["L1_attn_w_o"], f32)
        part = np.ascontiguousarray(WO[(1 - g) * HD:(2 - g) * HD])
        m["L1_wo"] = np.stack([
            np.ascontiguousarray(WO[g * HD:(g + 1) * HD]),
            part if g == 1 else zz,
            part if g == 0 else zz])

        m["wr0"] = TT(inp["L0_ffn_W_r"])
        m["wk0"] = TT(inp["L0_ffn_W_k"])
        m["wv0"] = TT(inp["L0_ffn_W_v"])
        m["mur"] = np.ascontiguousarray(
            np.asarray(inp["L0_ffn_mu_r"], f32).reshape(8, 128).T)
        m["muk"] = np.ascontiguousarray(
            np.asarray(inp["L0_ffn_mu_k"], f32).reshape(8, 128).T)

        m["w1t"] = TT(inp["L1_ffn_W1"])
        m["b1c"] = np.ascontiguousarray(
            np.asarray(inp["L1_ffn_b1"], f32).reshape(32, 128).T)
        m["w2t"] = TT(inp["L1_ffn_W2"])
        m["b2c"] = np.ascontiguousarray(
            np.asarray(inp["L1_ffn_b2"], f32).reshape(8, 128).T)
        m["scal"] = np.asarray(inp["L1_ffn_scale"], f32).reshape(1, 1)

        if g == 0:
            pvv = [0.0, 1.0, 1.0, 0.0, 0.0, 1.0, 0.0, 0.0]
        else:
            pvv = [1.0, 0.0, 0.0, 1.0, 1.0, 0.0, 1.0, 0.0]
        m["pvec"] = np.asarray(pvv, f32).reshape(1, 8)

        m["tric"] = np.ascontiguousarray(np.triu(np.ones((128, 128), f32)))
        m["idmc"] = np.eye(128, dtype=f32)
        e2 = np.zeros((128, 2), f32)
        e2[0:64, 0] = 1.0
        e2[64:128, 1] = 1.0
        m["e2cc"] = e2
        m["emc"] = np.eye(4, dtype=f32)
        in_maps.append(m)

    res = run_bass_kernel_spmd(nc, in_maps, core_ids=list(range(8)))
    out = np.empty((B, S, D), np.float32)
    for c in range(8):
        b, g = c // 2, c % 2
        out[b, g * T:(g + 1) * T, :] = res.results[c]["y"].T
    return out


if __name__ == "__main__":
    import reference as R
    inp = R.setup_inputs()
    got = kernel(**{k: np.asarray(v) for k, v in inp.items()})
    want = np.asarray(R.reference(**inp))
    num = float(((got - want) ** 2).mean())
    den = float((want ** 2).mean())
    print("resvar:", num / den)


# version 07
# speedup vs baseline: 1.3704x; 1.1146x over previous
import contextlib
import numpy as np

import concourse.bass as bass
import concourse.mybir as mybir
from concourse.bass_utils import run_bass_kernel_spmd

F32 = mybir.dt.float32
BF16 = mybir.dt.bfloat16
AF = mybir.ActivationFunctionType
ALU = mybir.AluOpType
AX = mybir.AxisListType

B, S, D = 4, 4096, 1024
T = 2048
HD = 512
NPC = 4
NDC = 8
NTT = 4
FH = 4096
EPS = 1e-5

ENGS = ("sync", "tensor", "vector", "scalar", "gpsimd")


class Sem:
    def __init__(self, h):
        self.h = h
        self.n = 0


class Prog:
    def __init__(self):
        self.q = {e: [] for e in ENGS}
        self.hw = {}

    def emit(self, eng, fn, waits=(), inc=None, inc_by=1):
        ss = getattr(self, "selfsem", {}).get(eng)
        if ss is not None:
            waits = tuple(waits) + ((ss, ss.n),)
            if inc is None:
                inc, inc_by = ss, 1
        xs = getattr(self, "crosssem", {}).get(eng)
        if xs is not None:
            waits = tuple(waits) + ((xs, xs.n),)
        eff = []
        for s, v in waits:
            k = (eng, id(s))
            if v > 0 and v > self.hw.get(k, 0):
                self.hw[k] = v
                eff.append((s, v))
        eff = tuple(eff)

        def g(obj, fn=fn, waits=eff, inc=inc, inc_by=inc_by):
            for s, v in waits:
                obj.wait_ge(s.h, v)
            r = fn(obj)
            if inc is not None:
                r.then_inc(inc.h, inc_by)

        self.q[eng].append(g)
        if inc is not None:
            inc.n += inc_by

    def dma(self, eng, out, in_, sem, waits=(), slow=False):
        eff = tuple((s, v) for (s, v) in waits if v > 0)
        for s, v in eff:
            k = (eng, id(s))
            self.hw[k] = max(self.hw.get(k, 0), v)

        def g(obj, out=out, in_=in_, slow=slow, eff=eff, sem=sem):
            for s, v in eff:
                obj.wait_ge(s.h, v)
            obj.dma_start(out=out, in_=in_,
                          allow_slow_non_contiguous=slow).then_inc(sem.h, 16)

        self.q[eng].append(g)
        sem.n += 16

    def wait(self, eng, *pairs):
        eff = []
        for s, v in pairs:
            k = (eng, id(s))
            if v > 0 and v > self.hw.get(k, 0):
                self.hw[k] = v
                eff.append((s, v))
        eff = tuple(eff)

        def g(obj):
            for s, v in eff:
                obj.wait_ge(s.h, v)

        self.q[eng].append(g)


def build_nc():
    nc = bass.Bass("TRN2", target_bir_lowering=False)

    def din(name, shape):
        return nc.dram_tensor(name, shape, F32, kind="ExternalInput").ap()

    x0 = din("x0", [D, T])
    wq, wk, wv, wo = {}, {}, {}, {}
    for L in (0, 1):
        wq[L] = din(f"L{L}_wq", [D, HD])
        wk[L] = din(f"L{L}_wk", [D, HD])
        wv[L] = din(f"L{L}_wv", [D, HD])
        wo[L] = din(f"L{L}_wo", [NDC, 128, 12, 128])
    qbias = din("qbias", [128, NPC])
    kbias = din("kbias", [128, NPC])
    vbias = din("vbias", [1, HD])
    obias = din("obias", [128, NDC])
    wr0 = din("wr0", [8, 128, 8, 128])
    wk0 = din("wk0", [8, 128, 8, 128])
    wv0 = din("wv0", [8, 128, 8, 128])
    mur = din("mur", [128, NDC])
    muk = din("muk", [128, NDC])
    w1t = din("w1t", [D, FH])
    b1c = din("b1c", [128, 32])
    w2t = din("w2t", [FH, D])
    b2c = din("b2c", [128, NDC])
    scal = din("scal", [1, 1])
    pvec = din("pvec", [1, 8])
    tric = din("tric", [128, 128])
    idmc = din("idmc", [128, 128])
    e2cc = din("e2cc", [128, 2])
    emc = din("emc", [4, 4])

    yout = nc.dram_tensor("y", [D, T], F32, kind="ExternalOutput").ap()

    g_in = nc.dram_tensor("g_in", [D, T], BF16).ap()
    g_out = nc.dram_tensor("g_out", [2, 2, 512, T], BF16).ap()
    c_in = nc.dram_tensor("c_in", [NPC, 128, 2065], BF16).ap()
    c_out = nc.dram_tensor("c_out", [2, 2, 2, 128, 2065], BF16).ap()
    w1bf = nc.dram_tensor("w1bf", [32, 128, 8, 128], BF16).ap()
    w2bf = nc.dram_tensor("w2bf", [8, 128, 32, 128], BF16).ap()
    scrb = nc.dram_tensor("scrb", [4, 1024], BF16).ap()
    scrf = nc.dram_tensor("scrf", [1, 16], F32).ap()

    P = Prog()
    LS = {}

    with contextlib.ExitStack() as es:
        A = lambda *a: es.enter_context(nc.sbuf_tensor(*a))
        PSA = lambda *a: es.enter_context(nc.psum_tensor(*a))

        X = [A(f"X{i}", [128, 2049], F32) for i in range(NDC)]
        XIMP = A("XIMP", [128, NDC * 16], F32)
        IMPX = A("IMPX", [128, NDC * 17], F32)
        AR = A("AR", [128, 58112], BF16)

        def ar(off, n):
            return AR[:, off:off + n]

        K_ = lambda pc: ar(pc * 4096, 4096)
        V_ = lambda tc: ar(16384 + tc * 520, 520)
        slot_ = lambda s, pc: ar((s * NPC + pc) * 2065, 2065)
        wqb_ = lambda dc: ar(33024 + dc * 512, 512)
        wkb_ = lambda dc: ar(37120 + dc * 512, 512)
        wvb_ = lambda dc: ar(41216 + dc * 512, 512)
        ctxself_ = lambda pc: ar(37120 + pc * 2048, 2048)
        xg_ = lambda buf, dc: ar(45312 + dc * 512, 512)
        Q_ = lambda pc: ar(49408 + pc * 512, 512)
        expS_ = lambda i: ar(51456 + i * 512, 512)
        ctxqb_ = lambda pc: ar(53504 + pc * 512, 512)
        contrib_ = lambda pc: ar(55552 + pc * 512, 512)
        qsqv = ar(57600, 512)
        xb_ = lambda dc, off=0: ar(dc * 2048 + off, 512)
        kact_ = lambda hc, off=0: ar(16384 + hc * 2048 + off, 512)
        h1_ = lambda hc: ar(16384 + hc * 512, 512)
        w1buf_ = lambda buf, dc: ar(52480 + buf * 1024 + dc * 128, 128)
        w2buf_ = lambda buf, hc: ar(33024 + buf * 4096 + hc * 128, 128)
        w128_ = lambda i: ar(51456 + i * 128, 128)

        LTB = [A(f"LTB{dc}", [128, 16], BF16) for dc in range(NDC)]
        LTT = AR[0:16, 41216:42240]
        LTEM = AR[0:4, 42240:46336]
        LTJM = AR[0:4, 46336:50432]
        A4 = AR[0:4, 50432:52480]
        B4 = [AR[0:4, 55552 + j * 512: 56064 + j * 512] for j in range(4)]
        IMPCTX = A("IMPCTX", [128, NPC * 17], BF16)
        IMPRES = A("IMPRES", [128, NDC * 17], BF16)
        IMPB = A("IMPB", [128, 17], BF16)
        IMPSQ = A("IMPSQ", [128, 17], BF16)

        WST = A("WST", [128, 1024], F32)
        PCST = A("PCST", [128, 1024], BF16)
        WST2 = A("WST2", [128, 1664], F32)
        T1 = A("T1", [128, 512], F32)
        RT = A("RT", [128, 512], F32)
        XBT = PCST[:, 0:512]
        XSQT = PCST[:, 512:1024]

        tri_b = A("tri_b", [128, 128], BF16)
        idm_b = A("idm_b", [128, 128], BF16)
        idmf = A("idmf", [128, 128], F32)
        nb4 = A("nb4", [4, 1], F32)
        e2c_b = A("e2c_b", [128, 2], BF16)
        onesrow_b = A("onesrow_b", [1, 64], BF16)
        rrow_b = A("rrow_b", [1, 512], BF16)
        emask_s = A("emask_s", [4, 4], F32)
        nb8 = A("nb8", [1, 8], F32)
        ones128f = A("ones128f", [1, 128], F32)
        ones128b = A("ones128b", [128, 1], BF16)
        pvec_s = A("pvec_s", [1, 8], F32)
        pbc = A("pbc", [128, 8], F32)
        scal_s = A("scal_s", [1, 1], F32)
        scal_bc = A("scal_bc", [128, 1], F32)
        qb_s = A("qb_s", [128, NPC], F32)
        kb_s = A("kb_s", [128, NPC], F32)
        ob_s = A("ob_s", [128, NDC], F32)
        vb_s = A("vb_s", [1, HD], F32)
        mur_s = A("mur_s", [128, NDC], F32)
        muk_s = A("muk_s", [128, NDC], F32)
        b1_s = A("b1_s", [128, 32], F32)
        b2_s = A("b2_s", [128, NDC], F32)
        bkm = A("bkm", [2, NPC], F32)
        bqm = A("bqm", [2, NPC], F32)
        nbound = A("nbound", [2, NPC], F32)
        bB = A("bB", [128, 8], F32)
        lrow = A("lrow", [1, 512], F32)
        rrow = A("rrow", [1, 512], F32)
        m1row = A("m1row", [1, 512], F32)
        m2row = lrow
        arow = rrow
        brow = lrow
        i1row = A("i1row", [1, 17], F32)
        i2row = A("i2row", [1, 17], F32)
        iarow = A("iarow", [1, 17], F32)
        ibrow = A("ibrow", [1, 17], F32)

        pbS = [PSA("pbS0", [128, 1024], F32), PSA("pbS1", [128, 1024], F32)]
        pb = [pbS[0][:, 0:512], pbS[0][:, 512:1024],
              pbS[1][:, 0:512], pbS[1][:, 512:1024]] + \
             [PSA(f"pb{i}", [128, 512], F32) for i in range(4, 7)]
        pbt = PSA("pbt", [128, 1024], BF16)

        sW = Sem(es.enter_context(nc.semaphore("sW")))
        sWa = Sem(es.enter_context(nc.semaphore("sWa")))
        sWb = Sem(es.enter_context(nc.semaphore("sWb")))
        sW1 = [Sem(es.enter_context(nc.semaphore(f"sW1{i}"))) for i in range(2)]
        sW2 = [Sem(es.enter_context(nc.semaphore(f"sW2{i}"))) for i in range(2)]
        sGx = Sem(es.enter_context(nc.semaphore("sGx")))
        sGsc = Sem(es.enter_context(nc.semaphore("sGsc")))
        sGg = Sem(es.enter_context(nc.semaphore("sGg")))
        sGm = Sem(es.enter_context(nc.semaphore("sGm")))
        sT = Sem(es.enter_context(nc.semaphore("sT")))
        sV = Sem(es.enter_context(nc.semaphore("sV")))
        sC = Sem(es.enter_context(nc.semaphore("sC")))
        sP = Sem(es.enter_context(nc.semaphore("sP")))
        sCC = Sem(es.enter_context(nc.semaphore("sCC")))
        P.selfsem = {"vector": sV, "scalar": sC}
        P.crosssem = {"vector": sC, "scalar": sV}

        def mm(out, lhsT, rhs, start, stop, waits=(), inc=None, tp=None):
            P.emit("tensor",
                   lambda t, out=out, lhsT=lhsT, rhs=rhs, s0=start, s1=stop, tp=tp:
                   t.matmul(out, lhsT, rhs, start=s0, stop=s1, tile_position=tp),
                   waits=waits, inc=inc)

        def act(out, in_, func, bias=None, scale=1.0, waits=(), inc=None):
            kw = {}
            if bias is not None:
                kw["bias"] = bias
            P.emit("scalar",
                   lambda sc, out=out, in_=in_, f=func, scale=scale, kw=kw:
                   sc.activation(out=out, in_=in_, func=f, scale=scale, **kw),
                   waits=waits, inc=inc)

        def vop(fn, waits=(), inc=None):
            P.emit("vector", fn, waits=waits, inc=inc)

        def gop(fn, waits=(), inc=None):
            P.emit("gpsimd", fn, waits=waits, inc=inc)

        class Stager:

            def __init__(self, buf, slot_elems, nslots, sem):
                self.buf, self.se, self.ns = buf, slot_elems, nslots
                self.sem = sem
                self.gates = [0] * nslots
                self.i = 0
                self.pend = []
                self.last = 0

            def stage(self, dst_bf, src, n, reader_gate=()):
                s = self.i % self.ns
                self.i += 1
                stg = self.buf[:, s * self.se: s * self.se + n]
                P.dma("sync", stg, src, self.sem,
                      waits=((sV, max(self.gates[s],
                                      self.last if not self.pend else 0)),))
                self.pend.append((s, stg, dst_bf, tuple(reader_gate)))
                if len(self.pend) == self.ns:
                    self.flush()

            def flush(self):
                if not self.pend:
                    return
                tot = self.sem.n
                for s, stg, dst, rg in self.pend:
                    vop(lambda v, stg=stg, dst=dst: v.tensor_copy(
                        out=dst, in_=stg), waits=((self.sem, tot),) + rg,
                        inc=sV)
                    self.gates[s] = sV.n
                self.pend = []
                self.last = sV.n

        stA = Stager(WST, 512, 2, sWa)
        stB = Stager(WST2, 128, 13, sWb)

        PRE_TASKS = []
        for cc in range(NDC):
            for j in range(8):
                PRE_TASKS.append((
                    w1bf[j * 4:(j + 1) * 4, :, cc, :].rearrange(
                        "h p w -> p h w"),
                    w1t[cc * 128:(cc + 1) * 128, j * 512:(j + 1) * 512]))
        for cc in range(32):
            for j in range(2):
                PRE_TASKS.append((
                    w2bf[j * 4:(j + 1) * 4, :, cc, :].rearrange(
                        "m p w -> p m w"),
                    w2t[cc * 128:(cc + 1) * 128, j * 512:(j + 1) * 512]))
        PRE_STATE = {"i": 0, "gates": [0, 0]}

        def emit_precast(n):
            pout = []
            while n > 0 and PRE_STATE["i"] < len(PRE_TASKS):
                dst, srcw = PRE_TASKS[PRE_STATE["i"]]
                pslot = PRE_STATE["i"] % 2
                PRE_STATE["i"] += 1
                n -= 1
                pcb = PCST[:, pslot * 512:(pslot + 1) * 512]
                stA.stage(pcb, srcw, 512,
                          reader_gate=((sGg, PRE_STATE["gates"][pslot]),))
                pout.append((pslot, dst, pcb))
                if len(pout) == 2:
                    stA.flush()
                    cv = sV.n
                    for ps, d, pc_ in pout:
                        P.dma("gpsimd", d,
                              pc_.rearrange("p (a w) -> p a w", w=128),
                              sGg, waits=((sV, cv),))
                    PRE_STATE["gates"][0] = PRE_STATE["gates"][1] = sGg.n
                    pout.clear()
                    LS["pcst_dma"] = sGg.n
            assert not pout

        def xg_src(dc, col0, ncol):
            s = col0 // T
            assert (col0 + ncol - 1) // T == s
            o = col0 - s * T
            h = dc // 4
            return g_out[h, s, (dc % 4) * 128:(dc % 4 + 1) * 128, o:o + ncol]

        for dc in range(NDC):
            P.dma("sync", X[dc][:, 1:2049], x0[dc * 128:(dc + 1) * 128, :], sW)
        for (dst, src) in ((qb_s, qbias), (kb_s, kbias), (ob_s, obias),
                           (vb_s, vbias), (mur_s, mur), (muk_s, muk),
                           (b1_s, b1c), (b2_s, b2c), (pvec_s, pvec),
                           (scal_s, scal)):
            P.dma("sync", dst[:, :], src[:, :], sW)
        P.dma("sync", WST2[:, 0:128], tric[:, :], sW)
        P.dma("sync", WST2[:, 128:256], idmc[:, :], sW)
        P.dma("sync", WST[:, 0:2], e2cc[:, :], sW)
        P.dma("sync", emask_s[:, :], emc[:, :], sW)
        consts_l = sW.n
        x_loaded = sW.n
        vop(lambda v: v.tensor_copy(out=tri_b[:, :], in_=WST2[:, 0:128]),
            waits=((sW, consts_l),), inc=sV)
        vop(lambda v: v.tensor_copy(out=idm_b[:, :], in_=WST2[:, 128:256]),
            inc=sV)
        vop(lambda v: v.tensor_copy(out=idmf[:, :], in_=WST2[:, 128:256]),
            inc=sV)
        vop(lambda v: v.tensor_copy(out=e2c_b[:, :], in_=WST[:, 0:2]), inc=sV)
        vop(lambda v: v.memset(ones128f[:, :], 1.0), inc=sV)
        vop(lambda v: v.memset(ones128b[:, :], 1.0), inc=sV)
        vop(lambda v: v.memset(onesrow_b[:, :], 1.0), inc=sV)
        stA.gates[0] = sV.n
        stB.gates[0] = stB.gates[1] = sV.n
        consts_v = sV.n
        mm(pb[0][:, 0:8], ones128f[:, :], pvec_s[:, :], start=True, stop=True,
           waits=((sW, consts_l), (sV, consts_v)), inc=sT)
        mm(pb[0][:, 8:9], ones128f[:, :], scal_s[:, :], start=True, stop=True,
           inc=sT)
        vop(lambda v: v.tensor_copy(out=pbc[:, :], in_=pb[0][:, 0:8]),
            waits=((sT, sT.n),), inc=sV)
        vop(lambda v: v.tensor_copy(out=scal_bc[:, :], in_=pb[0][:, 8:9]),
            inc=sV)
        LS["pbc_v"] = sV.n

        PV_ACTX, PV_BCTX, PV_ASELF, PV_BSELF, PV_G, PV_AL, PV_BL = range(7)

        def pv(i):
            return pbc[:, i:i + 1]

        LS["xg_gate"] = 0
        LS["gin_free_cc"] = 0
        LS["cin_free_cc"] = 0
        LS["arena_lo_T"] = 0
        LS["arena_md_T"] = 0
        LS["arena_hi_T"] = 0

        def emit_ln(make_xb):
            for tt in range(NTT):
                for dc in range(NDC):
                    vop(lambda v, dc=dc, tt=tt: v.tensor_copy(
                        out=XBT[:, :],
                        in_=X[dc][:, 1 + tt * 512:1 + (tt + 1) * 512]),
                        waits=((sT, LS.get("lnst", 0)),
                               (sGg, LS.get("pcst_dma", 0))), inc=sV)
                    vop(lambda v: v.tensor_tensor(
                        out=XSQT[:, :], in0=XBT[:, :], in1=XBT[:, :],
                        op=ALU.mult), inc=sV)
                    mm(pb[2][0:1, :], ones128b[:, :], XBT[:, :],
                       start=(dc == 0), stop=(dc == NDC - 1),
                       waits=((sV, sV.n), (sV, LS.get("lnev", 0))),
                       inc=sT)
                    mm(pb[3][0:1, :], ones128b[:, :], XSQT[:, :],
                       start=(dc == 0), stop=(dc == NDC - 1),
                       inc=sT)
                    LS["lnst"] = sT.n
                st_t = sT.n
                vop(lambda v: v.tensor_scalar(
                    out=m1row[:, :], in0=pb[2][0:1, :], scalar1=1.0 / D,
                    scalar2=None, op0=ALU.mult), waits=((sT, st_t),), inc=sV)
                vop(lambda v: v.tensor_scalar(
                    out=m2row[:, :], in0=pb[3][0:1, :], scalar1=1.0 / D,
                    scalar2=None, op0=ALU.mult), inc=sV)
                LS["lnev"] = sV.n
                vop(lambda v: v.tensor_tensor(out=arow[:, :], in0=m1row[:, :],
                                              in1=m1row[:, :], op=ALU.mult),
                    inc=sV)
                vop(lambda v: v.tensor_tensor(out=arow[:, :], in0=m2row[:, :],
                                              in1=arow[:, :],
                                              op=ALU.subtract), inc=sV)
                vop(lambda v: v.tensor_scalar(out=arow[:, :], in0=arow[:, :],
                                              scalar1=EPS, scalar2=None,
                                              op0=ALU.add), inc=sV)
                vop(lambda v: v.reciprocal(out=arow[:, :], in_=arow[:, :]),
                    inc=sV)
                act(arow[:, :], arow[:, :], AF.Sqrt, waits=((sV, sV.n),),
                    inc=sC)
                vop(lambda v: v.tensor_tensor(out=brow[:, :], in0=m1row[:, :],
                                              in1=arow[:, :], op=ALU.mult),
                    waits=((sC, sC.n),), inc=sV)
                ab_v = sV.n
                mm(pb[2][:, :], ones128f[:, :], arow[:, :], start=True,
                   stop=True, waits=((sV, ab_v), (sV, LS.get("lnap", 0))),
                   inc=sT)
                mm(pb[3][:, :], ones128f[:, :], brow[:, :], start=True,
                   stop=True, inc=sT)
                bc_t = sT.n
                for dc in range(NDC):
                    vop(lambda v, dc=dc, tt=tt: v.tensor_tensor(
                        out=T1[:, :],
                        in0=X[dc][:, 1 + tt * 512:1 + (tt + 1) * 512],
                        in1=pb[2][:, :], op=ALU.mult),
                        waits=((sT, bc_t),), inc=sV)
                    vop(lambda v, dc=dc, tt=tt: v.tensor_tensor(
                        out=X[dc][:, 1 + tt * 512:1 + (tt + 1) * 512],
                        in0=T1[:, :], in1=pb[3][:, :], op=ALU.subtract),
                        inc=sV)
                    if make_xb:
                        vop(lambda v, dc=dc, tt=tt: v.tensor_copy(
                            out=xb_(dc, tt * 512),
                            in_=X[dc][:, 1 + tt * 512:1 + (tt + 1) * 512]),
                            waits=((sT, LS["arena_lo_T"]),), inc=sV)
                LS["lnap"] = sV.n

        def emit_ffn0():
            def blends(mu_s):
                for dc in range(NDC):
                    for tt in range(NTT):
                        vop(lambda v, dc=dc, tt=tt, mu_s=mu_s: v.tensor_scalar(
                            out=T1[:, :],
                            in0=X[dc][:, tt * 512:tt * 512 + 512],
                            scalar1=mu_s[:, dc:dc + 1], scalar2=-1.0,
                            op0=ALU.mult, op1=ALU.mult),
                            waits=((sT, LS["arena_lo_T"]),), inc=sV)
                        vop(lambda v, dc=dc, tt=tt: v.tensor_tensor(
                            out=T1[:, :],
                            in0=X[dc][:, tt * 512:tt * 512 + 512],
                            in1=T1[:, :], op=ALU.add), inc=sV)
                        vop(lambda v, dc=dc, tt=tt, mu_s=mu_s:
                            v.scalar_tensor_tensor(
                                out=xb_(dc, tt * 512),
                                in0=X[dc][:, 1 + tt * 512:1 + tt * 512 + 512],
                                scalar=mu_s[:, dc:dc + 1], in1=T1[:, :],
                                op0=ALU.mult, op1=ALU.add), inc=sV)

            blends(muk_s)
            xkb_v = sV.n

            for hc in range(NDC):
                for half in range(2):
                    stA.stage(ar(51456 + half * 512, 512),
                              wk0[hc, :, half * 4:(half + 1) * 4, :].rearrange(
                                  "p d w -> p (d w)"), 512,
                              reader_gate=((sT, LS["arena_md_T"]),))
                stA.flush()
                wc_v = sV.n
                for tt in range(NTT):
                    bk = (hc % 2) * 2 + tt % 2
                    bank = pb[bk]
                    for dc in range(NDC):
                        mm(bank[:, :], ar(51456 + dc * 128, 128),
                           xb_(dc, tt * 512),
                           start=(dc == 0), stop=(dc == NDC - 1),
                           waits=((sV, wc_v), (sV, xkb_v),
                                  (sC, LS.get(f"f0e{bk}", 0)))
                           if dc == 0 else (),
                           inc=sT if dc == NDC - 1 else None)
                    LS["arena_md_T"] = sT.n
                    LS["arena_lo_T"] = sT.n
                    act(T1[:, :], bank[:, :], AF.Relu, waits=((sT, sT.n),),
                        inc=sC)
                    LS[f"f0e{bk}"] = sC.n
                    vop(lambda v, hc=hc, tt=tt: v.tensor_tensor(
                        out=kact_(hc, tt * 512), in0=T1[:, :], in1=T1[:, :],
                        op=ALU.mult),
                        waits=((sC, sC.n), (sT, LS["arena_lo_T"])),
                        inc=sV)
            kact_v = sV.n

            blends(mur_s)
            xrb_v = sV.n

            for mc in range(NDC):
                for half in range(2):
                    stA.stage(ar(51456 + half * 512, 512),
                              wr0[mc, :, half * 4:(half + 1) * 4, :].rearrange(
                                  "p d w -> p (d w)"), 512,
                              reader_gate=((sT, LS["arena_md_T"]),))
                stA.flush()
                wr_v = sV.n
                for half in range(2):
                    stA.stage(ar(51456 + 1024 + half * 512, 512),
                              wv0[mc, :, half * 4:(half + 1) * 4, :].rearrange(
                                  "p d w -> p (d w)"), 512,
                              reader_gate=((sT, LS["arena_md_T"]),))
                stA.flush()
                wv_v = sV.n
                for tt in range(NTT):
                    bank = pb[tt % 2]
                    for dc in range(NDC):
                        mm(bank[:, :], ar(51456 + dc * 128, 128),
                           xb_(dc, tt * 512),
                           start=(dc == 0), stop=(dc == NDC - 1),
                           waits=((sV, wr_v), (sV, xrb_v),
                                  (sC, LS.get(f"f0r{tt % 2}", 0)))
                           if dc == 0 else (),
                           inc=sT if dc == NDC - 1 else None)
                    LS["arena_md_T"] = sT.n
                    LS["arena_lo_T"] = sT.n
                    act(RT[:, :], bank[:, :], AF.Sigmoid,
                        waits=((sT, sT.n), (sV, LS.get(f"f0v{tt % 2}", 0))),
                        inc=sC)
                    LS[f"f0r{tt % 2}"] = sC.n
                    rsig_c = sC.n
                    bank2 = pb[2 + tt % 2]
                    for kc in range(NDC):
                        mm(bank2[:, :], ar(51456 + 1024 + kc * 128, 128),
                           kact_(kc, tt * 512),
                           start=(kc == 0), stop=(kc == NDC - 1),
                           waits=((sV, wv_v), (sV, kact_v),
                                  (sV, LS.get(f"f0o{tt % 2}", 0)))
                           if kc == 0 else (),
                           inc=sT if kc == NDC - 1 else None)
                    LS["arena_md_T"] = sT.n
                    vop(lambda v, bank2=bank2: v.tensor_tensor(
                        out=T1[:, :], in0=bank2[:, :],
                        in1=RT[:, :], op=ALU.mult),
                        waits=((sT, sT.n), (sC, rsig_c)), inc=sV)
                    LS[f"f0v{tt % 2}"] = sV.n
                    vop(lambda v, mc=mc, tt=tt: v.tensor_tensor(
                        out=X[mc][:, 1 + tt * 512:1 + (tt + 1) * 512],
                        in0=T1[:, :],
                        in1=X[mc][:, 1 + tt * 512:1 + (tt + 1) * 512],
                        op=ALU.add), inc=sV)
                    LS[f"f0o{tt % 2}"] = sV.n
            LS["arena_lo_T"] = sT.n
            LS["arena_md_T"] = sT.n

        def emit_ffn1():
            xb_v = LS["lnap"]

            for dc in range(NDC):
                vop(lambda v, dc=dc: v.tensor_scalar(
                    out=XBT[:, 0:16], in0=XIMP[:, dc * 16:(dc + 1) * 16],
                    scalar1=pv(PV_BL), scalar2=None, op0=ALU.mult),
                    waits=((sV, xb_v),), inc=sV)
                vop(lambda v, dc=dc: v.scalar_tensor_tensor(
                    out=LTB[dc][:, :], in0=X[dc][:, 2:18],
                    scalar=pv(PV_AL), in1=XBT[:, 0:16],
                    op0=ALU.mult, op1=ALU.add), inc=sV)
            ltb_v = sV.n
            vop(lambda v: v.memset(LTEM[:, :], 0.0),
                waits=((sT, LS["arena_md_T"]),), inc=sV)
            vop(lambda v: v.memset(LTJM[:, :], 0.0), inc=sV)
            for dc in range(NDC):
                P.emit("tensor",
                       lambda t, dc=dc: t.transpose(
                           pbt[0:16, 0:128], LTB[dc][:, :], idm_b[:, :]),
                       waits=((sV, ltb_v), (sV, LS.get("ltt_ev", 0))), inc=sT)
                vop(lambda v, dc=dc: v.tensor_copy(
                    out=LTT[:, dc * 128:(dc + 1) * 128], in_=pbt[0:16, 0:128]),
                    waits=((sT, sT.n),), inc=sV)
                LS["ltt_ev"] = sV.n
                P.emit("tensor",
                       lambda t, dc=dc: t.transpose(
                           pbt[0:4, 0:128],
                           LTB[dc].rearrange("p (a b) -> p a b", b=2)[:, 0:4, 0],
                           idm_b[:, :]),
                       waits=((sV, LS["ltt_ev"]),), inc=sT)
                for i in range(4):
                    vop(lambda v, dc=dc, i=i: v.tensor_scalar(
                        out=LTEM[0:4, i * 1024 + dc * 128:
                                 i * 1024 + (dc + 1) * 128],
                        in0=pbt[0:4, 0:128], scalar1=emask_s[:, i:i + 1],
                        scalar2=None, op0=ALU.mult),
                        waits=((sT, sT.n),), inc=sV)
                LS["ltt_ev"] = sV.n
            ltt_v = sV.n
            for i in range(4):
                P.dma("gpsimd", scrb[i, :], LTT[8 + 2 * i:9 + 2 * i, :], sGm,
                      waits=((sV, ltt_v),))
            sc_done = sGm.n
            for i in range(4):
                for j in range(4):
                    for half in range(2):
                        P.dma("gpsimd",
                              LTJM[i:i + 1, i * 1024 + j * 256 + half * 128:
                                   i * 1024 + j * 256 + half * 128 + 128],
                              scrb[i:i + 1,
                                   512 * half + j: 512 * half + j + 509:4],
                              sGm, waits=((sGm, sc_done), (sV, ltt_v)),
                              slow=True)
            ltj_done = sGm.n

            for tt in range(NTT):
                for dc in range(NDC):
                    mm(pb[4][0:4, :],
                       LTB[dc].rearrange("p (a b) -> p a b", b=2)[:, 0:4, 1],
                       xb_(dc, tt * 512),
                       start=(dc == 0), stop=(dc == NDC - 1),
                       waits=((sV, ltb_v), (sV, xb_v),
                              (sC, LS.get("a4ev", 0)))
                       if dc == 0 else (),
                       inc=sT if dc == NDC - 1 else None)
                act(A4[0:4, tt * 512:(tt + 1) * 512], pb[4][0:4, :],
                    AF.Identity, scale=scal_bc[0:4, 0:1],
                    waits=((sT, sT.n),), inc=sC)
                LS["a4ev"] = sC.n
            a4_c = sC.n
            LS["arena_lo_T"] = sT.n

            wbuf_gate = [LS["arena_md_T"], LS["arena_md_T"]]
            w2buf_gate = [LS["arena_md_T"], LS["arena_md_T"]]
            for tt in range(NTT):
                for hc in range(32):
                    bufw = hc % 2
                    P.dma("sync",
                          ar(52480 + bufw * 1024, 1024).rearrange(
                              "p (d w) -> p d w", w=128),
                          w1bf[hc], sW1[bufw],
                          waits=((sT, wbuf_gate[bufw]),
                                 (sGg, LS.get("pcst_dma", 0))))
                    w_rdy = sW1[bufw].n
                    bank = pb[hc % 2]
                    i4 = hc // 8
                    for dc in range(NDC):
                        mm(bank[:, :], w1buf_(bufw, dc), xb_(dc, tt * 512),
                           start=(dc == 0), stop=False,
                           waits=((sW1[bufw], w_rdy), (sV, xb_v),
                                  (sC, LS.get(f"f1h{hc % 2}", 0)))
                           if dc == 0 else ())
                    mm(bank[:, :],
                       LTEM[0:4, i4 * 1024 + (hc % 8) * 128:
                            i4 * 1024 + (hc % 8) * 128 + 128],
                       A4[0:4, tt * 512:(tt + 1) * 512],
                       start=False, stop=True,
                       waits=((sV, ltt_v), (sC, a4_c)), inc=sT)
                    wbuf_gate[bufw] = sT.n
                    act(h1_(hc), bank[:, :], AF.Gelu, bias=b1_s[:, hc:hc + 1],
                        waits=((sT, sT.n),), inc=sC)
                    LS[f"f1h{hc % 2}"] = sC.n
                h_c = sC.n
                LS["arena_lo_T"] = sT.n

                for j in range(4):
                    for dc in range(NDC):
                        hc = j * 8 + dc
                        mm(pb[4][0:4, :],
                           LTB[dc].rearrange("p (a b) -> p a b", b=2)[:, 4:8, 1],
                           h1_(hc), start=(dc == 0), stop=(dc == NDC - 1),
                           waits=((sC, h_c), (sC, LS.get("b4c", 0)))
                           if dc == 0 else (),
                           inc=sT if dc == NDC - 1 else None)
                    act(B4[j][:, :], pb[4][0:4, :], AF.Identity,
                        scale=scal_bc[0:4, 0:1], waits=((sT, sT.n),), inc=sC)
                    LS["b4c"] = sC.n
                b4_c = sC.n

                for mc in range(NDC):
                    bufw = mc % 2
                    i4 = mc // 2
                    half = mc % 2
                    P.dma("sync",
                          ar(33024 + bufw * 4096, 4096).rearrange(
                              "p (h w) -> p h w", w=128),
                          w2bf[mc], sW2[bufw],
                          waits=((sT, w2buf_gate[bufw]),
                                 (sGg, LS.get("pcst_dma", 0))))
                    w_rdy = sW2[bufw].n
                    bank = pb[2 + mc % 2]
                    for hc in range(32):
                        mm(bank[:, :], w2buf_(bufw, hc), h1_(hc),
                           start=(hc == 0), stop=False,
                           waits=((sW2[bufw], w_rdy), (sC, h_c),
                                  (sV, LS.get(f"f1o{mc % 2}", 0)))
                           if hc == 0 else ())
                    for j in range(4):
                        mm(bank[:, :],
                           LTJM[0:4, i4 * 1024 + j * 256 + half * 128:
                                i4 * 1024 + j * 256 + half * 128 + 128],
                           B4[j][0:4, :],
                           start=False, stop=(j == 3),
                           waits=((sC, b4_c), (sGm, ltj_done)),
                           inc=sT if j == 3 else None)
                    w2buf_gate[bufw] = sT.n
                    act(T1[:, :], bank[:, :], AF.Identity,
                        bias=b2_s[:, mc:mc + 1], waits=((sT, sT.n),), inc=sC)
                    vop(lambda v, mc=mc, tt=tt: v.tensor_tensor(
                        out=X[mc][:, 1 + tt * 512:1 + (tt + 1) * 512],
                        in0=T1[:, :],
                        in1=X[mc][:, 1 + tt * 512:1 + (tt + 1) * 512],
                        op=ALU.add), waits=((sC, sC.n),), inc=sV)
                    LS[f"f1o{mc % 2}"] = sV.n
            LS["arena_lo_T"] = sT.n
            LS["arena_md_T"] = sT.n

        def emit_layer(L):
            is0 = (L == 0)

            src_gate = ((sW, x_loaded),) if is0 else ()
            for dc in range(NDC):
                stgb = ar(45312 + (dc % 2) * 2048, 2048)
                vop(lambda v, dc=dc, stgb=stgb: v.tensor_copy(
                    out=stgb, in_=X[dc][:, 1:2049]),
                    waits=src_gate + ((sT, LS["arena_hi_T"]),
                                      (sGg, LS.get("stgb_dma", 0))), inc=sV)
                P.dma("gpsimd", g_in[dc * 128:(dc + 1) * 128, :], stgb, sGg,
                      waits=((sV, sV.n),))
                LS["stgb_dma"] = sGg.n
            gin_done = sGg.n
            gop(lambda g: g.collective_compute(
                "AllGather", ALU.bypass,
                replica_groups=[[0, 1], [2, 3], [4, 5], [6, 7]],
                ins=[g_in[0:512, :].opt()], outs=[g_out[0].opt()]),
                waits=((sGg, gin_done),), inc=sCC)
            xgather_h1 = sCC.n
            gop(lambda g: g.collective_compute(
                "AllGather", ALU.bypass,
                replica_groups=[[0, 1], [2, 3], [4, 5], [6, 7]],
                ins=[g_in[512:1024, :].opt()],
                outs=[g_out[1].opt()]), inc=sCC)
            xgather = sCC.n

            mdgate = ((sT, LS["arena_md_T"]),)
            for dc in range(NDC):
                stA.stage(wqb_(dc), wq[L][dc * 128:(dc + 1) * 128, :], 512,
                          reader_gate=mdgate)
                stA.stage(wkb_(dc), wk[L][dc * 128:(dc + 1) * 128, :], 512,
                          reader_gate=mdgate)
                stA.stage(wvb_(dc), wv[L][dc * 128:(dc + 1) * 128, :], 512,
                          reader_gate=mdgate)
            stA.flush()
            qkvw_v = sV.n

            for tc in range(32):
                vop(lambda v, tc=tc: v.memset(
                    V_(tc).rearrange("p (h w) -> p h w", w=65)[:, :, 64:65],
                    1.0), waits=((sT, LS["arena_lo_T"]),) if tc == 0 else ())
            vop(lambda v: v.memset(IMPCTX[:, :], 0.0),
                waits=((sGsc, LS.get("contrib_dma", 0)),), inc=sV)
            vzero_v = sV.n

            for tt8 in range(8):
                buf = tt8 % 2
                for dc in range(NDC):
                    P.dma("gpsimd", xg_(buf, dc), xg_src(dc, tt8 * 512, 512),
                          sGx, waits=((sCC, xgather_h1 if dc < 4 else xgather),
                                      (sT, LS["xg_gate"])))
                tile_rdy = sGx.n
                for pc in range(NPC):
                    bank = pb[pc % 2]
                    for dc in range(NDC):
                        mm(bank[:, :], wkb_(dc)[:, pc * 128:(pc + 1) * 128],
                           xg_(buf, dc), start=(dc == 0), stop=(dc == NDC - 1),
                           waits=((sGx, tile_rdy), (sV, qkvw_v),
                                  (sV, vzero_v),
                                  (sV, LS.get(f"kev{pc % 2}", 0)))
                           if dc == 0 else (),
                           inc=sT if dc == NDC - 1 else None)
                    if is0:
                        vop(lambda v, pc=pc, tt8=tt8, bank=bank: v.tensor_scalar(
                            out=K_(pc)[:, tt8 * 512:(tt8 + 1) * 512],
                            in0=bank[:, :], scalar1=kb_s[:, pc:pc + 1],
                            scalar2=None, op0=ALU.add),
                            waits=((sT, sT.n),), inc=sV)
                    else:
                        vop(lambda v, pc=pc, tt8=tt8, bank=bank: v.tensor_copy(
                            out=K_(pc)[:, tt8 * 512:(tt8 + 1) * 512],
                            in_=bank[:, :]), waits=((sT, sT.n),), inc=sV)
                    LS[f"kev{pc % 2}"] = sV.n
                for st in range(4):
                    tc = tt8 * 4 + st
                    bank = pb[2 + st % 2]
                    for dc in range(NDC):
                        mm(bank[:, :], xg_(buf, dc)[:, st * 128:(st + 1) * 128],
                           wvb_(dc), start=(dc == 0),
                           stop=(dc == NDC - 1 and not is0),
                           waits=((sV, LS.get(f"vev{st % 2}", 0)),)
                           if dc == 0 else (),
                           inc=sT if (dc == NDC - 1 and not is0) else None)
                    if is0:
                        mm(bank[:, :], ones128f[:, :], vb_s[:, :],
                           start=False, stop=True, inc=sT)
                    vop(lambda v, tc=tc, bank=bank: v.tensor_copy(
                        out=V_(tc).rearrange("p (h w) -> p h w",
                                             w=65)[:, :, 0:64],
                        in_=bank[:, :].rearrange("p (h w) -> p h w", w=64)),
                        waits=((sT, sT.n),), inc=sV)
                    LS[f"vev{st % 2}"] = sV.n
                LS["xg_gate"] = sT.n
            kproj_v = sV.n
            LS["arena_md_T"] = sT.n

            for pc in range(NPC):
                for i8 in range(8):
                    vop(lambda v, pc=pc, i8=i8: v.tensor_tensor(
                        out=qsqv[:, :], in0=K_(pc)[:, i8 * 512:(i8 + 1) * 512],
                        in1=K_(pc)[:, i8 * 512:(i8 + 1) * 512], op=ALU.mult),
                        waits=((sV, kproj_v), (sT, LS.get("nrmT", 0))),
                        inc=sV)
                    mm(pb[4][0:2, :], e2c_b[:, :], qsqv[:, :],
                       start=(i8 == 0), stop=(i8 == 7),
                       waits=((sV, sV.n), (sV, LS.get("ctxev", 0))),
                       inc=sT)
                    LS["nrmT"] = sT.n
                vop(lambda v, pc=pc: v.reduce_max(
                    out=bkm[:, pc:pc + 1], in_=pb[4][0:2, :], axis=AX.X),
                    waits=((sT, sT.n),), inc=sV)
            knorm_v = sV.n

            s_gate = [0, 0, 0, 0]
            exp_gate = [0, 0, 0, 0]
            qev_gate = [0, 0]
            for pair in range(4):
                for half2 in range(2):
                    qb = pair + half2 * 4
                    for dc in range(NDC):
                        P.dma("gpsimd", xg_(1, dc), xg_src(dc, qb * 512, 512),
                              sGx, waits=((sT, LS["xg_gate"]),))
                    qtile_rdy = sGx.n
                    for pc in range(NPC):
                        bank = pb[pc % 2]
                        for dc in range(NDC):
                            mm(bank[:, :],
                               wqb_(dc)[:, pc * 128:(pc + 1) * 128],
                               xg_(1, dc), start=(dc == 0),
                               stop=(dc == NDC - 1),
                               waits=((sGx, qtile_rdy),
                                      (sV, qev_gate[pc % 2]),
                                      (sC, s_gate[pc % 2]))
                               if dc == 0 else (),
                               inc=sT if dc == NDC - 1 else None)
                        if is0:
                            vop(lambda v, pc=pc, bank=bank: v.tensor_scalar(
                                out=Q_(pc), in0=bank[:, :],
                                scalar1=qb_s[:, pc:pc + 1], scalar2=None,
                                op0=ALU.add), waits=((sT, sT.n),), inc=sV)
                        else:
                            vop(lambda v, pc=pc, bank=bank: v.tensor_copy(
                                out=Q_(pc), in_=bank[:, :]),
                                waits=((sT, sT.n),), inc=sV)
                        qev_gate[pc % 2] = sV.n
                    LS["xg_gate"] = sT.n
                    qproj_v = sV.n

                    for pc in range(NPC):
                        vop(lambda v, pc=pc: v.tensor_tensor(
                            out=qsqv[:, :], in0=Q_(pc), in1=Q_(pc),
                            op=ALU.mult),
                            waits=((sV, qproj_v), (sT, LS.get("nrmT", 0))),
                            inc=sV)
                        mm(pb[4][0:2, :], e2c_b[:, :], qsqv[:, :],
                           start=True, stop=True,
                           waits=((sV, sV.n), (sV, LS.get("ctxev", 0))),
                           inc=sT)
                        LS["nrmT"] = sT.n
                        vop(lambda v, pc=pc: v.reduce_max(
                            out=bqm[:, pc:pc + 1], in_=pb[4][0:2, :],
                            axis=AX.X), waits=((sT, sT.n),), inc=sV)
                    vop(lambda v: v.tensor_tensor(
                        out=nbound[:, :], in0=bqm[:, :], in1=bkm[:, :],
                        op=ALU.mult),
                        waits=((sV, knorm_v), (sGm, LS.get("nb_dma", 0))),
                        inc=sV)
                    nbm_v = sV.n
                    P.emit("tensor",
                           lambda t: t.transpose(pb[6][0:4, 0:2],
                                                 nbound[:, :], idmf[0:2, 0:2]),
                           waits=((sV, nbm_v), (sV, LS.get("ctxev", 0))),
                           inc=sT)
                    vop(lambda v: v.reduce_max(out=nb4[:, :],
                                               in_=pb[6][0:4, 0:2],
                                               axis=AX.X),
                        waits=((sT, sT.n),), inc=sV)
                    act(nb4[:, :], nb4[:, :], AF.Sqrt, waits=((sV, sV.n),),
                        inc=sC)
                    vop(lambda v: v.tensor_scalar(
                        out=nb4[:, :], in0=nb4[:, :], scalar1=-0.125,
                        scalar2=44.0, op0=ALU.mult, op1=ALU.add),
                        waits=((sC, sC.n),), inc=sV)
                    nb_v = sV.n
                    P.dma("gpsimd", scrf[0, 0:4], nb4[:, :], sGm,
                          waits=((sV, nb_v), (sGm, LS.get("nb_dma", 0))))
                    P.dma("gpsimd", nb8[:, 0:4], scrf[0, 0:4], sGm,
                          waits=((sGm, sGm.n), (sT, LS.get("nb8_T", 0))))
                    LS["nb_dma"] = sGm.n
                    mm(pb[4][:, 0:4], ones128f[:, :], nb8[:, 0:4], start=True,
                       stop=True, waits=((sGm, LS["nb_dma"]),
                                         (sV, LS.get("ctxev", 0))), inc=sT)
                    LS["nb8_T"] = sT.n
                    LS["xg_gate"] = max(LS["xg_gate"], sT.n)
                    vop(lambda v: v.tensor_copy(out=bB[:, 0:4],
                                                in_=pb[4][:, 0:4]),
                        waits=((sT, sT.n), (sC, LS.get("expC", 0))), inc=sV)
                    bB_v = sV.n

                    nkc = 4 * (qb + 1)
                    for pc in range(NPC):
                        pend = None

                        def emit_av(pc, kc, c0, expv, trig, nkc):
                            for e in range(2):
                                sb_i = (kc % 2) * 2 + e
                                avw = [(sC, expv[e])]
                                if trig[e]:
                                    avw.append((sV, trig[e]))
                                if kc == 0:
                                    avw.append((sV, max(LS.get("ctxev", 0),
                                                        bB_v)))
                                mm(pb[4 + e][0:65, c0:512],
                                   V_(kc).rearrange("p (h w) -> p h w",
                                                    w=65)[:, pc * 2 + e, :],
                                   expS_(sb_i)[:, c0:512],
                                   start=(kc == 0), stop=(kc == nkc - 1),
                                   waits=tuple(avw), inc=sT)
                                exp_gate[sb_i] = sT.n

                        for kc in range(nkc):
                            rel = kc - 4 * qb
                            c0 = 0 if rel < 0 else 128 * rel
                            smm_t = {}
                            for e in range(2):
                                sb_i = (kc % 2) * 2 + e
                                sbank = pb[sb_i]
                                mm(sbank[:, c0:512],
                                   K_(pc)[e * 64:(e + 1) * 64,
                                          kc * 128:(kc + 1) * 128],
                                   Q_(pc)[e * 64:(e + 1) * 64, c0:512],
                                   start=True, stop=True,
                                   waits=((sC, s_gate[sb_i]), (sV, bB_v)),
                                   inc=sT, tp=(e * 64, 0))
                                smm_t[e] = sT.n
                            expv = {}
                            trig = {}
                            if rel < 0:
                                par = kc % 2
                                act(ar(51456 + par * 1024, 1024),
                                    pbS[par][:, :], AF.Exp,
                                    bias=bB[:, pc:pc + 1], scale=0.125,
                                    waits=((sT, smm_t[1]),
                                           (sT, max(exp_gate[par * 2],
                                                    exp_gate[par * 2 + 1]))),
                                    inc=sC)
                                s_gate[par * 2] = s_gate[par * 2 + 1] = sC.n
                                LS["expC"] = sC.n
                                expv[0] = expv[1] = sC.n
                                trig[0] = trig[1] = 0
                            else:
                                for e in range(2):
                                    sb_i = (kc % 2) * 2 + e
                                    act(expS_(sb_i)[:, c0:512],
                                        pb[sb_i][:, c0:512], AF.Exp,
                                        bias=bB[:, pc:pc + 1], scale=0.125,
                                        waits=((sT, smm_t[e]),
                                               (sT, exp_gate[sb_i])),
                                        inc=sC)
                                    s_gate[sb_i] = sC.n
                                    LS["expC"] = sC.n
                                    expv[e] = sC.n
                                    vop(lambda v, sb_i=sb_i, c0=c0:
                                        v.tensor_tensor(
                                            out=expS_(sb_i)[:, c0:c0 + 128],
                                            in0=expS_(sb_i)[:, c0:c0 + 128],
                                            in1=tri_b[:, :], op=ALU.mult),
                                        waits=((sC, expv[e]),), inc=sV)
                                    trig[e] = sV.n
                            if pend is not None:
                                emit_av(pc, *pend, nkc)
                            pend = (kc, c0, expv, trig)
                        emit_av(pc, *pend, nkc)
                        for e in range(2):
                            cbank = pb[4 + e]
                            av_done = exp_gate[(nkc - 1) % 2 * 2 + e]
                            vop(lambda v, cbank=cbank: v.tensor_scalar(
                                out=rrow[:, :], in0=cbank[64:65, :],
                                scalar1=1e-30, scalar2=None, op0=ALU.add),
                                waits=((sT, av_done),), inc=sV)
                            vop(lambda v: v.reciprocal(
                                out=rrow[:, :], in_=rrow[:, :]), inc=sV)
                            vop(lambda v: v.tensor_copy(
                                out=rrow_b[:, :], in_=rrow[:, :]), inc=sV)
                            rr_v = sV.n
                            mm(pb[6][0:64, :], onesrow_b[:, :], rrow_b[:, :],
                               start=True, stop=True,
                               waits=((sV, rr_v),
                                      (sV, LS.get("ctxev", 0))), inc=sT)
                            bc_t = sT.n
                            vop(lambda v, cbank=cbank: v.tensor_copy(
                                out=T1[0:64, :], in_=cbank[0:64, :]),
                                waits=((sT, bc_t),), inc=sV)
                            vop(lambda v, pc=pc, e=e: v.tensor_tensor(
                                out=ctxqb_(pc)[e * 64:(e + 1) * 64, :],
                                in0=T1[0:64, :], in1=pb[6][0:64, :],
                                op=ALU.mult), inc=sV)
                            LS["ctxev"] = sV.n
                    ctx_v = sV.n
                    if is0:
                        emit_precast(10)

                    if qb == 0:
                        for pc in range(NPC):
                            vop(lambda v, pc=pc: v.tensor_copy(
                                out=IMPCTX[:, pc * 17 + 1:pc * 17 + 17],
                                in_=ctxqb_(pc)[:, 1:17]), inc=sV)
                    if qb == 3:
                        for pc in range(NPC):
                            vop(lambda v, pc=pc: v.tensor_copy(
                                out=IMPCTX[:, pc * 17:pc * 17 + 1],
                                in_=ctxqb_(pc)[:, 511:512]), inc=sV)

                    if half2 == 0:
                        for pc in range(NPC):
                            vop(lambda v, pc=pc, pair=pair: v.tensor_scalar(
                                out=ctxself_(pc)[:,
                                                 pair * 512:(pair + 1) * 512],
                                in0=ctxqb_(pc), scalar1=pv(PV_ASELF),
                                scalar2=None, op0=ALU.mult),
                                waits=((sT, LS["arena_md_T"]),), inc=sV)
                            vop(lambda v, pc=pc: v.tensor_scalar(
                                out=contrib_(pc), in0=ctxqb_(pc),
                                scalar1=pv(PV_ACTX), scalar2=None,
                                op0=ALU.mult),
                                waits=((sGsc, LS.get("contrib_dma", 0)),),
                                inc=sV)
                    else:
                        for pc in range(NPC):
                            vop(lambda v, pc=pc, pair=pair:
                                v.scalar_tensor_tensor(
                                    out=ctxself_(pc)[:, pair * 512:
                                                     (pair + 1) * 512],
                                    in0=ctxqb_(pc), scalar=pv(PV_BSELF),
                                    in1=ctxself_(pc)[:, pair * 512:
                                                     (pair + 1) * 512],
                                    op0=ALU.mult, op1=ALU.add), inc=sV)
                            vop(lambda v, pc=pc: v.scalar_tensor_tensor(
                                out=contrib_(pc), in0=ctxqb_(pc),
                                scalar=pv(PV_BCTX), in1=contrib_(pc),
                                op0=ALU.mult, op1=ALU.add), inc=sV)
                        blend_v = sV.n
                        for pc in range(NPC):
                            P.dma("gpsimd",
                                  c_in[pc, :, pair * 512:(pair + 1) * 512],
                                  contrib_(pc), sGsc,
                                  waits=((sV, blend_v),
                                         (sCC, LS["cin_free_cc"])))
                        LS["contrib_dma"] = sGsc.n
            imp_v = sV.n
            for pc in range(NPC):
                P.dma("gpsimd", c_in[pc, :, 2048:2065],
                      IMPCTX[:, pc * 17:(pc + 1) * 17], sGsc,
                      waits=((sV, imp_v),))
            LS["contrib_dma"] = sGsc.n

            gop(lambda g: g.collective_compute(
                "AllGather", ALU.bypass,
                replica_groups=[[0, 1], [2, 3], [4, 5], [6, 7]],
                ins=[c_in[0:2].opt()], outs=[c_out[0].opt()]),
                waits=((sGsc, LS["contrib_dma"]),), inc=sCC)
            cgather_h1 = sCC.n
            gop(lambda g: g.collective_compute(
                "AllGather", ALU.bypass,
                replica_groups=[[0, 1], [2, 3], [4, 5], [6, 7]],
                ins=[c_in[2:4].opt()], outs=[c_out[1].opt()]), inc=sCC)
            cgather = sCC.n
            LS["cin_free_cc"] = cgather
            LS["gin_free_cc"] = cgather
            attn_T = sT.n

            for si in range(2):
                for pc in range(NPC):
                    P.dma("gpsimd", slot_(si, pc),
                          c_out[pc // 2, si, pc % 2, :, :], sGsc,
                          waits=((sCC, cgather_h1 if pc < 2 else cgather),
                                 (sT, attn_T)))
            for dc in range(NDC):
                P.dma("gpsimd", IMPRES[:, dc * 17:dc * 17 + 1],
                      xg_src(dc, 2047, 1), sGsc, waits=((sCC, cgather),),
                      slow=True)
                P.dma("gpsimd", IMPRES[:, dc * 17 + 1:dc * 17 + 17],
                      xg_src(dc, 1, 16), sGsc)
            impres_rdy = sGsc.n
            slots_rdy = sGsc.n

            for mc in range(NDC):
                for third in range(3):
                    stA.stage(ar(51456 + third * 512, 512),
                              wo[L][mc, :, third * 4:(third + 1) * 4,
                                    :].rearrange("p t w -> p (t w)"), 512,
                              reader_gate=((sT, LS.get("owT", attn_T)),))
                    stA.flush()
                owc = sV.n
                LS["owT"] = 0
                for tt in range(NTT):
                    bank = pb[tt % 2]
                    nmm = 0
                    for ci in range(3):
                        for c4 in range(NPC):
                            wt = ar(51456 + (ci * 4 + c4) * 128, 128)
                            if ci == 0:
                                rhs = ctxself_(c4)[:, tt * 512:(tt + 1) * 512]
                            else:
                                rhs = slot_(ci - 1, c4)[:,
                                                        tt * 512:(tt + 1) * 512]
                            nmm += 1
                            mm(bank[:, :], wt, rhs, start=(nmm == 1),
                               stop=(nmm == 12),
                               waits=((sV, owc), (sGsc, slots_rdy),
                                      (sV, LS.get(f"ores{tt % 2}", 0)))
                               if nmm == 1 else (),
                               inc=sT if nmm == 12 else None)
                    LS["owT"] = max(LS["owT"], sT.n)
                    if is0:
                        act(T1[:, :], bank[:, :], AF.Identity,
                            bias=ob_s[:, mc:mc + 1], waits=((sT, sT.n),),
                            inc=sC)
                        vop(lambda v, mc=mc, tt=tt: v.tensor_tensor(
                            out=X[mc][:, 1 + tt * 512:1 + (tt + 1) * 512],
                            in0=T1[:, :],
                            in1=X[mc][:, 1 + tt * 512:1 + (tt + 1) * 512],
                            op=ALU.add), waits=((sC, sC.n),), inc=sV)
                    else:
                        vop(lambda v, mc=mc, tt=tt, bank=bank: v.tensor_tensor(
                            out=X[mc][:, 1 + tt * 512:1 + (tt + 1) * 512],
                            in0=bank[:, :],
                            in1=X[mc][:, 1 + tt * 512:1 + (tt + 1) * 512],
                            op=ALU.add), waits=((sT, sT.n),), inc=sV)
                    LS[f"ores{tt % 2}"] = sV.n
                bank = pb[6]
                nmm = 0
                for ci in range(3):
                    for c4 in range(NPC):
                        wt = ar(51456 + (ci * 4 + c4) * 128, 128)
                        if ci == 0:
                            rhs = IMPCTX[:, c4 * 17:(c4 + 1) * 17]
                        else:
                            rhs = slot_(ci - 1, c4)[:, 2048:2065]
                        nmm += 1
                        mm(bank[:, 0:17], wt, rhs, start=(nmm == 1),
                           stop=(nmm == 12),
                           waits=((sV, LS.get("impev", 0)),)
                           if nmm == 1 else (),
                           inc=sT if nmm == 12 else None)
                LS["owT"] = max(LS["owT"], sT.n)
                if is0:
                    act(IMPX[:, mc * 17:(mc + 1) * 17], bank[:, 0:17],
                        AF.Identity, bias=ob_s[:, mc:mc + 1],
                        waits=((sT, sT.n),), inc=sC)
                else:
                    act(IMPX[:, mc * 17:(mc + 1) * 17], bank[:, 0:17],
                        AF.Identity, waits=((sT, sT.n),), inc=sC)
                vop(lambda v, mc=mc: v.tensor_tensor(
                    out=IMPX[:, mc * 17:(mc + 1) * 17],
                    in0=IMPX[:, mc * 17:(mc + 1) * 17],
                    in1=IMPRES[:, mc * 17:(mc + 1) * 17], op=ALU.add),
                    waits=((sC, sC.n), (sGsc, impres_rdy)), inc=sV)
                LS["impev"] = sV.n
            LS["arena_md_T"] = sT.n
            LS["arena_lo_T"] = sT.n

            for mc in range(NDC):
                vop(lambda v, mc=mc: v.tensor_copy(
                    out=IMPB[:, :], in_=IMPX[:, mc * 17:(mc + 1) * 17]),
                    waits=((sT, LS.get("impstT", 0)),), inc=sV)
                vop(lambda v: v.tensor_tensor(
                    out=IMPSQ[:, :], in0=IMPB[:, :], in1=IMPB[:, :],
                    op=ALU.mult), inc=sV)
                mm(pb[4][0:1, 0:17], ones128b[:, :], IMPB[:, :],
                   start=(mc == 0), stop=(mc == NDC - 1),
                   waits=((sV, sV.n),), inc=sT)
                mm(pb[5][0:1, 0:17], ones128b[:, :], IMPSQ[:, :],
                   start=(mc == 0), stop=(mc == NDC - 1),
                   inc=sT)
                LS["impstT"] = sT.n
            ist_t = sT.n
            vop(lambda v: v.tensor_scalar(out=i1row[:, :],
                                          in0=pb[4][0:1, 0:17],
                                          scalar1=1.0 / D, scalar2=None,
                                          op0=ALU.mult),
                waits=((sT, ist_t),), inc=sV)
            vop(lambda v: v.tensor_scalar(out=i2row[:, :],
                                          in0=pb[5][0:1, 0:17],
                                          scalar1=1.0 / D, scalar2=None,
                                          op0=ALU.mult), inc=sV)
            vop(lambda v: v.tensor_tensor(out=iarow[:, :], in0=i1row[:, :],
                                          in1=i1row[:, :], op=ALU.mult),
                inc=sV)
            vop(lambda v: v.tensor_tensor(out=iarow[:, :], in0=i2row[:, :],
                                          in1=iarow[:, :], op=ALU.subtract),
                inc=sV)
            vop(lambda v: v.tensor_scalar(out=iarow[:, :], in0=iarow[:, :],
                                          scalar1=EPS, scalar2=None,
                                          op0=ALU.add), inc=sV)
            vop(lambda v: v.reciprocal(out=iarow[:, :], in_=iarow[:, :]),
                inc=sV)
            act(iarow[:, :], iarow[:, :], AF.Sqrt, waits=((sV, sV.n),),
                inc=sC)
            vop(lambda v: v.tensor_tensor(out=ibrow[:, :], in0=i1row[:, :],
                                          in1=iarow[:, :], op=ALU.mult),
                waits=((sC, sC.n),), inc=sV)
            iab_v = sV.n
            mm(pb[4][:, 0:17], ones128f[:, :], iarow[:, :], start=True,
               stop=True, waits=((sV, iab_v),), inc=sT)
            mm(pb[5][:, 0:17], ones128f[:, :], ibrow[:, :], start=True,
               stop=True, inc=sT)
            ibc_t = sT.n
            for mc in range(NDC):
                vop(lambda v, mc=mc: v.tensor_tensor(
                    out=IMPX[:, mc * 17:(mc + 1) * 17],
                    in0=IMPX[:, mc * 17:(mc + 1) * 17],
                    in1=pb[4][:, 0:17], op=ALU.mult),
                    waits=((sT, ibc_t),), inc=sV)
                vop(lambda v, mc=mc: v.tensor_tensor(
                    out=IMPX[:, mc * 17:(mc + 1) * 17],
                    in0=IMPX[:, mc * 17:(mc + 1) * 17],
                    in1=pb[5][:, 0:17], op=ALU.subtract), inc=sV)
                vop(lambda v, mc=mc: v.tensor_scalar(
                    out=X[mc][:, 0:1], in0=IMPX[:, mc * 17:mc * 17 + 1],
                    scalar1=pv(PV_G), scalar2=None, op0=ALU.mult), inc=sV)
                vop(lambda v, mc=mc: v.tensor_copy(
                    out=XIMP[:, mc * 16:(mc + 1) * 16],
                    in_=IMPX[:, mc * 17 + 1:(mc + 1) * 17]), inc=sV)

            if is0:
                emit_precast(len(PRE_TASKS))
            emit_ln(make_xb=(not is0))

            if is0:
                emit_ffn0()
            else:
                emit_ffn1()

            emit_ln(make_xb=False)
            LS["arena_hi_T"] = sT.n

        emit_layer(0)
        emit_layer(1)

        out_v = LS["lnap"]
        for dc in range(NDC):
            P.dma("gpsimd", yout[dc * 128:(dc + 1) * 128, :],
                  X[dc][:, 1:2049], sGm, waits=((sV, out_v),))
        P.wait("gpsimd", (sGm, sGm.n), (sGx, sGx.n), (sGsc, sGsc.n),
               (sGg, sGg.n))
        P.wait("sync", (sW, sW.n), (sWa, sWa.n), (sWb, sWb.n),
               (sW1[0], sW1[0].n), (sW1[1], sW1[1].n),
               (sW2[0], sW2[0].n), (sW2[1], sW2[1].n))

        with nc.Block() as block:
            @block.sync
            def _(sync):
                for f in P.q["sync"]:
                    f(sync)

            @block.tensor
            def _(tensor):
                for f in P.q["tensor"]:
                    f(tensor)

            @block.vector
            def _(vector):
                for f in P.q["vector"]:
                    f(vector)

            @block.scalar
            def _(scalar):
                for f in P.q["scalar"]:
                    f(scalar)

            @block.gpsimd
            def _(gpsimd):
                for f in P.q["gpsimd"]:
                    f(gpsimd)

    return nc


def kernel(**inp):
    f32 = np.float32

    def TT(a):
        return np.ascontiguousarray(np.asarray(a, dtype=f32).T)

    nc = build_nc()
    in_maps = []
    for c in range(8):
        b, g = c // 2, c % 2
        m = {}
        x = np.asarray(inp["L0_attn_x"], f32)
        m["x0"] = np.ascontiguousarray(x[b, g * T:(g + 1) * T, :].T)

        hsl = slice(g * 8, g * 8 + 8)
        psl = slice((1 - g) * 8, (1 - g) * 8 + 8)
        for nm, w in (("wq", "L0_attn_Q_w"), ("wk", "L0_attn_K_w"),
                      ("wv", "L0_attn_V_w")):
            W = np.asarray(inp[w], f32)[hsl]
            m[f"L0_{nm}"] = np.ascontiguousarray(
                W.transpose(1, 0, 2).reshape(D, HD))
        def tile_o(slabs):
            a = np.stack(slabs)
            a = a.reshape(3, 4, 128, 8, 128)
            return np.ascontiguousarray(a.transpose(3, 2, 0, 1, 4).reshape(
                8, 128, 12, 128))

        OW = np.asarray(inp["L0_attn_O_w"], f32)
        zz = np.zeros((HD, D), f32)
        part = np.ascontiguousarray(OW[psl].reshape(HD, D))
        m["L0_wo"] = tile_o([
            np.ascontiguousarray(OW[hsl].reshape(HD, D)),
            part if g == 1 else zz,
            part if g == 0 else zz])
        m["qbias"] = np.ascontiguousarray(
            np.asarray(inp["L0_attn_Q_b"], f32)[hsl].reshape(4, 128).T)
        m["kbias"] = np.ascontiguousarray(
            np.asarray(inp["L0_attn_K_b"], f32)[hsl].reshape(4, 128).T)
        m["vbias"] = np.asarray(
            inp["L0_attn_V_b"], f32)[hsl].reshape(1, HD).copy()
        m["obias"] = np.ascontiguousarray(
            np.asarray(inp["L0_attn_O_b"], f32).reshape(8, 128).T)

        for nm, w in (("wq", "L1_attn_w_q"), ("wk", "L1_attn_w_k"),
                      ("wv", "L1_attn_w_v")):
            W = np.asarray(inp[w], f32)[hsl]
            m[f"L1_{nm}"] = np.ascontiguousarray(W.reshape(HD, D).T)
        WO = np.asarray(inp["L1_attn_w_o"], f32)
        part = np.ascontiguousarray(WO[(1 - g) * HD:(2 - g) * HD])
        m["L1_wo"] = tile_o([
            np.ascontiguousarray(WO[g * HD:(g + 1) * HD]),
            part if g == 1 else zz,
            part if g == 0 else zz])

        def tile8(wT):
            return np.ascontiguousarray(
                wT.reshape(8, 128, 8, 128).transpose(2, 1, 0, 3))

        m["wr0"] = tile8(TT(inp["L0_ffn_W_r"]))
        m["wk0"] = tile8(TT(inp["L0_ffn_W_k"]))
        m["wv0"] = tile8(TT(inp["L0_ffn_W_v"]))
        m["mur"] = np.ascontiguousarray(
            np.asarray(inp["L0_ffn_mu_r"], f32).reshape(8, 128).T)
        m["muk"] = np.ascontiguousarray(
            np.asarray(inp["L0_ffn_mu_k"], f32).reshape(8, 128).T)

        m["w1t"] = TT(inp["L1_ffn_W1"])
        m["b1c"] = np.ascontiguousarray(
            np.asarray(inp["L1_ffn_b1"], f32).reshape(32, 128).T)
        m["w2t"] = TT(inp["L1_ffn_W2"])
        m["b2c"] = np.ascontiguousarray(
            np.asarray(inp["L1_ffn_b2"], f32).reshape(8, 128).T)
        m["scal"] = np.asarray(inp["L1_ffn_scale"], f32).reshape(1, 1)

        if g == 0:
            pvv = [0.0, 1.0, 1.0, 0.0, 0.0, 1.0, 0.0, 0.0]
        else:
            pvv = [1.0, 0.0, 0.0, 1.0, 1.0, 0.0, 1.0, 0.0]
        m["pvec"] = np.asarray(pvv, f32).reshape(1, 8)

        m["tric"] = np.ascontiguousarray(np.triu(np.ones((128, 128), f32)))
        m["idmc"] = np.eye(128, dtype=f32)
        e2 = np.zeros((128, 2), f32)
        e2[0:64, 0] = 1.0
        e2[64:128, 1] = 1.0
        m["e2cc"] = e2
        m["emc"] = np.eye(4, dtype=f32)
        in_maps.append(m)

    res = run_bass_kernel_spmd(nc, in_maps, core_ids=list(range(8)))
    out = np.empty((B, S, D), np.float32)
    for c in range(8):
        b, g = c // 2, c % 2
        out[b, g * T:(g + 1) * T, :] = res.results[c]["y"].T
    return out


if __name__ == "__main__":
    import reference as R
    inp = R.setup_inputs()
    got = kernel(**{k: np.asarray(v) for k, v in inp.items()})
    want = np.asarray(R.reference(**inp))
    num = float(((got - want) ** 2).mean())
    den = float((want ** 2).mean())
    print("resvar:", num / den)
